# version 05
# baseline (speedup 1.0000x reference)
import numpy as np
import ml_dtypes
from contextlib import ExitStack

import concourse.bass as bass
import concourse.mybir as mybir
from concourse.bass_utils import run_bass_kernel_spmd

F32 = mybir.dt.float32
BF16 = mybir.dt.bfloat16
I32 = mybir.dt.int32
AF = mybir.ActivationFunctionType
ALU = mybir.AluOpType
AX = mybir.AxisListType.X
BFD = ml_dtypes.bfloat16

B, T, C = 2, 4096, 2048
H, DKH, DVH = 16, 64, 128
HPC = 4
L = 128
NCH = T // L
TBS = 512
NTB = T // TBS
CPB = TBS // L
TOK = 1024
NE, EI, SI = 16, 1408, 5632
SCALE = DKH ** -0.5
NEG = -1e30
LNS = float(np.log(SCALE))
EPSF = float(np.finfo(np.float32).eps)

STAGE = 3
PROBE = False
MODE = "fused"
DBG3 = False
ENGS = ("tensor", "vector", "scalar", "gpsimd", "sync")


class Op:
    __slots__ = ("eng", "fn", "deps", "dma", "slot", "need_inc", "sem", "val", "inc")

    def __init__(self, eng, fn, deps, dma, slot, inc):
        self.eng, self.fn, self.deps, self.dma = eng, fn, deps, dma
        self.slot = slot
        self.need_inc = dma
        self.sem = None
        self.val = None
        self.inc = inc


class Prog:
    def __init__(self):
        self.ops = []

    def op(self, eng, fn, deps=(), dma=False, slot=None, inc=16):
        if dma:
            assert slot is not None, "DMA ops need a slot"
        o = Op(eng, fn, [d for d in deps if d is not None], dma, slot, inc)
        self.ops.append(o)
        return o

    def emit(self, nc, blk, es):
        for o in self.ops:
            for d in o.deps:
                d.need_inc = True
        csem = {e: es.enter_context(nc.semaphore(f"c_{e}")) for e in ENGS}
        slots = {}
        for o in self.ops:
            if o.dma and o.slot not in slots:
                slots[o.slot] = [es.enter_context(nc.semaphore(f"s_{o.slot}")), 0]
        cctr = {e: 0 for e in ENGS}
        for o in self.ops:
            if o.dma:
                sl = slots[o.slot]
                sl[1] += o.inc
                o.sem, o.val = sl[0], sl[1]
            elif o.need_inc:
                cctr[o.eng] += 1
                o.sem, o.val = csem[o.eng], cctr[o.eng]

        def body(eng_name):
            def f(eng):
                satisfied = {}
                for o in self.ops:
                    if o.eng != eng_name:
                        continue
                    bysem = {}
                    for d in o.deps:
                        if satisfied.get(id(d.sem), -1) >= d.val:
                            continue
                        prev = bysem.get(id(d.sem))
                        if prev is None or d.val > prev[1]:
                            bysem[id(d.sem)] = (d.sem, d.val)
                    for smh, v in bysem.values():
                        eng.wait_ge(smh, v)
                        satisfied[id(smh)] = max(satisfied.get(id(smh), -1), v)
                    inst = o.fn(eng)
                    if o.dma:
                        inst.then_inc(o.sem, o.inc)
                    elif o.need_inc:
                        if inst is None:
                            inst = eng.nop()
                        inst.then_inc(o.sem, 1)
            return f
        for name in ENGS:
            getattr(blk, name)(body(name))

    def final_ops(self):
        last = {}
        for o in self.ops:
            last[o.eng] = o
        return list(last.values())


def build_program(part="fused"):
    nc = bass.Bass("TRN2", target_bir_lowering=False)
    P1 = part in ("fused", "p1")
    P3 = part in ("fused", "p3")
    P = Prog()
    o = P.op

    dt = nc.dram_tensor
    FUSED = part == "fused"
    if P1:
        xT_h = dt("xT", [NTB, 128, 16 * TBS], F32, kind="ExternalInput")
        wproj_h = dt("wproj", [128, 16 * 1552], F32, kind="ExternalInput")
        convw_h = dt("convw", [128, 48], F32, kind="ExternalInput")
        cvec_h = dt("cvec", [128, 12], F32, kind="ExternalInput")
        alog_h = dt("alog", [1, HPC], F32, kind="ExternalInput")
    kcf_h = dt("kcf", [128, 652], F32, kind="ExternalInput")
    kcb_h = dt("kcb", [128, 130], BF16, kind="ExternalInput")
    if FUSED:
        meta_h = dt("meta", [1, 3], I32, kind="ExternalInput")
    if P3:
        xrows_h = dt("xrows", [TOK, C], F32, kind="ExternalInput")
        owT_h = dt("owT", [16, 128, 2048], F32, kind="ExternalInput")
        gwT_h = dt("gwT", [128, 16 * 16], F32, kind="ExternalInput")
        sgvT_h = dt("sgvT", [128, 16], F32, kind="ExternalInput")
        sel16_h = dt("sel16", [16, 16 * 128], BF16, kind="ExternalInput")
        wguT_h = dt("wguT", [NE, 22, 128, 16 * 128], F32, kind="ExternalInput")
        wdT4_h = dt("wdT4", [NE, 4, 128, 11 * 512], F32, kind="ExternalInput")
        sguT_h = dt("sguT", [88, 128, 16 * 128], F32, kind="ExternalInput")
        sdT_h = dt("sdT", [4, 128, 44 * 512], F32, kind="ExternalInput")
        if not FUSED:
            ogin_h = dt("ogin", [128, 16 * TOK], BF16, kind="ExternalInput")

    dbg_h = None
    if P3:
        out_h = dt("out", [TOK, C], F32, kind="ExternalOutput")
        if DBG3 and not FUSED:
            dbg3_h = dt("dbg3", [TOK, C], F32, kind="ExternalOutput")
            dbg4_h = dt("dbg4", [TOK, C], F32, kind="ExternalOutput")
            dbg5_h = dt("dbg5", [128, 2304], F32, kind="ExternalOutput")
        x1sp_h = dt("x1_spill", [TOK, C], F32, kind="Internal")
        acc_h = dt("acc_dram", [TOK, C], F32, kind="Internal")
    if P1 and not P3:
        slab_h = dt("slab", [HPC * 128, T], BF16, kind="ExternalOutput")
    if FUSED:
        myslab_h = dt("myslab", [HPC * 128, T], BF16, kind="Internal")
        ccagg_h = dt("ccagg", [8 * HPC * 128, T], BF16, kind="Internal")
        ogst_h = dt("ogst", [2048, TOK], BF16, kind="Internal")

    es = ExitStack()
    E = es.enter_context
    es1 = ExitStack()
    E1 = es1.enter_context
    st = nc.sbuf_tensor

    kcf_sb = E(st("kcf_sb", [128, 652], F32))
    kcb_sb = E(st("kcb_sb", [128, 130], BF16))
    if FUSED:
        meta_sb = E(st("meta_sb", [1, 3], I32))
    pt = nc.psum_tensor
    pp = [E(pt(f"pp{i}", [128, TBS], F32)) for i in range(2)]
    if P1:
        wproj_sb = E1(st("wproj_sb", [128, 16 * 1552], BF16))
        xsb = [E1(st(f"xsb{i}", [128, 16 * TBS], BF16)) for i in range(2)]
        convw_sb = E1(st("convw_sb", [128, 48], F32))
        cvec_sb = E1(st("cvec_sb", [128, 12], F32))
        alog_sb = E1(st("alog_sb", [1, HPC], F32))
        eavec = E1(st("eavec", [1, 2 * HPC], F32))
        negea_bc = E1(st("negea_bc", [128, HPC], F32))

        qrw = [E1(st(f"qrw{h}", [64, TBS + 3], BF16)) for h in range(HPC)]
        krw = [E1(st(f"krw{h}", [64, TBS + 3], BF16)) for h in range(HPC)]
        vrw = [E1(st(f"vrw{h}", [128, TBS + 3], BF16)) for h in range(HPC)]
        qpo = [E1(st(f"qpo{h}", [64, 2 * TBS], BF16)) for h in range(HPC)]
        kpo = [E1(st(f"kpo{h}", [64, 2 * TBS], BF16)) for h in range(HPC)]
        vpo = [E1(st(f"vpo{h}", [128, 2 * TBS], BF16)) for h in range(HPC)]
        gpo = [E1(st(f"gpo{h}", [128, 2 * TBS], BF16)) for h in range(HPC)]
        qsq = [E1(st(f"qsq{h}", [64, 2 * TBS], BF16)) for h in range(HPC)]
        ksq = [E1(st(f"ksq{h}", [64, 2 * TBS], BF16)) for h in range(HPC)]
        asb = E1(st("asb", [4, 2 * TBS], F32))
        bsb = E1(st("bsb", [4, 2 * TBS], F32))
        ctmp = [E1(st(f"ctmp{i}", [128, TBS], F32)) for i in range(3)]
        ctmv = [E1(st(f"ctmv{i}", [128, TBS], F32)) for i in range(2)]

        NL = 2
        dA = [E1(st(f"dA{l}", [128, 128], F32)) for l in range(NL)]
        dM = [E1(st(f"dM{l}", [128, 128], F32)) for l in range(NL)]
        uf = [E1(st(f"uf{l}", [128, 128], F32)) for l in range(NL)]
        tf = [E1(st(f"tf{l}", [128, 128], F32)) for l in range(NL)]
        off_ = [E1(st(f"off{l}", [128, 128], F32)) for l in range(NL)]
        osc = [E1(st(f"osc{l}", [128, 128], F32)) for l in range(NL)]
        bA = [E1(st(f"bA{l}", [128, 128], BF16)) for l in range(NL)]
        bTA0 = [E1(st(f"bTA0{l}", [128, 128], BF16)) for l in range(NL)]
        bTAs = [[E1(st(f"bTAs{l}_{i}", [128, 128], BF16)) for i in range(6)] for l in range(NL)]
        bPs = [[E1(st(f"bPs{l}_{i}", [128, 128], BF16)) for i in range(2)] for l in range(NL)]
        bMT = [E1(st(f"bMT{l}", [128, 128], BF16)) for l in range(NL)]
        bu = [E1(st(f"bu{l}", [128, 128], BF16)) for l in range(NL)]
        bV = [E1(st(f"bV{l}", [128, 128], BF16)) for l in range(NL)]
        bK = [E1(st(f"bK{l}", [128, 64], BF16)) for l in range(NL)]
        bS16 = [E1(st(f"bS16{l}", [64, DVH], BF16)) for l in range(NL)]
        bon = [E1(st(f"bon{l}", [128, 128], BF16)) for l in range(NL)]
        gsil = [E1(st(f"gsil{l}", [128, L], BF16)) for l in range(NL)]
        rowsf = [E1(st(f"rowsf{l}", [1, 256], F32)) for l in range(NL)]
        Ssb = [E1(st(f"Ssb{h}", [64, DVH], F32)) for h in range(HPC)]
        S2 = [E1(st(f"S2_{h}", [64, DVH], F32)) for h in range(HPC)]
        colh = [E1(st(f"colh{h}", [128, 12], F32)) for h in range(HPC)]
        cols = E1(st("cols", [128, 24], F32))
        gam4 = E1(st("gam4", [128, 4], F32))
        Gam4 = E1(st("Gam4", [4, 1], F32))
        Gbc = E1(st("Gbc", [128, HPC], F32))
        eGbc = E1(st("eGbc", [64, HPC], F32))
        grow = E1(st("grow", [1, 4], F32))
        oslab = E1(st("oslab", [128, HPC * TBS], BF16))

        pf = [[E1(pt(f"pf{l}_{i}", [128, 128], F32)) for i in range(2)] for l in range(NL)]
        pbt = [E1(pt(f"pbt{l}", [128, 128], BF16)) for l in range(NL)]

    maskA = kcf_sb[:, 0:128]
    maskMT = kcf_sb[:, 128:256]
    idf32 = kcf_sb[:, 256:384]
    tuones = kcf_sb[:, 384:512]
    onesf = kcf_sb[:, 512:640]
    ones_row_f = kcf_sb[0:1, 512:640]
    ones_col_f = kcf_sb[:, 512:513]
    KC = {"zero": 640, "eps6": 641, "one": 642, "eps5": 643, "epsf": 644,
          "neghalf": 645, "half": 646, "two": 647, "lnscale": 648,
          "inv128": 649, "negone": 650, "inv2048": 651}
    def kc(name, p=128):
        return kcf_sb[0:p, KC[name]:KC[name] + 1]
    id_bf = kcb_sb[:, 0:128]
    ones64_bf = kcb_sb[0:64, 128:129]

    if P1:
        ld_wproj = o("gpsimd", lambda g: g.dma_start(out=wproj_sb[:, :], in_=wproj_h[:, :]), dma=True, slot="wproj")
    ld_misc = []
    pairs = [(kcf_sb, kcf_h), (kcb_sb, kcb_h)]
    if P1:
        pairs = [(convw_sb, convw_h), (cvec_sb, cvec_h), (alog_sb, alog_h)] + pairs
    if FUSED:
        pairs = pairs + [(meta_sb, meta_h)]
    for dst, src in pairs:
        ld_misc.append(o("sync", (lambda d, s_: lambda g: g.dma_start(out=d[:, :], in_=s_[:, :]))(dst, src), dma=True, slot="misc"))
    mA = mCV = mAL = mKF = mKB = mMT = ld_misc[-1]
    MISC = ld_misc

    if P1:
        t_cv = o("vector", lambda g: g.tensor_add(cvec_sb[:, 4:8], cvec_sb[:, 4:8], cvec_sb[:, 8:12]), deps=MISC)
        t_ea = o("scalar", lambda g: g.activation(eavec[0:1, 0:HPC], alog_sb[0:1, :], AF.Exp), deps=MISC)
        t_ea2 = o("vector", lambda g: g.tensor_scalar_mul(eavec[0:1, HPC:2 * HPC], eavec[0:1, 0:HPC], -1.0), deps=[t_ea])
        t_eab = o("tensor", lambda g: g.matmul(pf[0][0][:, 0:HPC], ones_row_f, eavec[0:1, HPC:2 * HPC], start=True, stop=True), deps=[t_ea2, mKF])
        t_eac = o("vector", lambda g: g.tensor_copy(negea_bc[:, :], pf[0][0][:, 0:HPC]), deps=[t_eab])

        init_misc = [t_cv, t_eac, mA, mKB, mMT]
        sinit = []
        for h in range(HPC):
            sinit.append(o("vector", (lambda hh: lambda g: g.memset(Ssb[hh][:, :], 0.0))(h)))
            sinit.append(o("vector", (lambda hh: lambda g: g.memset(S2[hh][:, :], 0.0))(h)))
        halo0 = []
        for h in range(HPC):
            for rw, pw in ((qrw[h], 64), (krw[h], 64), (vrw[h], 128)):
                halo0.append(o("vector", (lambda r, p_: lambda g: g.memset(r[0:p_, 0:3], 0.0))(rw, pw)))

        ld_x = [None] * NTB
        ld_x[0] = o("gpsimd", lambda g: g.dma_start(out=xsb[0][:, :], in_=xT_h[0, :, :]), dma=True, slot="x0", deps=[])

    ftiles = []
    offc = 0
    for h in range(HPC):
        ftiles.append(("q", h, 64, offc)); offc += 64
    for h in range(HPC):
        ftiles.append(("k", h, 64, offc)); offc += 64
    for h in range(HPC):
        ftiles.append(("v", h, 128, offc)); offc += 128
    for h in range(HPC):
        ftiles.append(("g", h, 128, offc)); offc += 128
    ftiles.append(("a", 0, 4, offc)); offc += 4
    ftiles.append(("b", 0, 4, offc)); offc += 4

    bank_evac = [None, None]
    conv_fin = [None] * NTB
    delta_fin = [None] * NTB
    slab_fin = [[None] * HPC for _ in range(NTB)]
    lane_last = [None, None]
    chunk_shared_last = [None]
    state_cur, state_nxt = (Ssb, S2) if P1 else (None, None)
    state_op = [None] * HPC
    ab_evac = [None] * NTB

    evac_toggle = [0]
    def evac_eng():
        evac_toggle[0] ^= 1
        return "vector" if evac_toggle[0] else "scalar"

    def ev_copy(dst, src, deps):
        e = evac_eng()
        if e == "vector":
            return o("vector", lambda g: g.tensor_copy(dst, src), deps=deps)
        return o("scalar", lambda g: g.activation(dst, src, AF.Copy), deps=deps)

    regs = {}
    t_regs = None
    def setup_regs():
        def fn(g):
            r0 = g.alloc_register("rrow2")
            r1 = g.alloc_register("rcol2")
            g.reg_load(r0, meta_sb[0:1, 0:1])
            g.reg_load(r1, meta_sb[0:1, 1:2])
            regs["row2"] = g.snap(r0, min_val=0, max_val=HPC * 128 * 4)
            regs["col2"] = g.snap(r1, min_val=0, max_val=3 * TOK)
            return None
        return fn
    if FUSED:
        t_regs = o("sync", setup_regs(), deps=[mMT])

    def proj_block(tb):
        deps = [ld_x[tb], ld_wproj]
        if tb >= 2 and delta_fin[tb - 2] is not None:
            deps += delta_fin[tb - 2]
        col0 = (tb % 2) * TBS
        evacs = {}
        for fi, (kind, h, fw, fo) in enumerate(ftiles):
            bank = pp[fi % 2]
            mmdeps = list(deps)
            if bank_evac[fi % 2] is not None:
                mmdeps.append(bank_evac[fi % 2])
            if kind in ("q", "k", "v") and conv_fin[tb - 1] is not None:
                mmdeps += conv_fin[tb - 1]
            first = None
            for k in range(16):
                mm = o("tensor", (lambda kk, ffo, ffw, bk, ttb: lambda g: g.matmul(
                    bk[0:ffw, :], wproj_sb[:, kk * 1552 + ffo: kk * 1552 + ffo + ffw],
                    xsb[ttb % 2][:, kk * TBS:(kk + 1) * TBS],
                    start=(kk == 0), stop=(kk == 15)))(k, fo, fw, bank, tb),
                    deps=(mmdeps if k == 0 else ()))
                if first is None:
                    first = mm
            if kind == "q":
                dst = qrw[h][:, 3:3 + TBS]
            elif kind == "k":
                dst = krw[h][:, 3:3 + TBS]
            elif kind == "v":
                dst = vrw[h][:, 3:3 + TBS]
            elif kind == "g":
                dst = gpo[h][:, col0:col0 + TBS]
            elif kind == "a":
                dst = asb[:, col0:col0 + TBS]
            else:
                dst = bsb[:, col0:col0 + TBS]
            evdeps = [mm] + halo0
            if tb >= 2 and delta_fin[tb - 2] is not None:
                evdeps += delta_fin[tb - 2]
            ev = o("vector", (lambda d, bk, ffw: lambda g: g.tensor_copy(d, bk[0:ffw, :]))(dst, bank, fw), deps=evdeps)
            bank_evac[fi % 2] = ev
            evacs[(kind, h)] = ev
        ab_evac[tb] = [evacs[("a", 0)], evacs[("b", 0)]]
        return evacs

    conv_buf_last = {}
    def conv_block(tb, evacs):
        col0 = (tb % 2) * TBS
        finals = []
        qk_i = 0
        v_i = 0
        for kind, raws, posts, sqs, cwo in (("q", qrw, qpo, qsq, 0),
                                            ("k", krw, kpo, ksq, 16),
                                            ("v", vrw, vpo, None, 32)):
            for h in range(HPC):
                raw, pw = raws[h], (64 if kind != "v" else 128)
                eng = "vector"
                if kind == "v":
                    buf = ctmv[v_i % 2]
                    bkey = ("v", v_i % 2)
                    v_i += 1
                else:
                    buf = ctmp[qk_i % 3]
                    bkey = ("qk", qk_i % 3)
                    qk_i += 1
                dep = [evacs[(kind, h)], conv_buf_last.get(bkey)]
                cc = o(eng, (lambda rw, p_, co, bf: lambda g: g.tensor_scalar_mul(
                    bf[0:p_, :], rw[0:p_, 3:3 + TBS], convw_sb[0:p_, co + 3:co + 4]))(raw, pw, cwo + 4 * h, buf), deps=dep)
                for i in (2, 1, 0):
                    cc = o(eng, (lambda rw, p_, co, ii, bf: lambda g: g.scalar_tensor_tensor(
                        bf[0:p_, :], rw[0:p_, ii:ii + TBS], convw_sb[0:p_, co + ii:co + ii + 1],
                        bf[0:p_, :], ALU.mult, ALU.add))(raw, pw, cwo + 4 * h, i, buf), deps=[cc])
                tcp = o(eng, (lambda rw, p_: lambda g: g.tensor_copy(
                    rw[0:p_, 0:3], rw[0:p_, TBS:TBS + 3]))(raw, pw), deps=[cc])
                sil = o("scalar", (lambda po, p_, bf: lambda g: g.activation(
                    po[0:p_, col0:col0 + TBS], bf[0:p_, :], AF.Silu))(posts[h], pw, buf), deps=[cc])
                conv_buf_last[bkey] = sil
                finals.append(tcp)
                finals.append(sil)
                if sqs is not None:
                    sq = o("scalar", (lambda s_, po, p_: lambda g: g.activation(
                        s_[0:p_, col0:col0 + TBS], po[0:p_, col0:col0 + TBS], AF.Square))(sqs[h], posts[h], pw), deps=[sil])
                    finals.append(sq)
        return finals

    def delta_chunk(tb, cisub, convdep):
        ci = tb * CPB + cisub
        col0 = (tb % 2) * TBS
        cc0 = col0 + cisub * L
        banks = [{"rot": 0, "free": [None, None]} for _ in range(NL)]

        def takebank(l):
            b = banks[l]
            i = b["rot"]
            b["rot"] ^= 1
            dep = [b["free"][i]] if b["free"][i] else []
            return pf[l][i], dep, i

        def setreader(l, i, op_):
            banks[l]["free"][i] = op_

        shdep = ([x for x in lane_last if x] + [chunk_shared_last[0]] +
                 ab_evac[tb] + [t_cv, t_eac, mKF, mKB] + convdep)
        bk0, d0, i0 = takebank(0)
        trA = o("tensor", (lambda bk: lambda g: g.transpose(bk[:, 0:4], asb[:, cc0:cc0 + L], idf32[0:4, 0:4]))(bk0), deps=shdep + d0)
        bk1, d1, i1 = takebank(1)
        trB = o("tensor", (lambda bk: lambda g: g.transpose(bk[:, 0:4], bsb[:, cc0:cc0 + L], idf32[0:4, 0:4]))(bk1), deps=shdep + d1)
        za = o("vector", (lambda bk: lambda g: g.tensor_add(cols[:, 12:16], bk[:, 0:4], cvec_sb[:, 0:4]))(bk0), deps=[trA])
        setreader(0, i0, za)
        zb = o("vector", (lambda bk: lambda g: g.tensor_add(cols[:, 16:20], bk[:, 0:4], cvec_sb[:, 4:8]))(bk1), deps=[trB])
        setreader(1, i1, zb)
        bsig = o("scalar", lambda g: g.activation(cols[:, 0:4], cols[:, 16:20], AF.Sigmoid), deps=[zb])
        bet = o("vector", lambda g: g.tensor_scalar_mul(cols[:, 0:4], cols[:, 0:4], 2.0), deps=[bsig])
        sab = o("scalar", lambda g: g.activation(cols[:, 20:24], cols[:, 12:16], AF.Abs), deps=[za])
        sex = o("scalar", lambda g: g.activation(cols[:, 20:24], cols[:, 20:24], AF.Exp, scale=kc("negone")), deps=[sab])
        sln = o("scalar", lambda g: g.activation(cols[:, 20:24], cols[:, 20:24], AF.Ln, bias=kc("one")), deps=[sex])
        srl = o("scalar", lambda g: g.activation(cols[:, 12:16], cols[:, 12:16], AF.Relu), deps=[sln])
        ssp = o("vector", lambda g: g.tensor_add(cols[:, 12:16], cols[:, 12:16], cols[:, 20:24]), deps=[srl])
        gg = o("vector", lambda g: g.tensor_mul(cols[:, 4:8], cols[:, 12:16], negea_bc[:, :]), deps=[ssp, t_eac])
        bk2, d2, i2 = takebank(0)
        gmm = o("tensor", (lambda bk: lambda g: g.matmul(bk[:, 0:4], tuones, cols[:, 4:8], start=True, stop=True))(bk2), deps=d2 + [gg])
        gev = o("vector", (lambda bk: lambda g: g.tensor_copy(gam4[:, :], bk[:, 0:4]))(bk2), deps=[gmm])
        setreader(0, i2, gev)
        bk3, d3, i3 = takebank(1)
        Gmm = o("tensor", (lambda bk: lambda g: g.matmul(bk[0:4, 0:1], cols[:, 4:8], ones_col_f, start=True, stop=True))(bk3), deps=d3 + [gg])
        Gev = o("vector", (lambda bk: lambda g: g.tensor_copy(Gam4[:, :], bk[0:4, 0:1]))(bk3), deps=[Gmm])
        setreader(1, i3, Gev)
        bk4, d4, i4 = takebank(0)
        trG = o("tensor", (lambda bk: lambda g: g.transpose(bk[0:1, 0:4], Gam4[:, :], idf32[0:4, 0:4]))(bk4), deps=d4 + [Gev])
        gre = o("vector", (lambda bk: lambda g: g.tensor_copy(grow[:, :], bk[0:1, 0:4]))(bk4), deps=[trG])
        setreader(0, i4, gre)
        bk5, d5, i5 = takebank(1)
        Gbm = o("tensor", (lambda bk: lambda g: g.matmul(bk[:, 0:4], ones_row_f, grow[:, :], start=True, stop=True))(bk5), deps=d5 + [gre])
        Gbe = o("vector", (lambda bk: lambda g: g.tensor_copy(Gbc[:, :], bk[:, 0:4]))(bk5), deps=[Gbm])
        setreader(1, i5, Gbe)
        eGe = o("scalar", lambda g: g.activation(eGbc[:, :], Gbc[0:64, :], AF.Exp), deps=[Gbe])
        chunk_shared_last[0] = eGe
        if PROBE and STAGE == 1 and tb == 0 and cisub == 0:
            pass

        lane_new_last = [None, None]
        probes = []

        def dump(src, dcol, width, dep, rows=128, cast=False):
            if not (PROBE and STAGE == 1 and tb == 0 and cisub == 0):
                return
            eng = "gpsimd" if cast else "sync"
            probes.append(o(eng, (lambda s_, dc, w_, r_: lambda g: g.dma_start(
                out=dbg2_h[0:r_, dc:dc + w_], in_=s_))(src, dcol, width, rows),
                dma=True, slot="probe", deps=[dep, t_regs]))

        def head_body(h, lane):
            Scur = state_cur[h]
            Snxt = state_nxt[h]
            ldep = [lane_new_last[lane] if lane_new_last[lane] is not None else lane_last[lane]]
            ldep = [x for x in ldep if x is not None]
            ch = colh[h]
            gcol = gam4[:, h:h + 1]
            bcol = cols[:, h:h + 1]
            Gcol = Gbc[:, h:h + 1]
            kchunk = kpo[h][:, cc0:cc0 + L]
            qchunk = qpo[h][:, cc0:cc0 + L]
            vchunk = vpo[h][:, cc0:cc0 + L]
            gchunk = gpo[h][:, cc0:cc0 + L]

            bkS, dS, iS = takebank(lane)
            m1 = o("tensor", (lambda bk: lambda g: g.matmul(bk[:, 0:1], qsq[h][:, cc0:cc0 + L], ones64_bf, start=True, stop=True))(bkS), deps=dS + ldep + [gev, bet, eGe])
            m2 = o("tensor", (lambda bk: lambda g: g.matmul(bk[:, 1:2], ksq[h][:, cc0:cc0 + L], ones64_bf, start=True, stop=True))(bkS), deps=[m1])
            l1 = o("scalar", (lambda bk: lambda g: g.activation(ch[:, 0:2], bk[:, 0:2], AF.Ln, bias=kc("eps6")))(bkS), deps=[m2])
            setreader(lane, iS, l1)
            c_uA = o("vector", lambda g: g.scalar_tensor_tensor(ch[:, 2:3], ch[:, 1:2], -0.5, gcol, ALU.mult, ALU.add), deps=[l1, gev] + ldep)
            c_w = o("vector", lambda g: g.scalar_tensor_tensor(ch[:, 3:4], ch[:, 1:2], 0.5, gcol, ALU.mult, ALU.add), deps=[l1])
            c_nw = o("vector", lambda g: g.tensor_scalar_mul(ch[:, 4:5], ch[:, 3:4], -1.0), deps=[c_w])
            c_a0 = o("vector", lambda g: g.scalar_tensor_tensor(ch[:, 5:6], ch[:, 0:1], -0.5, gcol, ALU.mult, ALU.add), deps=[l1])
            c_a = o("vector", lambda g: g.tensor_scalar_add(ch[:, 5:6], ch[:, 5:6], LNS), deps=[c_a0])
            c_pk = o("scalar", lambda g: g.activation(ch[:, 6:7], ch[:, 2:3], AF.Exp), deps=[c_uA])
            c_qs = o("scalar", lambda g: g.activation(ch[:, 7:8], ch[:, 5:6], AF.Exp), deps=[c_a])
            c_r0 = o("vector", lambda g: g.tensor_sub(ch[:, 8:9], Gcol, ch[:, 3:4]), deps=[c_w, Gbe])
            c_rr = o("scalar", lambda g: g.activation(ch[:, 8:9], ch[:, 8:9], AF.Exp), deps=[c_r0])

            bkW, dW, iW = takebank(lane)
            trW = o("tensor", (lambda bk: lambda g: g.transpose(bk[0:1, :], ch[:, 3:4], idf32))(bkW), deps=dW + [c_w])
            evW = o("vector", (lambda bk: lambda g: g.tensor_copy(rowsf[lane][0:1, 0:128], bk[0:1, :]))(bkW), deps=[trW])
            setreader(lane, iW, evW)
            bkA_, dA_, iA_ = takebank(lane)
            trA2 = o("tensor", (lambda bk: lambda g: g.transpose(bk[0:1, :], ch[:, 5:6], idf32))(bkA_), deps=dA_ + [c_a])
            evA2 = o("vector", (lambda bk: lambda g: g.tensor_copy(rowsf[lane][0:1, 128:256], bk[0:1, :]))(bkA_), deps=[trA2])
            setreader(lane, iA_, evA2)

            bkG, dG, iG = takebank(lane)
            G2A = o("tensor", (lambda bk: lambda g: g.matmul(bk[:, :], ones_row_f, rowsf[lane][0:1, 0:128], start=True, stop=True))(bkG), deps=dG + [evW])
            Dm = o("vector", (lambda bk: lambda g: g.scalar_tensor_tensor(
                dA[lane][:, :], maskA, ch[:, 2:3], bk[:, :], ALU.add, ALU.subtract))(bkG), deps=[c_uA, mKF, G2A] + ldep)
            setreader(lane, iG, Dm)
            Aex = o("scalar", lambda g: g.activation(dA[lane][:, :], dA[lane][:, :], AF.Exp), deps=[Dm])
            bkK, dK, iK = takebank(lane)
            KKt = o("tensor", (lambda bk: lambda g: g.matmul(bk[:, :], kchunk, kchunk, start=True, stop=True))(bkK), deps=dK + [Aex])
            Af = o("vector", (lambda bk: lambda g: g.tensor_mul(osc[lane][:, :], dA[lane][:, :], bk[:, :]))(bkK), deps=[KKt, Aex])
            setreader(lane, iK, Af)
            Amat = o("vector", lambda g: g.tensor_scalar_mul(bA[lane][:, :], osc[lane][:, :], bcol), deps=[Af, bet] + ldep)

            bkG2, dG2, iG2 = takebank(lane)
            G2M = o("tensor", (lambda bk: lambda g: g.matmul(bk[:, :], ones_row_f, rowsf[lane][0:1, 128:256], start=True, stop=True))(bkG2), deps=dG2 + [evA2])
            DmM = o("vector", (lambda bk: lambda g: g.scalar_tensor_tensor(
                dM[lane][:, :], maskMT, ch[:, 4:5], bk[:, :], ALU.add, ALU.add))(bkG2), deps=[c_nw, mKF, G2M] + ldep)
            setreader(lane, iG2, DmM)
            MTe = o("scalar", lambda g: g.activation(dM[lane][:, :], dM[lane][:, :], AF.Exp), deps=[DmM])
            bkQ, dQ, iQ = takebank(lane)
            KQt = o("tensor", (lambda bk: lambda g: g.matmul(bk[:, :], kchunk, qchunk, start=True, stop=True))(bkQ), deps=dQ + [MTe])
            MTm = o("vector", (lambda bk: lambda g: g.tensor_mul(bMT[lane][:, :], dM[lane][:, :], bk[:, :]))(bkQ), deps=[KQt, MTe] + ldep)
            setreader(lane, iQ, MTm)

            Scp = o("scalar", lambda g: g.activation(bS16[lane][:, :], Scur[:, :], AF.Copy), deps=[state_op[h]] + ldep)
            bkU, dU, iU = takebank(lane)
            KS0 = o("tensor", (lambda bk: lambda g: g.matmul(bk[:, :], kchunk, bS16[lane][:, :], start=True, stop=True))(bkU), deps=dU + [Scp, MTm])
            Vtr = o("tensor", lambda g: g.transpose(pbt[lane][:, :], vchunk, id_bf), deps=[mKB, m1] + ldep)
            Vev = o("vector", lambda g: g.tensor_copy(bV[lane][:, :], pbt[lane][:, :]), deps=[Vtr] + ldep)
            u1 = o("vector", (lambda bk: lambda g: g.scalar_tensor_tensor(uf[lane][:, :], bk[:, :], ch[:, 6:7], bV[lane][:, :], ALU.mult, ALU.subtract))(bkU), deps=[KS0, Vev, c_pk] + ldep)
            setreader(lane, iU, u1)
            u2 = o("vector", lambda g: g.tensor_scalar(uf[lane][:, :], uf[lane][:, :], bcol, -1.0, ALU.mult, ALU.mult), deps=[u1, bet])
            ub = o("scalar", lambda g: g.activation(bu[lane][:, :], uf[lane][:, :], AF.Copy), deps=[u2] + ldep)

            TA0t = o("tensor", lambda g: g.transpose(pbt[lane][:, :], bA[lane][:, :], id_bf), deps=[Amat, Vev])
            TA0e = o("vector", lambda g: g.tensor_copy(bTA0[lane][:, :], pbt[lane][:, :]), deps=[TA0t] + ldep)
            Pcur, TAcur = bA[lane], bTA0[lane]
            Pop, TAop = Amat, TA0e
            tev = None
            for lev in range(1, 7):
                bkp, dp, ip_ = takebank(lane)
                tmm = o("tensor", (lambda bk, Pc, TAc: lambda g: g.matmul(bk[:, :], Pc[:, :], TAc[:, :], start=True, stop=True))(bkp, Pcur, TAcur), deps=dp + [TAop, Pop])
                tev = ev_copy(bTAs[lane][lev - 1][:, :], bkp[:, :], [tmm] + ldep)
                setreader(lane, ip_, tev)
                if lev < 6:
                    bkq, dq, iq_ = takebank(lane)
                    pmm = o("tensor", (lambda bk, Pc, TAc: lambda g: g.matmul(bk[:, :], TAc[:, :], Pc[:, :], start=True, stop=True))(bkq, Pcur, TAcur), deps=dq + [TAop, Pop])
                    pev = ev_copy(bPs[lane][lev % 2][:, :], bkq[:, :], [pmm] + ldep)
                    setreader(lane, iq_, pev)
                    Pcur, Pop = bPs[lane][lev % 2], pev
                TAcur, TAop = bTAs[lane][lev - 1], tev
            tprev_f, top_f, top_b = uf[lane], u2, ub
            for lev in range(6, 0, -1):
                bkp, dp, ip_ = takebank(lane)
                amm = o("tensor", (lambda bk, TAl: lambda g: g.matmul(bk[:, :], TAl[:, :], bu[lane][:, :], start=True, stop=True))(bkp, bTAs[lane][lev - 1]), deps=dp + [top_b, tev])
                att = o("vector", (lambda bk, tp: lambda g: g.tensor_add(tf[lane][:, :], tp[:, :], bk[:, :]))(bkp, tprev_f), deps=[amm, top_f])
                setreader(lane, ip_, att)
                acp = o("scalar", lambda g: g.activation(bu[lane][:, :], tf[lane][:, :], AF.Copy), deps=[att])
                tprev_f, top_f, top_b = tf[lane], att, acp
            bkp2, dp2, ip2 = takebank(lane)
            fmm = o("tensor", (lambda bk: lambda g: g.matmul(bk[:, :], bTA0[lane][:, :], bu[lane][:, :], start=True, stop=True))(bkp2), deps=dp2 + [top_b])
            fde = o("vector", (lambda bk: lambda g: g.tensor_sub(tf[lane][:, :], tf[lane][:, :], bk[:, :]))(bkp2), deps=[fmm, top_f])
            setreader(lane, ip2, fde)
            dcp = o("scalar", lambda g: g.activation(bu[lane][:, :], tf[lane][:, :], AF.Copy), deps=[fde])

            bkO, dO, iO = takebank(lane)
            QS0 = o("tensor", (lambda bk: lambda g: g.matmul(bk[:, :], qchunk, bS16[lane][:, :], start=True, stop=True))(bkO), deps=dO + [Scp, dcp])
            o1 = o("vector", (lambda bk: lambda g: g.tensor_scalar_mul(osc[lane][:, :], bk[:, :], ch[:, 7:8]))(bkO), deps=[QS0, c_qs, Af])
            setreader(lane, iO, o1)
            bkO2, dO2, iO2 = takebank(lane)
            Mdl = o("tensor", (lambda bk: lambda g: g.matmul(bk[:, :], bMT[lane][:, :], bu[lane][:, :], start=True, stop=True))(bkO2), deps=dO2 + [dcp])
            oo = o("vector", (lambda bk: lambda g: g.tensor_add(off_[lane][:, :], osc[lane][:, :], bk[:, :]))(bkO2), deps=[Mdl, o1] + ldep)
            setreader(lane, iO2, oo)

            dr = o("vector", lambda g: g.tensor_scalar_mul(bu[lane][:, :], tf[lane][:, :], ch[:, 8:9]), deps=[Mdl, c_rr, fde])
            Ktr = o("tensor", lambda g: g.transpose(pbt[lane][0:128, 0:64], kchunk, id_bf[0:64, 0:64]), deps=[TA0e, Vev])
            Kev = o("vector", lambda g: g.tensor_copy(bK[lane][:, :], pbt[lane][0:128, 0:64]), deps=[Ktr] + ldep)
            bkSU, dSU, iSU = takebank(lane)
            SMM = o("tensor", (lambda bk: lambda g: g.matmul(bk[0:64, :], bK[lane][:, :], bu[lane][:, :], start=True, stop=True))(bkSU), deps=dSU + [dr, Kev])
            Sup = o("vector", (lambda bk: lambda g: g.scalar_tensor_tensor(
                Snxt[:, :], Scur[:, :], eGbc[0:64, h:h + 1], bk[0:64, :], ALU.mult, ALU.add))(bkSU), deps=[SMM, state_op[h], eGe])
            setreader(lane, iSU, Sup)
            state_op[h] = Sup

            if h == 0:
                dump(cols[:, 0:24], 0, 24, gg)
                dump(gam4[:, :], 24, 4, gev)
                dump(Gbc[:, 0:4], 28, 4, Gbe)
                dump(colh[0][:, 0:12], 32, 12, c_rr)
                dump(qpo[0][:, cc0:cc0 + L], 128, 128, m1, rows=64, cast=True)
                dump(ksq[0][:, cc0:cc0 + L], 256, 128, m1, rows=64, cast=True)
                dump(dA[lane][:, :], 384, 128, Aex)
                dump(bA[lane][:, :], 512, 128, Amat, cast=True)
                dump(uf[lane][:, :], 640, 128, u2)
                dump(tf[lane][:, :], 768, 128, fde)
                dump(off_[lane][:, :], 896, 128, oo)
                dump(bMT[lane][:, :], 1024, 128, MTm, cast=True)
                dump(bV[lane][:, :], 1152, 128, Vev, cast=True)
                dump(cvec_sb[:, 0:12], 44, 12, t_cv)
                dump(negea_bc[:, 0:4], 56, 4, t_eac)
            sqo = o("scalar", lambda g: g.activation(osc[lane][:, :], off_[lane][:, :], AF.Square), deps=[oo, o1])
            rsum = o("vector", lambda g: g.reduce_sum(ch[:, 9:10], osc[lane][:, :], AX), deps=[sqo])
            rl = o("scalar", lambda g: g.activation(ch[:, 10:11], ch[:, 9:10], AF.Ln, scale=kc("inv128"), bias=kc("epsf")), deps=[rsum])
            re = o("scalar", lambda g: g.activation(ch[:, 10:11], ch[:, 10:11], AF.Exp, scale=kc("neghalf")), deps=[rl])
            on = o("vector", lambda g: g.tensor_scalar_mul(bon[lane][:, :], off_[lane][:, :], ch[:, 10:11]), deps=[re, oo])
            ontr = o("tensor", lambda g: g.transpose(pbt[lane][:, :], bon[lane][:, :], id_bf), deps=[on, Kev])
            gsl = o("scalar", lambda g: g.activation(gsil[lane][:, :], gchunk, AF.Silu), deps=ldep + [re])
            og = o("vector", (lambda hh, cis: lambda g: g.tensor_mul(
                oslab[:, hh * TBS + cis * L: hh * TBS + (cis + 1) * L], pbt[lane][:, :], gsil[lane][:, :]))(h, cisub),
                deps=[ontr, gsl, Sup] + probes + ([slab_fin[tb - 1][h]] if tb > 0 else []))
            lane_new_last[lane] = og

        for h in range(HPC):
            head_body(h, h % 2)
        for lpos in range(NL):
            if lane_new_last[lpos] is not None:
                lane_last[lpos] = lane_new_last[lpos]
        return [x for x in lane_new_last if x]

    def do_delta(tb):
        nonlocal state_cur, state_nxt
        dfin = []
        for cisub in range(CPB):
            dfin = delta_chunk(tb, cisub, conv_fin[tb])
            state_cur, state_nxt = state_nxt, state_cur
        delta_fin[tb] = dfin
        for h in range(HPC):
            hd = [lane_last[h % 2]]
            tgt = myslab_h if FUSED else slab_h
            sd = o("sync", (lambda hh, ttb, tg: lambda g: g.dma_start(
                out=tg[hh * 128:(hh + 1) * 128, ttb * TBS:(ttb + 1) * TBS],
                in_=oslab[:, hh * TBS:(hh + 1) * TBS]))(h, tb, tgt),
                dma=True, slot=f"slab{h}", deps=hd + [t_regs])
            slab_fin[tb][h] = sd

    if P1:
        for tb in range(NTB):
            if tb + 1 < NTB:
                ld_x[tb + 1] = o("gpsimd", (lambda t: lambda g: g.dma_start(
                    out=xsb[t % 2][:, :], in_=xT_h[t, :, :]))(tb + 1), dma=True,
                    slot=f"x{(tb + 1) % 2}",
                    deps=[ld_x[tb]] + ([bank_evac[0], bank_evac[1]] if tb >= 1 else []))
            evacs = proj_block(tb)
            conv_fin[tb] = conv_block(tb, evacs)
            if tb >= 1:
                do_delta(tb - 1)
        do_delta(NTB - 1)


    cc_op = None
    if FUSED:
        allslab = [slab_fin[tb][h] for tb in range(NTB) for h in range(HPC)]
        cc_op = o("gpsimd", lambda g: g.collective_compute(
            "AllGather", ALU.bypass, replica_groups=[list(range(8))],
            ins=[myslab_h.ap()],
            outs=[ccagg_h.ap()]),
            dma=True, slot="cc", inc=1, deps=allslab + [t_regs])

    if P3:
        if P1:
            es1.close()
        es2 = ExitStack()
        E2 = es2.enter_context
        x1fm = E2(st("x1fm", [128, 16 * TOK], BF16))
        comb_sb = E2(st("comb_sb", [128, 144], F32))
        combfm = E2(st("combfm", [16, TOK], F32))
        combbf = E2(st("combbf", [16, TOK], BF16))
        sgvsig = E2(st("sgvsig", [128, 8], F32))
        gwT_sb = E2(st("gwT_sb", [128, 16 * 16], BF16))
        sgvT_sb = E2(st("sgvT_sb", [128, 16], BF16))
        pq = [E2(pt(f"pq{i}", [128, 512], F32)) for i in range(4)]

        ld3 = []
        ld3.append(o("gpsimd", lambda g: g.dma_start(out=gwT_sb[:, :], in_=gwT_h[:, :]), dma=True, slot="m3", deps=[cc_op]))
        ld3.append(o("gpsimd", lambda g: g.dma_start(out=sgvT_sb[:, :], in_=sgvT_h[:, :]), dma=True, slot="m3", deps=[cc_op]))
        m3 = ld3[-1]
        dfin_gate = delta_fin[NTB - 1][0] if P1 else None

        es3 = ExitStack()
        E3 = es3.enter_context
        pr = E3(pt("pr", [128, 128], F32))
        pxb = E3(pt("pxb", [128, 128], BF16))
        xrt = [E3(st(f"xrt{i}", [128, 2048], F32)) for i in range(2)]
        x1t = [E3(st(f"x1t{i}", [128, 2048], F32)) for i in range(2)]
        sscr = E3(st("sscr", [128, 2048], BF16))
        rscr = E3(st("rscr", [128, 64], F32))
        xbf = E3(st("xbf", [128, 2048], BF16))
        ogth = E3(st("ogth", [128, 16 * TOK], BF16))
        owT_sb = E3(st("owT_sb", [128, 16 * 2048], BF16))

        og_ops = []
        if FUSED:
            og_dyn = []
            for q in range(4):
                og_dyn.append(o("sync", (lambda qq: lambda g: g.dma_start(
                    out=ogst_h[qq * 512:(qq + 1) * 512, :],
                    in_=ccagg_h.ap()[bass.ds(regs["row2"] + qq * 512, 512),
                                     bass.ds(regs["col2"], TOK)]))(q),
                    dma=True, slot="ogd", deps=[cc_op, t_regs]))
            for k in range(16):
                og_ops.append(o("sync", (lambda kk: lambda g: g.dma_start(
                    out=ogth[:, kk * TOK:(kk + 1) * TOK],
                    in_=ogst_h[kk * 128:(kk + 1) * 128, :]))(k),
                    dma=True, slot="og", deps=og_dyn))
        else:
            og_ops.append(o("sync", lambda g: g.dma_start(out=ogth[:, :], in_=ogin_h[:, :]),
                            dma=True, slot="og", deps=[]))
        ow_ops = []
        for q in range(4):
            ow_ops.append(o("gpsimd", (lambda qq: lambda g: g.dma_start(
                out=owT_sb[:, qq * 4 * 2048:(qq + 1) * 4 * 2048],
                in_=owT_h.ap()[qq * 4:(qq + 1) * 4, :, :].rearrange("k p d -> p k d")))(q),
                dma=True, slot="ow", deps=[cc_op]))

        LNV = {}
        x1w_ops = []
        ln_done = []
        rt_last = [None]
        xr_ops = [None, None]
        xrt_free = [None, None]
        x1t_free = [None, None]
        pq_free = [None, None, None, None]
        pxb_free = [None]
        pr_free = [None]
        for tt in range(8):
            xr_ops[tt % 2] = o("sync", (lambda t_: lambda g: g.dma_start(
                out=xrt[t_ % 2][:, :], in_=xrows_h[t_ * 128:(t_ + 1) * 128, :]))(tt),
                dma=True, slot=f"xr{tt % 2}", deps=[cc_op, xrt_free[tt % 2]])
            attev = []
            for oc in range(4):
                bk = pq[oc % 2]
                mmd = og_ops + ow_ops + [pq_free[oc % 2], dfin_gate]
                mm = None
                for k in range(16):
                    mm = o("tensor", (lambda kk, occ, bkk, t_: lambda g: g.matmul(
                        bkk[:, :], ogth[:, kk * TOK + t_ * 128: kk * TOK + (t_ + 1) * 128],
                        owT_sb[:, kk * 2048 + occ * 512: kk * 2048 + (occ + 1) * 512],
                        start=(kk == 0), stop=(kk == 15)))(k, oc, bk, tt),
                        deps=(mmd if k == 0 else ()))
                ev = o("vector", (lambda occ, bkk, t_: lambda g: g.tensor_add(
                    x1t[t_ % 2][:, occ * 512:(occ + 1) * 512], xrt[t_ % 2][:, occ * 512:(occ + 1) * 512],
                    bkk[:, :]))(oc, bk, tt), deps=[mm, xr_ops[tt % 2], x1t_free[tt % 2]])
                pq_free[oc % 2] = ev
                xrt_free[tt % 2] = ev
                attev.append(ev)
            xt = x1t[tt % 2]
            rs = o("vector", lambda g, xt=xt: g.reduce_sum(comb_sb[:, 133:134], xt[:, :], AX), deps=attev)
            mn = o("vector", lambda g: g.tensor_scalar_mul(comb_sb[:, 128:129], comb_sb[:, 133:134], 1.0 / 2048), deps=[rs])
            sq = o("scalar", lambda g, xt=xt: g.activation(sscr[:, :], xt[:, :], AF.Square), deps=attev + [ln_done[-1] if ln_done else None, rt_last[0]])
            ss = o("vector", lambda g: g.reduce_sum(comb_sb[:, 129:130], sscr[:, :], AX), deps=[sq])
            ms = o("vector", lambda g: g.tensor_mul(comb_sb[:, 130:131], comb_sb[:, 128:129], comb_sb[:, 128:129]), deps=[mn])
            vr = o("vector", lambda g: g.scalar_tensor_tensor(comb_sb[:, 131:132], comb_sb[:, 129:130], 1.0 / 2048, comb_sb[:, 130:131], ALU.mult, ALU.subtract), deps=[ss, ms])
            rl = o("scalar", lambda g: g.activation(comb_sb[:, 132:133], comb_sb[:, 131:132], AF.Ln, bias=kc("eps5")), deps=[vr])
            re_ = o("scalar", lambda g: g.activation(comb_sb[:, 132:133], comb_sb[:, 132:133], AF.Exp, scale=kc("neghalf")), deps=[rl])
            x1n = o("vector", lambda g, xt=xt: g.tensor_scalar(xt[:, :], xt[:, :], comb_sb[:, 128:129], comb_sb[:, 132:133], ALU.subtract, ALU.mult), deps=[re_, mn] + attev)
            ln_done.append(x1n)
            x1w = o("sync", (lambda t_, xt=xt: lambda g: g.dma_start(
                out=x1sp_h[t_ * 128:(t_ + 1) * 128, :], in_=xt[:, :]))(tt), dma=True, slot="x1w", deps=[x1n])
            x1w_ops.append(x1w)
            cb = o("scalar", lambda g, xt=xt: g.activation(xbf[:, :], xt[:, :], AF.Copy), deps=[x1n, rt_last[0], x1w])
            x1t_free[tt % 2] = cb
            for k in range(16):
                trx = o("tensor", (lambda kk: lambda g: g.transpose(
                    pxb[:, :], xbf[:, kk * 128:(kk + 1) * 128], id_bf))(k), deps=[cb, pxb_free[0], mKB])
                evx = o("vector", (lambda kk, t_: lambda g: g.tensor_copy(
                    x1fm[:, kk * TOK + t_ * 128: kk * TOK + (t_ + 1) * 128], pxb[:, :]))(k, tt), deps=[trx])
                pxb_free[0] = evx
            xfm_done = evx
            rmm = None
            for k in range(16):
                rmm = o("tensor", (lambda kk, t_: lambda g: g.matmul(
                    pr[:, 0:16], x1fm[:, kk * TOK + t_ * 128: kk * TOK + (t_ + 1) * 128],
                    gwT_sb[:, kk * 16:(kk + 1) * 16], start=(kk == 0), stop=(kk == 15)))(k, tt),
                    deps=([xfm_done, m3, pr_free[0]] if k == 0 else ()))
            smm = None
            for k in range(16):
                smm = o("tensor", (lambda kk, t_: lambda g: g.matmul(
                    pr[:, 16:17], x1fm[:, kk * TOK + t_ * 128: kk * TOK + (t_ + 1) * 128],
                    sgvT_sb[:, kk:kk + 1], start=(kk == 0), stop=(kk == 15)))(k, tt),
                    deps=([rmm] if k == 0 else ()))
            sgv_o = o("scalar", (lambda t_: lambda g: g.activation(sgvsig[:, t_:t_ + 1], pr[:, 16:17], AF.Sigmoid))(tt), deps=[smm])
            lcp = o("vector", lambda g: g.tensor_copy(rscr[:, 0:16], pr[:, 0:16]), deps=[rmm, sq])
            pr_free[0] = lcp
            lmx = o("vector", lambda g: g.reduce_max(rscr[:, 16:17], rscr[:, 0:16], AX), deps=[lcp])
            nmx = o("vector", lambda g: g.tensor_scalar_mul(rscr[:, 16:17], rscr[:, 16:17], -1.0), deps=[lmx])
            pex = o("scalar", lambda g: g.activation(rscr[:, 0:16], rscr[:, 0:16], AF.Exp, bias=rscr[:, 16:17]), deps=[nmx])
            wcp = o("vector", lambda g: g.tensor_copy(rscr[:, 32:48], rscr[:, 0:16]), deps=[pex])
            last = wcp
            for rnd in range(3):
                mx = o("vector", lambda g: g.reduce_max(rscr[:, 17:18], rscr[:, 32:48], AX), deps=[last])
                mk = o("vector", lambda g: g.tensor_scalar(rscr[:, 48:64], rscr[:, 32:48], rscr[:, 17:18], -1e30, ALU.is_ge, ALU.mult), deps=[mx])
                ad = o("vector", lambda g: g.tensor_add(rscr[:, 32:48], rscr[:, 32:48], rscr[:, 48:64]), deps=[mk])
                last = ad
            m4 = o("vector", lambda g: g.reduce_max(rscr[:, 17:18], rscr[:, 32:48], AX), deps=[last])
            kp = o("vector", lambda g: g.tensor_single_scalar(rscr[:, 32:48], rscr[:, 0:16], rscr[:, 17:18], ALU.is_ge), deps=[m4])
            cw = o("vector", lambda g: g.tensor_mul(rscr[:, 0:16], rscr[:, 0:16], rscr[:, 32:48]), deps=[kp])
            sm = o("vector", lambda g: g.reduce_sum(rscr[:, 16:17], rscr[:, 0:16], AX), deps=[cw])
            rc = o("vector", lambda g: g.reciprocal(rscr[:, 16:17], rscr[:, 16:17]), deps=[sm])
            cmb = o("vector", (lambda t_: lambda g: g.tensor_scalar_mul(
                comb_sb[:, t_ * 16:(t_ + 1) * 16], rscr[:, 0:16], rscr[:, 16:17]))(tt), deps=[rc])
            trc = o("tensor", (lambda t_: lambda g: g.transpose(
                pr[0:16, 0:128], comb_sb[:, t_ * 16:(t_ + 1) * 16], idf32))(tt), deps=[cmb, pr_free[0]])
            evc = o("vector", (lambda t_: lambda g: g.tensor_copy(
                combfm[:, t_ * 128:(t_ + 1) * 128], pr[0:16, 0:128]))(tt), deps=[trc])
            pr_free[0] = evc
            rt_last[0] = evc
        cbf = o("vector", lambda g: g.tensor_copy(combbf[:, :], combfm[:, :]), deps=[rt_last[0]])
    if P3:
        es3.close()
        sgbuf = [E2(st(f"sgbuf{i}", [128, 16 * 128], BF16)) for i in range(4)]
        silb = [E2(st(f"silb{i}", [128, 512], BF16)) for i in range(2)]
        stg = [E2(st(f"stg{i}", [128, 512], F32)) for i in range(2)]
        es4 = ExitStack()
        E4 = es4.enter_context
        hS = E4(st("hS", [128, 44 * 512], BF16))
        sdoc = [E4(st("sdoc0", [128, 44 * 512], BF16))] * 2

        ph3a_done = [cbf] + x1w_ops + ln_done
        acc_last = [None]

        sg_free = [None, None, None, None]
        sd_free = [None, None]
        hs_free = [None]
        psg_free = [None, None]
        pdn_free = [None, None]
        silb_free_sh = [None, None]
        stg_free_sh = [None, None]
        for th in range(2):
            hs_ops = []
            for j in range(44):
                bi = j % 2
                dg = o("gpsimd", (lambda jj, bb, t_: lambda g: g.dma_start(
                    out=sgbuf[bb][:, :], in_=sguT_h[jj, :, :]))(j, bi, th),
                    dma=True, slot=f"sg{bi}", deps=[sg_free[bi], ph3a_done[0], cc_op])
                du = o("gpsimd", (lambda jj, bb, t_: lambda g: g.dma_start(
                    out=sgbuf[2 + bb][:, :], in_=sguT_h[44 + jj, :, :]))(j, bi, th),
                    dma=True, slot=f"sg{2 + bi}", deps=[sg_free[2 + bi], ph3a_done[0], cc_op])
                mmg = None
                for k in range(16):
                    mmg = o("tensor", (lambda kk, bb, t_: lambda g: g.matmul(
                        pq[0][:, :], sgbuf[bb][:, kk * 128:(kk + 1) * 128],
                        x1fm[:, kk * TOK + t_ * 512: kk * TOK + (t_ + 1) * 512],
                        start=(kk == 0), stop=(kk == 15)))(k, bi, th),
                        deps=([dg, psg_free[0], cbf] if k == 0 else ()))
                mmu = None
                for k in range(16):
                    mmu = o("tensor", (lambda kk, bb, t_: lambda g: g.matmul(
                        pq[1][:, :], sgbuf[2 + bb][:, kk * 128:(kk + 1) * 128],
                        x1fm[:, kk * TOK + t_ * 512: kk * TOK + (t_ + 1) * 512],
                        start=(kk == 0), stop=(kk == 15)))(k, bi, th),
                        deps=([du, psg_free[1]] if k == 0 else ()))
                sil = o("scalar", (lambda bb: lambda g: g.activation(
                    silb[bb][:, :], pq[0][:, :], AF.Silu))(bi), deps=[mmg, hs_free[0], silb_free_sh[bi]])
                psg_free[0] = sil
                hsm = o("vector", (lambda jj, bb: lambda g: g.tensor_mul(
                    hS[:, jj * 512:(jj + 1) * 512], silb[bb][:, :], pq[1][:, :]))(j, bi), deps=[sil, mmu])
                psg_free[1] = hsm
                silb_free_sh[bi] = hsm
                sg_free[bi] = mmg
                sg_free[2 + bi] = mmu
                hs_ops.append(hsm)
            dn_ops = []
            for oc in range(4):
                sb_i = 0
                sld = o("gpsimd", (lambda occ, sb: lambda g: g.dma_start(
                    out=sdoc[sb][:, :], in_=sdT_h[occ, :, :]))(oc, sb_i),
                    dma=True, slot=f"sd{sb_i}", deps=[sd_free[sb_i], hs_ops[0] if oc < 2 else None])
                for ts_ in range(4):
                    bk = pq[2 + ts_ % 2]
                    mmd = None
                    for j in range(44):
                        mmd = o("tensor", (lambda jj, occ, tss, bkk, t_, sb: lambda g: g.matmul(
                            bkk[:, :], hS[:, jj * 512 + tss * 128: jj * 512 + (tss + 1) * 128],
                            sdoc[sb][:, jj * 512:(jj + 1) * 512],
                            start=(jj == 0), stop=(jj == 43)))(j, oc, ts_, bk, th, sb_i),
                            deps=(([sld, pdn_free[ts_ % 2]] + hs_ops) if j == 0 else ()))
                    stv = o("vector", (lambda tss, bkk, t_: lambda g: g.tensor_scalar_mul(
                        stg[tss % 2][:, :], bkk[:, :], sgvsig[:, t_ * 4 + tss: t_ * 4 + tss + 1]))(ts_, bk, th),
                        deps=[mmd, sgv_o, acc_last[0], stg_free_sh[ts_ % 2]])
                    pdn_free[ts_ % 2] = stv
                    wr = o("gpsimd", (lambda tss, occ, t_, ss: lambda g: g.dma_start(
                        out=acc_h[(t_ * 4 + tss) * 128:(t_ * 4 + tss + 1) * 128, occ * 512:(occ + 1) * 512],
                        in_=ss[:, :]))(ts_, oc, th, stg[ts_ % 2]),
                        dma=True, slot="acc", deps=[stv])
                    stg_free_sh[ts_ % 2] = wr
                    dn_ops.append(wr)
                sd_free[sb_i] = dn_ops[-1] if dn_ops else None
            hs_free[0] = dn_ops[-1]
            acc_last[0] = dn_ops[-1]
        shared_done = acc_last[0]
        es4.close()
        es5 = ExitStack()
        E5 = es5.enter_context
        sel16_sb = E5(st("sel16_sb", [16, 16 * 128], BF16))
        sel_ld = o("sync", lambda g: g.dma_start(out=sel16_sb[:, :], in_=sel16_h[:, :]), dma=True, slot="m3", deps=[cc_op, shared_done])
        cbsb = [E5(st(f"cbsb{i}", [128, 1024], BF16)) for i in range(2)]
        hE = [E5(st(f"hE{i}", [128, 11 * 1024], BF16)) for i in range(2)]
        wdb = [E5(st(f"wdb{i}", [128, 22 * 512], BF16)) for i in range(2)]
        pd = [E5(pt(f"pd{i}", [128, 512], F32)) for i in range(2)]

        wd_free = [None, None]
        he_free = [None, None]
        pge_free = list(psg_free) + list(pdn_free)
        pde_free = [None, None]
        silb_free = [None, None]
        stg_free = [None, None]
        exp_add_last = [shared_done]
        for e in range(NE):
            eb = e % 2
            cbops = []
            for tb2 in range(2):
                cbm = o("tensor", (lambda ee, tbb: lambda g: g.matmul(
                    pq[0][:, :], sel16_sb[:, ee * 128:(ee + 1) * 128],
                    combbf[:, tbb * 512:(tbb + 1) * 512], start=True, stop=True))(e, tb2),
                    deps=[cbf, pge_free[0], pde_free[0], sel_ld, exp_add_last[0] if e == 0 else None])
                cbe = o("vector", (lambda tbb, bb: lambda g: g.tensor_copy(
                    cbsb[bb][:, tbb * 512:(tbb + 1) * 512], pq[0][:, :]))(tb2, eb), deps=[cbm, he_free[eb]])
                pge_free[0] = cbe
                pde_free[0] = cbe
                cbops.append(cbe)
            he_ops = []
            for j in range(11):
                bi = j % 2
                dg = o("gpsimd", (lambda ee, jj, bb: lambda g: g.dma_start(
                    out=sgbuf[bb][:, :], in_=wguT_h[ee, jj, :, :]))(e, j, bi),
                    dma=True, slot=f"sg{bi}", deps=[sg_free[bi]])
                du = o("gpsimd", (lambda ee, jj, bb: lambda g: g.dma_start(
                    out=sgbuf[2 + bb][:, :], in_=wguT_h[ee, 11 + jj, :, :]))(e, j, bi),
                    dma=True, slot=f"sg{2 + bi}", deps=[sg_free[2 + bi]])
                mmg = [None, None]
                for tb2 in range(2):
                    for k in range(16):
                        mmg[tb2] = o("tensor", (lambda kk, bb, tbb: lambda g: g.matmul(
                            pq[0 + tbb][:, :], sgbuf[bb][:, kk * 128:(kk + 1) * 128],
                            x1fm[:, kk * TOK + tbb * 512: kk * TOK + (tbb + 1) * 512],
                            start=(kk == 0), stop=(kk == 15)))(k, bi, tb2),
                            deps=([dg, pge_free[tb2]] if k == 0 else ()))
                mmu = [None, None]
                for tb2 in range(2):
                    for k in range(16):
                        mmu[tb2] = o("tensor", (lambda kk, bb, tbb: lambda g: g.matmul(
                            pq[2 + tbb][:, :], sgbuf[2 + bb][:, kk * 128:(kk + 1) * 128],
                            x1fm[:, kk * TOK + tbb * 512: kk * TOK + (tbb + 1) * 512],
                            start=(kk == 0), stop=(kk == 15)))(k, bi, tb2),
                            deps=([du, pge_free[2 + tb2]] if k == 0 else ()))
                for tb2 in range(2):
                    sil = o("scalar", (lambda bb, tbb: lambda g: g.activation(
                        silb[bb][:, :], pq[0 + tbb][:, :], AF.Silu))(tb2 % 2, tb2), deps=[mmg[tb2], he_free[eb], silb_free[tb2 % 2]])
                    pge_free[tb2] = sil
                    hsm = o("vector", (lambda jj, bb, tbb, hb: lambda g: g.scalar_tensor_tensor(
                        hE[hb][:, jj * 1024 + tbb * 512: jj * 1024 + (tbb + 1) * 512],
                        pq[2 + tbb][:, :], 1.0, silb[bb][:, :], ALU.mult, ALU.mult))(j, tb2 % 2, tb2, eb), deps=[sil, mmu[tb2], cbops[-1]])
                    hcm = o("vector", (lambda jj, tbb, hb, bb: lambda g: g.tensor_mul(
                        hE[hb][:, jj * 1024 + tbb * 512: jj * 1024 + (tbb + 1) * 512],
                        hE[hb][:, jj * 1024 + tbb * 512: jj * 1024 + (tbb + 1) * 512],
                        cbsb[bb][:, tbb * 512:(tbb + 1) * 512]))(j, tb2, eb, eb), deps=[hsm] + cbops)
                    pge_free[2 + tb2] = hcm
                    silb_free[tb2 % 2] = hsm
                    he_ops.append(hcm)
                sg_free[bi] = mmg[1]
                sg_free[2 + bi] = mmu[1]
            dn_last = None
            for ocp in range(2):
                wdd = o("gpsimd", (lambda ee, op_, bb: lambda g: g.dma_start(
                    out=wdb[bb][:, :],
                    in_=wdT4_h.ap()[ee, 2 * op_:2 * op_ + 2, :, :].rearrange("a p t -> p a t")))(e, ocp, eb),
                    dma=True, slot=f"wd{eb}", deps=[wd_free[eb], he_ops[0] if e > 0 or ocp > 0 else cbops[0]])
                for ts_ in range(8):
                    mmd = [None, None]
                    for oh in range(2):
                        for j in range(11):
                            mmd[oh] = o("tensor", (lambda jj, ohh, tss, hb, wb: lambda g: g.matmul(
                                pd[ohh][:, :], hE[hb][:, jj * 1024 + tss * 128: jj * 1024 + (tss + 1) * 128],
                                wb[:, (ohh * 11 + jj) * 512:(ohh * 11 + jj + 1) * 512],
                                start=(jj == 0), stop=(jj == 10)))(j, oh, ts_, eb, wdb[eb]),
                                deps=(([wdd, pde_free[oh]] + (he_ops if oh == 0 else [])) if j == 0 else ()))
                    for oh in range(2):
                        oc = 2 * ocp + oh
                        stv = o("vector", (lambda ohh, sg: lambda g: g.tensor_copy(
                            sg[:, :], pd[ohh][:, :]))(oh, stg[oh]), deps=[mmd[oh], exp_add_last[0], stg_free[oh]])
                        pde_free[oh] = stv
                        wr = o("gpsimd", (lambda tss, occ, ss: lambda g: g.dma_start(
                            out=acc_h[tss * 128:(tss + 1) * 128, occ * 512:(occ + 1) * 512],
                            in_=ss[:, :], accum_op=ALU.add))(ts_, oc, stg[oh]),
                            dma=True, slot="acc", deps=[stv])
                        stg_free[oh] = wr
                        dn_last = wr
                wd_free[eb] = dn_last
            exp_add_last[0] = dn_last
            he_free[eb] = dn_last

        if DBG3 and not FUSED:
            o("sync", lambda g: g.dma_start(out=dbg3_h[:, :], in_=x1sp_h[:, :]),
              dma=True, slot="dbg3", deps=x1w_ops)
            o("sync", lambda g: g.dma_start(out=dbg4_h[:, :], in_=acc_h[:, :]),
              dma=True, slot="dbg3", deps=[exp_add_last[0]])
            o("sync", lambda g: g.dma_start(out=dbg5_h[:, 0:144], in_=comb_sb[:, :]),
              dma=True, slot="dbg3", deps=[cbf, exp_add_last[0]])
            o("sync", lambda g: g.dma_start(out=dbg5_h[:, 144:152], in_=sgvsig[:, :]),
              dma=True, slot="dbg3", deps=[exp_add_last[0]])
            o("gpsimd", lambda g: g.dma_start(out=dbg5_h[:, 256:1280], in_=cbsb[1][:, :]),
              dma=True, slot="dbg3b", deps=[exp_add_last[0]])
            o("gpsimd", lambda g: g.dma_start(out=dbg5_h[:, 1280:2304], in_=hE[1][:, 0:1024]),
              dma=True, slot="dbg3b", deps=[exp_add_last[0]])
        es5.close()
        es6 = ExitStack()
        E6 = es6.enter_context
        fxr = [E6(st(f"fxr{i}", [128, 2048], F32)) for i in range(2)]
        fx1 = [E6(st(f"fx1{i}", [128, 2048], F32)) for i in range(2)]
        fsq = E6(st("fsq", [128, 2048], BF16))
        ac_free = [None, None]
        xs_free = [None, None]
        ln2_prev = [None]
        for tt in range(8):
            ar = o("sync", (lambda t_: lambda g: g.dma_start(
                out=fxr[t_ % 2][:, :], in_=acc_h[t_ * 128:(t_ + 1) * 128, :]))(tt),
                dma=True, slot=f"xr{tt % 2}", deps=[exp_add_last[0], ac_free[tt % 2]])
            xr_ = o("sync", (lambda t_: lambda g: g.dma_start(
                out=fx1[t_ % 2][:, :], in_=x1sp_h[t_ * 128:(t_ + 1) * 128, :]))(tt),
                dma=True, slot=f"x1r{tt % 2}", deps=x1w_ops + [xs_free[tt % 2]])
            ad = o("vector", (lambda t_: lambda g: g.tensor_add(
                fx1[t_ % 2][:, :], fx1[t_ % 2][:, :], fxr[t_ % 2][:, :]))(tt), deps=[ar, xr_])
            xt = fx1[tt % 2]
            rs = o("vector", lambda g, xt=xt: g.reduce_sum(comb_sb[:, 133:134], xt[:, :], AX), deps=[ad])
            mn = o("vector", lambda g: g.tensor_scalar_mul(comb_sb[:, 128:129], comb_sb[:, 133:134], 1.0 / 2048), deps=[rs])
            sq = o("scalar", lambda g, xt=xt: g.activation(fsq[:, 0:2048], xt[:, :], AF.Square), deps=[ad, ln2_prev[0]])
            ss = o("vector", lambda g: g.reduce_sum(comb_sb[:, 129:130], fsq[:, 0:2048], AX), deps=[sq])
            ms = o("vector", lambda g: g.tensor_mul(comb_sb[:, 130:131], comb_sb[:, 128:129], comb_sb[:, 128:129]), deps=[mn])
            vr = o("vector", lambda g: g.scalar_tensor_tensor(comb_sb[:, 131:132], comb_sb[:, 129:130], 1.0 / 2048, comb_sb[:, 130:131], ALU.mult, ALU.subtract), deps=[ss, ms])
            rl = o("scalar", lambda g: g.activation(comb_sb[:, 132:133], comb_sb[:, 131:132], AF.Ln, bias=kc("eps5")), deps=[vr])
            re_ = o("scalar", lambda g: g.activation(comb_sb[:, 132:133], comb_sb[:, 132:133], AF.Exp, scale=kc("neghalf")), deps=[rl])
            x1n = o("vector", lambda g, xt=xt: g.tensor_scalar(xt[:, :], xt[:, :], comb_sb[:, 128:129], comb_sb[:, 132:133], ALU.subtract, ALU.mult), deps=[re_, mn, ad])
            ln2_prev[0] = x1n
            wo = o("scalar", (lambda t_, xt=xt: lambda g: g.dma_start(
                out=out_h[t_ * 128:(t_ + 1) * 128, :], in_=xt[:, :]))(tt), dma=True, slot="out", deps=[x1n])
            ac_free[tt % 2] = x1n
            xs_free[tt % 2] = wo

    fo = P.final_ops()
    o("sync", lambda g: None, deps=[x for x in fo if x.eng != "sync"])

    if P3:
        es6.close()
        es2.close()
    elif P1:
        es1.close()
    blk = E(nc.Block())
    P.emit(nc, blk, es)
    es.close()
    return nc



def _prep_core(c, I):
    b, hg = c // 4, c % 4
    hs = slice(hg * HPC, (hg + 1) * HPC)
    x = I["L0_attn_x"]
    qk_w, vg_w = I["L0_attn_qk_w"], I["L0_attn_vg_w"]
    a_w, b_w = I["L0_attn_a_w"], I["L0_attn_b_w"]

    xb = np.ascontiguousarray(x[b].T)
    xT = np.ascontiguousarray(
        xb.reshape(16, 128, NTB, TBS).transpose(2, 1, 0, 3).reshape(NTB, 128, 16 * TBS))

    rows = []
    for h in range(HPC):
        rows.append(qk_w[(hg * HPC + h) * 64:(hg * HPC + h + 1) * 64])
    for h in range(HPC):
        rows.append(qk_w[1024 + (hg * HPC + h) * 64: 1024 + (hg * HPC + h + 1) * 64])
    for h in range(HPC):
        rows.append(vg_w[(hg * HPC + h) * 128:(hg * HPC + h + 1) * 128])
    for h in range(HPC):
        rows.append(vg_w[2048 + (hg * HPC + h) * 128: 2048 + (hg * HPC + h + 1) * 128])
    rows.append(a_w[hs])
    rows.append(b_w[hs])
    wcat = np.concatenate(rows, axis=0)
    wcat = np.concatenate([wcat, np.zeros((8, C), np.float32)], axis=0)
    wproj = np.ascontiguousarray(wcat.T.reshape(16, 128, 1552).transpose(1, 0, 2).reshape(128, 16 * 1552))

    convw = np.zeros((128, 48), np.float32)
    for h in range(HPC):
        convw[0:64, h * 4:(h + 1) * 4] = I["L0_attn_q_conv_w"][(hg * HPC + h) * 64:(hg * HPC + h + 1) * 64, 0, :]
        convw[0:64, 16 + h * 4:16 + (h + 1) * 4] = I["L0_attn_k_conv_w"][(hg * HPC + h) * 64:(hg * HPC + h + 1) * 64, 0, :]
        convw[:, 32 + h * 4:32 + (h + 1) * 4] = I["L0_attn_v_conv_w"][(hg * HPC + h) * 128:(hg * HPC + h + 1) * 128, 0, :]

    cvec = np.zeros((128, 12), np.float32)
    cvec[:, 0:4] = np.broadcast_to(I["L0_attn_dt_bias"][hs], (128, 4))
    cvec[:, 4:8] = np.broadcast_to(I["L0_attn_b_bias"][hs], (128, 4))
    cvec[:, 8:12] = np.broadcast_to(I["L0_attn_beta_bias"][hs], (128, 4))
    alog = np.ascontiguousarray(I["L0_attn_A_log"][hs][None, :])

    meta = np.array([[b * HPC * 128 * 4, (c % 4) * TOK, 0]], np.int32)
    d = dict(xT=xT, wproj=wproj, convw=convw, cvec=cvec, alog=alog, meta=meta)
    if STAGE >= 2:
        tok0 = c * TOK
        d["xrows"] = np.ascontiguousarray(x.reshape(-1, C)[tok0:tok0 + TOK])
    return d


def _shared_weights(I):
    d = {}
    if STAGE >= 2:
        d["owT"] = np.ascontiguousarray(I["L0_attn_o_w"].T.reshape(16, 128, 2048))
        d["gwT"] = np.ascontiguousarray(
            I["L0_ffn_gate_w"].T.reshape(16, 128, 16).transpose(1, 0, 2).reshape(128, 256))
        d["sgvT"] = np.ascontiguousarray(
            I["L0_ffn_shared_gate_vec"].T.reshape(16, 128, 1).transpose(1, 0, 2).reshape(128, 16))
        sel = np.zeros((16, 16 * 128), np.float32)
        for e in range(16):
            sel[e, e * 128:(e + 1) * 128] = 1.0
        d["sel16"] = sel.astype(BFD)
    if STAGE >= 3:
        egu = I["L0_ffn_expert_gate_up"]
        d["wguT"] = np.ascontiguousarray(
            egu.transpose(0, 2, 1).reshape(16, 16, 128, 22, 128).transpose(0, 3, 2, 1, 4).reshape(16, 22, 128, 16 * 128))
        ed = I["L0_ffn_expert_down"]
        d["wdT4"] = np.ascontiguousarray(
            ed.transpose(0, 2, 1).reshape(16, 11, 128, 4, 512).transpose(0, 3, 2, 1, 4).reshape(16, 4, 128, 11 * 512))
        d["sguT"] = np.ascontiguousarray(
            I["L0_ffn_shared_gate_up"].T.reshape(16, 128, 88, 128).transpose(2, 1, 0, 3).reshape(88, 128, 16 * 128))
        d["sdT"] = np.ascontiguousarray(
            I["L0_ffn_shared_down"].T.reshape(44, 128, 4, 512).transpose(2, 1, 0, 3).reshape(4, 128, 44 * 512))
    return d


def _consts():
    jj, ll = np.meshgrid(np.arange(128), np.arange(128), indexing="ij")
    kcf = np.zeros((128, 652), np.float32)
    kcf[:, 0:128] = np.where(ll < jj, 0.0, NEG)
    kcf[:, 128:256] = np.where(jj <= ll, 0.0, NEG)
    kcf[:, 256:384] = np.eye(128, dtype=np.float32)
    kcf[:, 384:512] = (jj <= ll).astype(np.float32)
    kcf[:, 512:640] = 1.0
    vals = {"zero": 0.0, "eps6": 1e-6, "one": 1.0, "eps5": 1e-5, "epsf": EPSF,
            "neghalf": -0.5, "half": 0.5, "two": 2.0, "lnscale": LNS,
            "inv128": 1.0 / 128, "negone": -1.0, "inv2048": 1.0 / 2048}
    order = ["zero", "eps6", "one", "eps5", "epsf", "neghalf", "half", "two",
             "lnscale", "inv128", "negone", "inv2048"]
    for i, k in enumerate(order):
        kcf[:, 640 + i] = vals[k]
    kcb = np.zeros((128, 130), BFD)
    kcb[:, 0:128] = np.eye(128, dtype=np.float32).astype(BFD)
    kcb[:, 128] = np.ones(128, BFD)
    kcb[:, 129] = np.ones(128, BFD)
    return kcf, kcb


def kernel(**inputs):
    kcf, kcb = _consts()
    shared = _shared_weights(inputs)
    if MODE == "fused":
        nc = build_program("fused")
        in_maps = []
        for c in range(8):
            m = _prep_core(c, inputs)
            m["kcf"] = kcf
            m["kcb"] = kcb
            m.update(shared)
            in_maps.append(m)
        res = run_bass_kernel_spmd(nc, in_maps, core_ids=list(range(8)))
        outs = [np.asarray(r["out"]) for r in res.results]
        return np.concatenate(outs, axis=0).reshape(B, T, C).astype(np.float32)

    nc1 = build_program("p1")
    in1 = []
    for c in range(8):
        m = _prep_core(c, inputs)
        m.pop("meta", None)
        m.pop("xrows", None)
        m["kcf"] = kcf
        m["kcb"] = kcb
        in1.append(m)
    res1 = run_bass_kernel_spmd(nc1, in1, core_ids=list(range(8)))
    slabs = [np.asarray(r["slab"]) for r in res1.results]
    ogf = np.zeros((2048, 2 * T), BFD)
    for c in range(8):
        b, hg = c // 4, c % 4
        ogf[hg * 512:(hg + 1) * 512, b * T:(b + 1) * T] = slabs[c]
    nc3 = build_program("p3")
    in3 = []
    x = inputs["L0_attn_x"].astype(np.float32, copy=False)
    for c in range(8):
        oga = ogf[:, c * TOK:(c + 1) * TOK]
        ogin = np.ascontiguousarray(oga.reshape(16, 128, TOK).transpose(1, 0, 2).reshape(128, 16 * TOK))
        m = dict(kcf=kcf, kcb=kcb, ogin=ogin,
                 xrows=np.ascontiguousarray(x.reshape(-1, C)[c * TOK:(c + 1) * TOK]))
        m.update(shared)
        in3.append(m)
    res3 = run_bass_kernel_spmd(nc3, in3, core_ids=list(range(8)))
    outs = [np.asarray(r["out"]) for r in res3.results]
    if DBG3:
        kernel.dbg3 = [np.asarray(r["dbg3"]) for r in res3.results]
        kernel.dbg4 = [np.asarray(r["dbg4"]) for r in res3.results]
        kernel.dbg5 = [np.asarray(r["dbg5"]) for r in res3.results]
        kernel.slabs = slabs
    return np.concatenate(outs, axis=0).reshape(B, T, C).astype(np.float32)


# version 06
# speedup vs baseline: 1.1424x; 1.1424x over previous
import numpy as np
import ml_dtypes
from contextlib import ExitStack

import concourse.bass as bass
import concourse.mybir as mybir
from concourse.bass_utils import run_bass_kernel_spmd

F32 = mybir.dt.float32
BF16 = mybir.dt.bfloat16
I32 = mybir.dt.int32
AF = mybir.ActivationFunctionType
ALU = mybir.AluOpType
AX = mybir.AxisListType.X
BFD = ml_dtypes.bfloat16

B, T, C = 2, 4096, 2048
H, DKH, DVH = 16, 64, 128
HPC = 4
L = 128
NCH = T // L
TBS = 512
NTB = T // TBS
CPB = TBS // L
TOK = 1024
NE, EI, SI = 16, 1408, 5632
SCALE = DKH ** -0.5
NEG = -1e30
LNS = float(np.log(SCALE))
EPSF = float(np.finfo(np.float32).eps)

STAGE = 3
PROBE = False
MODE = "fused"
DBG3 = False
ENGS = ("tensor", "vector", "scalar", "gpsimd", "sync")


class Op:
    __slots__ = ("eng", "fn", "deps", "dma", "slot", "need_inc", "sem", "val", "inc")

    def __init__(self, eng, fn, deps, dma, slot, inc):
        self.eng, self.fn, self.deps, self.dma = eng, fn, deps, dma
        self.slot = slot
        self.need_inc = dma
        self.sem = None
        self.val = None
        self.inc = inc


class Prog:
    def __init__(self):
        self.ops = []

    def op(self, eng, fn, deps=(), dma=False, slot=None, inc=16):
        if dma:
            assert slot is not None, "DMA ops need a slot"
        o = Op(eng, fn, [d for d in deps if d is not None], dma, slot, inc)
        self.ops.append(o)
        return o

    def emit(self, nc, blk, es):
        for o in self.ops:
            for d in o.deps:
                d.need_inc = True
        csem = {e: es.enter_context(nc.semaphore(f"c_{e}")) for e in ENGS}
        slots = {}
        for o in self.ops:
            if o.dma and o.slot not in slots:
                slots[o.slot] = [es.enter_context(nc.semaphore(f"s_{o.slot}")), 0]
        cctr = {e: 0 for e in ENGS}
        for o in self.ops:
            if o.dma:
                sl = slots[o.slot]
                sl[1] += o.inc
                o.sem, o.val = sl[0], sl[1]
            elif o.need_inc:
                cctr[o.eng] += 1
                o.sem, o.val = csem[o.eng], cctr[o.eng]

        def body(eng_name):
            def f(eng):
                satisfied = {}
                for o in self.ops:
                    if o.eng != eng_name:
                        continue
                    bysem = {}
                    for d in o.deps:
                        if satisfied.get(id(d.sem), -1) >= d.val:
                            continue
                        prev = bysem.get(id(d.sem))
                        if prev is None or d.val > prev[1]:
                            bysem[id(d.sem)] = (d.sem, d.val)
                    for smh, v in bysem.values():
                        eng.wait_ge(smh, v)
                        satisfied[id(smh)] = max(satisfied.get(id(smh), -1), v)
                    inst = o.fn(eng)
                    if o.dma:
                        inst.then_inc(o.sem, o.inc)
                    elif o.need_inc:
                        if inst is None:
                            inst = eng.nop()
                        inst.then_inc(o.sem, 1)
            return f
        for name in ENGS:
            getattr(blk, name)(body(name))

    def final_ops(self):
        last = {}
        for o in self.ops:
            last[o.eng] = o
        return list(last.values())


def build_program(part="fused"):
    nc = bass.Bass("TRN2", target_bir_lowering=False)
    P1 = part in ("fused", "p1")
    P3 = part in ("fused", "p3")
    P = Prog()
    o = P.op

    dt = nc.dram_tensor
    FUSED = part == "fused"
    if P1:
        xT_h = dt("xT", [NTB, 128, 16 * TBS], F32, kind="ExternalInput")
        wproj_h = dt("wproj", [128, 16 * 1552], F32, kind="ExternalInput")
        convw_h = dt("convw", [128, 48], F32, kind="ExternalInput")
        cvec_h = dt("cvec", [128, 12], F32, kind="ExternalInput")
        alog_h = dt("alog", [1, HPC], F32, kind="ExternalInput")
    kcf_h = dt("kcf", [128, 652], F32, kind="ExternalInput")
    kcb_h = dt("kcb", [128, 130], BF16, kind="ExternalInput")
    if FUSED:
        meta_h = dt("meta", [1, 3], I32, kind="ExternalInput")
    if P3:
        xrows_h = dt("xrows", [TOK, C], F32, kind="ExternalInput")
        owT_h = dt("owT", [16, 128, 2048], F32, kind="ExternalInput")
        gwT_h = dt("gwT", [128, 16 * 16], F32, kind="ExternalInput")
        sgvT_h = dt("sgvT", [128, 16], F32, kind="ExternalInput")
        sel16_h = dt("sel16", [16, 16 * 128], BF16, kind="ExternalInput")
        wguT_h = dt("wguT", [NE, 22, 128, 16 * 128], F32, kind="ExternalInput")
        wdT4_h = dt("wdT4", [NE, 4, 128, 11 * 512], F32, kind="ExternalInput")
        sguT_h = dt("sguT", [88, 128, 16 * 128], F32, kind="ExternalInput")
        sdT_h = dt("sdT", [4, 128, 44 * 512], F32, kind="ExternalInput")
        if not FUSED:
            ogin_h = dt("ogin", [128, 16 * TOK], BF16, kind="ExternalInput")

    dbg_h = None
    if P3:
        out_h = dt("out", [TOK, C], F32, kind="ExternalOutput")
        if DBG3 and not FUSED:
            dbg3_h = dt("dbg3", [TOK, C], F32, kind="ExternalOutput")
            dbg4_h = dt("dbg4", [TOK, C], F32, kind="ExternalOutput")
            dbg5_h = dt("dbg5", [128, 2304], F32, kind="ExternalOutput")
        x1sp_h = dt("x1_spill", [TOK, C], F32, kind="Internal")
        acc_h = dt("acc_dram", [TOK, C], F32, kind="Internal")
    if P1 and not P3:
        slab_h = dt("slab", [HPC * 128, T], BF16, kind="ExternalOutput")
    if FUSED:
        myslab_h = dt("myslab", [HPC * 128, T], BF16, kind="Internal")
        ccagg_h = dt("ccagg", [8 * HPC * 128, T], BF16, kind="Internal")
        ogst_h = dt("ogst", [2048, TOK], BF16, kind="Internal")

    es = ExitStack()
    E = es.enter_context
    es1 = ExitStack()
    E1 = es1.enter_context
    st = nc.sbuf_tensor

    kcf_sb = E(st("kcf_sb", [128, 652], F32))
    kcb_sb = E(st("kcb_sb", [128, 130], BF16))
    if FUSED:
        meta_sb = E(st("meta_sb", [1, 3], I32))
    pt = nc.psum_tensor
    pp = [E(pt(f"pp{i}", [128, TBS], F32)) for i in range(2)]
    if P1:
        wproj_sb = E1(st("wproj_sb", [128, 16 * 1552], BF16))
        xsb = [E1(st(f"xsb{i}", [128, 16 * TBS], BF16)) for i in range(2)]
        convw_sb = E1(st("convw_sb", [128, 48], F32))
        cvec_sb = E1(st("cvec_sb", [128, 12], F32))
        alog_sb = E1(st("alog_sb", [1, HPC], F32))
        eavec = E1(st("eavec", [1, 2 * HPC], F32))
        negea_bc = E1(st("negea_bc", [128, HPC], F32))

        qrw = [E1(st(f"qrw{h}", [64, TBS + 3], BF16)) for h in range(HPC)]
        krw = [E1(st(f"krw{h}", [64, TBS + 3], BF16)) for h in range(HPC)]
        vrw = [E1(st(f"vrw{h}", [128, TBS + 3], BF16)) for h in range(HPC)]
        qpo = [E1(st(f"qpo{h}", [64, 2 * TBS], BF16)) for h in range(HPC)]
        kpo = [E1(st(f"kpo{h}", [64, 2 * TBS], BF16)) for h in range(HPC)]
        vpo = [E1(st(f"vpo{h}", [128, 2 * TBS], BF16)) for h in range(HPC)]
        gpo = [E1(st(f"gpo{h}", [128, 2 * TBS], BF16)) for h in range(HPC)]
        qsq = [E1(st(f"qsq{h}", [64, 2 * TBS], BF16)) for h in range(HPC)]
        ksq = [E1(st(f"ksq{h}", [64, 2 * TBS], BF16)) for h in range(HPC)]
        asb = E1(st("asb", [4, 2 * TBS], F32))
        bsb = E1(st("bsb", [4, 2 * TBS], F32))
        ctmp = [E1(st(f"ctmp{i}", [128, TBS], F32)) for i in range(3)]
        ctmv = [E1(st(f"ctmv{i}", [128, TBS], F32)) for i in range(2)]

        NL = 2
        dA = [E1(st(f"dA{l}", [128, 128], F32)) for l in range(NL)]
        dM = [E1(st(f"dM{l}", [128, 128], F32)) for l in range(NL)]
        uf = [E1(st(f"uf{l}", [128, 128], F32)) for l in range(NL)]
        tf = [E1(st(f"tf{l}", [128, 128], F32)) for l in range(NL)]
        off_ = [E1(st(f"off{l}", [128, 128], F32)) for l in range(NL)]
        osc = [E1(st(f"osc{l}", [128, 128], F32)) for l in range(NL)]
        bA = [E1(st(f"bA{l}", [128, 128], BF16)) for l in range(NL)]
        bTA0 = [E1(st(f"bTA0{l}", [128, 128], BF16)) for l in range(NL)]
        bTAs = [[E1(st(f"bTAs{l}_{i}", [128, 128], BF16)) for i in range(6)] for l in range(NL)]
        bPs = [[E1(st(f"bPs{l}_{i}", [128, 128], BF16)) for i in range(2)] for l in range(NL)]
        bMT = [E1(st(f"bMT{l}", [128, 128], BF16)) for l in range(NL)]
        bu = [E1(st(f"bu{l}", [128, 128], BF16)) for l in range(NL)]
        bV = [E1(st(f"bV{l}", [128, 128], BF16)) for l in range(NL)]
        bK = [E1(st(f"bK{l}", [128, 64], BF16)) for l in range(NL)]
        bS16 = [E1(st(f"bS16{l}", [64, DVH], BF16)) for l in range(NL)]
        bon = [E1(st(f"bon{l}", [128, 128], BF16)) for l in range(NL)]
        gsilb = [E1(st(f"gsilb{h}", [128, 2 * TBS], BF16)) for h in range(HPC)]
        rowsf = [E1(st(f"rowsf{l}", [1, 256], F32)) for l in range(NL)]
        Ssb = [E1(st(f"Ssb{h}", [64, DVH], F32)) for h in range(HPC)]
        S2 = [E1(st(f"S2_{h}", [64, DVH], F32)) for h in range(HPC)]
        colh = [E1(st(f"colh{h}", [128, 12], F32)) for h in range(HPC)]
        cols = E1(st("cols", [128, 24], F32))
        gam4 = E1(st("gam4", [128, 4], F32))
        Gam4 = E1(st("Gam4", [4, 1], F32))
        Gbc = E1(st("Gbc", [128, HPC], F32))
        eGbc = E1(st("eGbc", [64, HPC], F32))
        grow = E1(st("grow", [1, 4], F32))
        oslab = E1(st("oslab", [128, HPC * TBS], BF16))

        pf = [[E1(pt(f"pf{l}_{i}", [128, 128], F32)) for i in range(2)] for l in range(NL)]
        pbt = [E1(pt(f"pbt{l}", [128, 128], BF16)) for l in range(NL)]

    maskA = kcf_sb[:, 0:128]
    maskMT = kcf_sb[:, 128:256]
    idf32 = kcf_sb[:, 256:384]
    tuones = kcf_sb[:, 384:512]
    onesf = kcf_sb[:, 512:640]
    ones_row_f = kcf_sb[0:1, 512:640]
    ones_col_f = kcf_sb[:, 512:513]
    KC = {"zero": 640, "eps6": 641, "one": 642, "eps5": 643, "epsf": 644,
          "neghalf": 645, "half": 646, "two": 647, "lnscale": 648,
          "inv128": 649, "negone": 650, "inv2048": 651}
    def kc(name, p=128):
        return kcf_sb[0:p, KC[name]:KC[name] + 1]
    id_bf = kcb_sb[:, 0:128]
    ones64_bf = kcb_sb[0:64, 128:129]

    if P1:
        ld_wproj = o("gpsimd", lambda g: g.dma_start(out=wproj_sb[:, :], in_=wproj_h[:, :]), dma=True, slot="wproj")
    ld_misc = []
    pairs = [(kcf_sb, kcf_h), (kcb_sb, kcb_h)]
    if P1:
        pairs = [(convw_sb, convw_h), (cvec_sb, cvec_h), (alog_sb, alog_h)] + pairs
    if FUSED:
        pairs = pairs + [(meta_sb, meta_h)]
    for dst, src in pairs:
        ld_misc.append(o("sync", (lambda d, s_: lambda g: g.dma_start(out=d[:, :], in_=s_[:, :]))(dst, src), dma=True, slot="misc"))
    mA = mCV = mAL = mKF = mKB = mMT = ld_misc[-1]
    MISC = ld_misc

    if P1:
        t_cv = o("vector", lambda g: g.tensor_add(cvec_sb[:, 4:8], cvec_sb[:, 4:8], cvec_sb[:, 8:12]), deps=MISC)
        t_ea = o("scalar", lambda g: g.activation(eavec[0:1, 0:HPC], alog_sb[0:1, :], AF.Exp), deps=MISC)
        t_ea2 = o("vector", lambda g: g.tensor_scalar_mul(eavec[0:1, HPC:2 * HPC], eavec[0:1, 0:HPC], -1.0), deps=[t_ea])
        t_eab = o("tensor", lambda g: g.matmul(pf[0][0][:, 0:HPC], ones_row_f, eavec[0:1, HPC:2 * HPC], start=True, stop=True), deps=[t_ea2, mKF])
        t_eac = o("vector", lambda g: g.tensor_copy(negea_bc[:, :], pf[0][0][:, 0:HPC]), deps=[t_eab])

        init_misc = [t_cv, t_eac, mA, mKB, mMT]
        sinit = []
        for h in range(HPC):
            sinit.append(o("vector", (lambda hh: lambda g: g.memset(Ssb[hh][:, :], 0.0))(h)))
            sinit.append(o("vector", (lambda hh: lambda g: g.memset(S2[hh][:, :], 0.0))(h)))
        halo0 = []
        for h in range(HPC):
            for rw, pw in ((qrw[h], 64), (krw[h], 64), (vrw[h], 128)):
                halo0.append(o("vector", (lambda r, p_: lambda g: g.memset(r[0:p_, 0:3], 0.0))(rw, pw)))

        ld_x = [None] * NTB
        ld_x[0] = o("gpsimd", lambda g: g.dma_start(out=xsb[0][:, :], in_=xT_h[0, :, :]), dma=True, slot="x0", deps=[])

    ftiles = []
    offc = 0
    for h in range(HPC):
        ftiles.append(("q", h, 64, offc)); offc += 64
    for h in range(HPC):
        ftiles.append(("k", h, 64, offc)); offc += 64
    for h in range(HPC):
        ftiles.append(("v", h, 128, offc)); offc += 128
    for h in range(HPC):
        ftiles.append(("g", h, 128, offc)); offc += 128
    ftiles.append(("a", 0, 4, offc)); offc += 4
    ftiles.append(("b", 0, 4, offc)); offc += 4

    bank_evac = [None, None]
    conv_fin = [None] * NTB
    delta_fin = [None] * NTB
    slab_fin = [[None] * HPC for _ in range(NTB)]
    lane_last = [None, None]
    chunk_shared_last = [None]
    state_cur, state_nxt = (Ssb, S2) if P1 else (None, None)
    state_op = [None] * HPC
    ab_evac = [None] * NTB

    evac_toggle = [0]
    def evac_eng():
        evac_toggle[0] ^= 1
        return "vector" if evac_toggle[0] else "scalar"

    def ev_copy(dst, src, deps):
        e = evac_eng()
        if e == "vector":
            return o("vector", lambda g: g.tensor_copy(dst, src), deps=deps)
        return o("scalar", lambda g: g.activation(dst, src, AF.Copy), deps=deps)

    regs = {}
    t_regs = None
    def setup_regs():
        def fn(g):
            r0 = g.alloc_register("rrow2")
            r1 = g.alloc_register("rcol2")
            g.reg_load(r0, meta_sb[0:1, 0:1])
            g.reg_load(r1, meta_sb[0:1, 1:2])
            regs["row2"] = g.snap(r0, min_val=0, max_val=HPC * 128 * 4)
            regs["col2"] = g.snap(r1, min_val=0, max_val=3 * TOK)
            return None
        return fn
    if FUSED:
        t_regs = o("sync", setup_regs(), deps=[mMT])

    conv_buf_last = {}
    gsops = {}
    def projconv_gen(tb):
        if tb + 1 < NTB:
            ld_x[tb + 1] = o("gpsimd", (lambda t: lambda g: g.dma_start(
                out=xsb[t % 2][:, :], in_=xT_h[t, :, :]))(tb + 1), dma=True,
                slot=f"x{(tb + 1) % 2}",
                deps=[ld_x[tb]] + ([bank_evac[0], bank_evac[1]] if tb >= 1 else []))
            yield
        deps = [ld_x[tb], ld_wproj]
        if tb >= 2 and delta_fin[tb - 2] is not None:
            deps += delta_fin[tb - 2]
        col0 = (tb % 2) * TBS
        evacs = {}
        for fi, (kind, h, fw, fo) in enumerate(ftiles):
            bank = pp[fi % 2]
            mmdeps = list(deps)
            if bank_evac[fi % 2] is not None:
                mmdeps.append(bank_evac[fi % 2])
            if kind in ("q", "k", "v") and conv_fin[tb - 1] is not None:
                mmdeps += conv_fin[tb - 1]
            for k in range(16):
                mm = o("tensor", (lambda kk, ffo, ffw, bk, ttb: lambda g: g.matmul(
                    bk[0:ffw, :], wproj_sb[:, kk * 1552 + ffo: kk * 1552 + ffo + ffw],
                    xsb[ttb % 2][:, kk * TBS:(kk + 1) * TBS],
                    start=(kk == 0), stop=(kk == 15)))(k, fo, fw, bank, tb),
                    deps=(mmdeps if k == 0 else ()))
            if kind == "q":
                dst = qrw[h][:, 3:3 + TBS]
            elif kind == "k":
                dst = krw[h][:, 3:3 + TBS]
            elif kind == "v":
                dst = vrw[h][:, 3:3 + TBS]
            elif kind == "g":
                dst = gpo[h][:, col0:col0 + TBS]
            elif kind == "a":
                dst = asb[:, col0:col0 + TBS]
            else:
                dst = bsb[:, col0:col0 + TBS]
            evdeps = [mm] + halo0
            if tb >= 2 and delta_fin[tb - 2] is not None:
                evdeps += delta_fin[tb - 2]
            ev = o("vector", (lambda d, bk, ffw: lambda g: g.tensor_copy(d, bk[0:ffw, :]))(dst, bank, fw), deps=evdeps)
            bank_evac[fi % 2] = ev
            evacs[(kind, h)] = ev
            yield
        ab_evac[tb] = [evacs[("a", 0)], evacs[("b", 0)]]
        finals = []
        qk_i = 0
        v_i = 0
        for kind, raws, posts, sqs, cwo in (("q", qrw, qpo, qsq, 0),
                                            ("k", krw, kpo, ksq, 16),
                                            ("v", vrw, vpo, None, 32)):
            for h in range(HPC):
                raw, pw = raws[h], (64 if kind != "v" else 128)
                eng = "vector"
                if kind == "v":
                    buf = ctmv[v_i % 2]
                    bkey = ("v", v_i % 2)
                    v_i += 1
                else:
                    buf = ctmp[qk_i % 3]
                    bkey = ("qk", qk_i % 3)
                    qk_i += 1
                dep = [evacs[(kind, h)], conv_buf_last.get(bkey)]
                cc = o(eng, (lambda rw, p_, co, bf: lambda g: g.tensor_scalar_mul(
                    bf[0:p_, :], rw[0:p_, 3:3 + TBS], convw_sb[0:p_, co + 3:co + 4]))(raw, pw, cwo + 4 * h, buf), deps=dep)
                yield
                for i in (2, 1, 0):
                    cc = o(eng, (lambda rw, p_, co, ii, bf: lambda g: g.scalar_tensor_tensor(
                        bf[0:p_, :], rw[0:p_, ii:ii + TBS], convw_sb[0:p_, co + ii:co + ii + 1],
                        bf[0:p_, :], ALU.mult, ALU.add))(raw, pw, cwo + 4 * h, i, buf), deps=[cc])
                    yield
                tcp = o(eng, (lambda rw, p_: lambda g: g.tensor_copy(
                    rw[0:p_, 0:3], rw[0:p_, TBS:TBS + 3]))(raw, pw), deps=[cc])
                sil = o("scalar", (lambda po, p_, bf: lambda g: g.activation(
                    po[0:p_, col0:col0 + TBS], bf[0:p_, :], AF.Silu))(posts[h], pw, buf), deps=[cc])
                conv_buf_last[bkey] = sil
                finals.append(tcp)
                finals.append(sil)
                yield
                if sqs is not None:
                    sq = o("scalar", (lambda s_, po, p_: lambda g: g.activation(
                        s_[0:p_, col0:col0 + TBS], po[0:p_, col0:col0 + TBS], AF.Square))(sqs[h], posts[h], pw), deps=[sil])
                    finals.append(sq)
                    yield
        for h in range(HPC):
            gdep = [evacs[("g", h)]]
            if tb >= 2 and delta_fin[tb - 2] is not None:
                gdep += delta_fin[tb - 2]
            gs = o("scalar", (lambda hh: lambda g: g.activation(
                gsilb[hh][:, col0:col0 + TBS], gpo[hh][:, col0:col0 + TBS], AF.Silu))(h), deps=gdep)
            gsops[(tb, h)] = gs
            finals.append(gs)
            yield
        conv_fin[tb] = finals

    def delta_chunk(tb, cisub, convdep):
        ci = tb * CPB + cisub
        col0 = (tb % 2) * TBS
        cc0 = col0 + cisub * L
        banks = [{"rot": 0, "free": [None, None]} for _ in range(NL)]

        def takebank(l):
            b = banks[l]
            i = b["rot"]
            b["rot"] ^= 1
            dep = [b["free"][i]] if b["free"][i] else []
            return pf[l][i], dep, i

        def setreader(l, i, op_):
            banks[l]["free"][i] = op_

        shdep = ([x for x in lane_last if x] + [chunk_shared_last[0]] +
                 ab_evac[tb] + [t_cv, t_eac, mKF, mKB] + convdep)
        bk0, d0, i0 = takebank(0)
        trA = o("tensor", (lambda bk: lambda g: g.transpose(bk[:, 0:4], asb[:, cc0:cc0 + L], idf32[0:4, 0:4]))(bk0), deps=shdep + d0)
        bk1, d1, i1 = takebank(1)
        trB = o("tensor", (lambda bk: lambda g: g.transpose(bk[:, 0:4], bsb[:, cc0:cc0 + L], idf32[0:4, 0:4]))(bk1), deps=shdep + d1)
        za = o("vector", (lambda bk: lambda g: g.tensor_add(cols[:, 12:16], bk[:, 0:4], cvec_sb[:, 0:4]))(bk0), deps=[trA])
        setreader(0, i0, za)
        zb = o("vector", (lambda bk: lambda g: g.tensor_add(cols[:, 16:20], bk[:, 0:4], cvec_sb[:, 4:8]))(bk1), deps=[trB])
        setreader(1, i1, zb)
        bsig = o("scalar", lambda g: g.activation(cols[:, 0:4], cols[:, 16:20], AF.Sigmoid), deps=[zb])
        bet = o("vector", lambda g: g.tensor_scalar_mul(cols[:, 0:4], cols[:, 0:4], 2.0), deps=[bsig])
        sab = o("scalar", lambda g: g.activation(cols[:, 20:24], cols[:, 12:16], AF.Abs), deps=[za])
        sex = o("scalar", lambda g: g.activation(cols[:, 20:24], cols[:, 20:24], AF.Exp, scale=kc("negone")), deps=[sab])
        sln = o("scalar", lambda g: g.activation(cols[:, 20:24], cols[:, 20:24], AF.Ln, bias=kc("one")), deps=[sex])
        srl = o("scalar", lambda g: g.activation(cols[:, 12:16], cols[:, 12:16], AF.Relu), deps=[sln])
        ssp = o("vector", lambda g: g.tensor_add(cols[:, 12:16], cols[:, 12:16], cols[:, 20:24]), deps=[srl])
        gg = o("vector", lambda g: g.tensor_mul(cols[:, 4:8], cols[:, 12:16], negea_bc[:, :]), deps=[ssp, t_eac])
        bk2, d2, i2 = takebank(0)
        gmm = o("tensor", (lambda bk: lambda g: g.matmul(bk[:, 0:4], tuones, cols[:, 4:8], start=True, stop=True))(bk2), deps=d2 + [gg])
        gev = o("vector", (lambda bk: lambda g: g.tensor_copy(gam4[:, :], bk[:, 0:4]))(bk2), deps=[gmm])
        setreader(0, i2, gev)
        bk3, d3, i3 = takebank(1)
        Gmm = o("tensor", (lambda bk: lambda g: g.matmul(bk[0:4, 0:1], cols[:, 4:8], ones_col_f, start=True, stop=True))(bk3), deps=d3 + [gg])
        Gev = o("vector", (lambda bk: lambda g: g.tensor_copy(Gam4[:, :], bk[0:4, 0:1]))(bk3), deps=[Gmm])
        setreader(1, i3, Gev)
        bk4, d4, i4 = takebank(0)
        trG = o("tensor", (lambda bk: lambda g: g.transpose(bk[0:1, 0:4], Gam4[:, :], idf32[0:4, 0:4]))(bk4), deps=d4 + [Gev])
        gre = o("vector", (lambda bk: lambda g: g.tensor_copy(grow[:, :], bk[0:1, 0:4]))(bk4), deps=[trG])
        setreader(0, i4, gre)
        bk5, d5, i5 = takebank(1)
        Gbm = o("tensor", (lambda bk: lambda g: g.matmul(bk[:, 0:4], ones_row_f, grow[:, :], start=True, stop=True))(bk5), deps=d5 + [gre])
        Gbe = o("vector", (lambda bk: lambda g: g.tensor_copy(Gbc[:, :], bk[:, 0:4]))(bk5), deps=[Gbm])
        setreader(1, i5, Gbe)
        eGe = o("scalar", lambda g: g.activation(eGbc[:, :], Gbc[0:64, :], AF.Exp), deps=[Gbe])
        chunk_shared_last[0] = eGe
        if PROBE and STAGE == 1 and tb == 0 and cisub == 0:
            pass

        lane_new_last = [None, None]
        probes = []

        def dump(src, dcol, width, dep, rows=128, cast=False):
            if not (PROBE and STAGE == 1 and tb == 0 and cisub == 0):
                return
            eng = "gpsimd" if cast else "sync"
            probes.append(o(eng, (lambda s_, dc, w_, r_: lambda g: g.dma_start(
                out=dbg2_h[0:r_, dc:dc + w_], in_=s_))(src, dcol, width, rows),
                dma=True, slot="probe", deps=[dep, t_regs]))

        def head_body(h, lane):
            Scur = state_cur[h]
            Snxt = state_nxt[h]
            ldep = [lane_new_last[lane] if lane_new_last[lane] is not None else lane_last[lane]]
            ldep = [x for x in ldep if x is not None]
            ch = colh[h]
            gcol = gam4[:, h:h + 1]
            bcol = cols[:, h:h + 1]
            Gcol = Gbc[:, h:h + 1]
            kchunk = kpo[h][:, cc0:cc0 + L]
            qchunk = qpo[h][:, cc0:cc0 + L]
            vchunk = vpo[h][:, cc0:cc0 + L]

            bkS, dS, iS = takebank(lane)
            m1 = o("tensor", (lambda bk: lambda g: g.matmul(bk[:, 0:1], qsq[h][:, cc0:cc0 + L], ones64_bf, start=True, stop=True))(bkS), deps=dS + ldep + [gev, bet, eGe])
            m2 = o("tensor", (lambda bk: lambda g: g.matmul(bk[:, 1:2], ksq[h][:, cc0:cc0 + L], ones64_bf, start=True, stop=True))(bkS), deps=[m1])
            yield
            l1 = o("scalar", (lambda bk: lambda g: g.activation(ch[:, 0:2], bk[:, 0:2], AF.Ln, bias=kc("eps6")))(bkS), deps=[m2])
            setreader(lane, iS, l1)
            yield
            c_uA = o("vector", lambda g: g.scalar_tensor_tensor(ch[:, 2:3], ch[:, 1:2], -0.5, gcol, ALU.mult, ALU.add), deps=[l1, gev] + ldep)
            c_w = o("vector", lambda g: g.scalar_tensor_tensor(ch[:, 3:4], ch[:, 1:2], 0.5, gcol, ALU.mult, ALU.add), deps=[l1])
            yield
            c_nw = o("vector", lambda g: g.tensor_scalar_mul(ch[:, 4:5], ch[:, 3:4], -1.0), deps=[c_w])
            c_a0 = o("vector", lambda g: g.scalar_tensor_tensor(ch[:, 5:6], ch[:, 0:1], -0.5, gcol, ALU.mult, ALU.add), deps=[l1])
            yield
            c_a = o("vector", lambda g: g.tensor_scalar_add(ch[:, 5:6], ch[:, 5:6], LNS), deps=[c_a0])
            c_pk = o("scalar", lambda g: g.activation(ch[:, 6:7], ch[:, 2:3], AF.Exp), deps=[c_uA])
            yield
            c_qs = o("scalar", lambda g: g.activation(ch[:, 7:8], ch[:, 5:6], AF.Exp), deps=[c_a])
            c_r0 = o("vector", lambda g: g.tensor_sub(ch[:, 8:9], Gcol, ch[:, 3:4]), deps=[c_w, Gbe])
            yield
            c_rr = o("scalar", lambda g: g.activation(ch[:, 8:9], ch[:, 8:9], AF.Exp), deps=[c_r0])
            yield

            bkW, dW, iW = takebank(lane)
            trW = o("tensor", (lambda bk: lambda g: g.transpose(bk[0:1, :], ch[:, 3:4], idf32))(bkW), deps=dW + [c_w])
            evW = o("vector", (lambda bk: lambda g: g.tensor_copy(rowsf[lane][0:1, 0:128], bk[0:1, :]))(bkW), deps=[trW])
            setreader(lane, iW, evW)
            yield
            bkA_, dA_, iA_ = takebank(lane)
            trA2 = o("tensor", (lambda bk: lambda g: g.transpose(bk[0:1, :], ch[:, 5:6], idf32))(bkA_), deps=dA_ + [c_a])
            evA2 = o("vector", (lambda bk: lambda g: g.tensor_copy(rowsf[lane][0:1, 128:256], bk[0:1, :]))(bkA_), deps=[trA2])
            setreader(lane, iA_, evA2)
            yield

            bkG, dG, iG = takebank(lane)
            G2A = o("tensor", (lambda bk: lambda g: g.matmul(bk[:, :], ones_row_f, rowsf[lane][0:1, 0:128], start=True, stop=True))(bkG), deps=dG + [evW])
            Dm = o("vector", (lambda bk: lambda g: g.scalar_tensor_tensor(
                dA[lane][:, :], maskA, ch[:, 2:3], bk[:, :], ALU.add, ALU.subtract))(bkG), deps=[c_uA, mKF, G2A] + ldep)
            setreader(lane, iG, Dm)
            yield
            Aex = o("scalar", lambda g: g.activation(dA[lane][:, :], dA[lane][:, :], AF.Exp), deps=[Dm])
            yield
            bkK, dK, iK = takebank(lane)
            KKt = o("tensor", (lambda bk: lambda g: g.matmul(bk[:, :], kchunk, kchunk, start=True, stop=True))(bkK), deps=dK + [Aex])
            Af = o("vector", (lambda bk: lambda g: g.tensor_mul(osc[lane][:, :], dA[lane][:, :], bk[:, :]))(bkK), deps=[KKt, Aex])
            setreader(lane, iK, Af)
            yield
            Amat = o("vector", lambda g: g.tensor_scalar_mul(bA[lane][:, :], osc[lane][:, :], bcol), deps=[Af, bet] + ldep)
            yield

            bkG2, dG2, iG2 = takebank(lane)
            G2M = o("tensor", (lambda bk: lambda g: g.matmul(bk[:, :], ones_row_f, rowsf[lane][0:1, 128:256], start=True, stop=True))(bkG2), deps=dG2 + [evA2])
            DmM = o("vector", (lambda bk: lambda g: g.scalar_tensor_tensor(
                dM[lane][:, :], maskMT, ch[:, 4:5], bk[:, :], ALU.add, ALU.add))(bkG2), deps=[c_nw, mKF, G2M] + ldep)
            setreader(lane, iG2, DmM)
            yield
            MTe = o("scalar", lambda g: g.activation(dM[lane][:, :], dM[lane][:, :], AF.Exp), deps=[DmM])
            yield
            bkQ, dQ, iQ = takebank(lane)
            KQt = o("tensor", (lambda bk: lambda g: g.matmul(bk[:, :], kchunk, qchunk, start=True, stop=True))(bkQ), deps=dQ + [MTe])
            MTm = o("vector", (lambda bk: lambda g: g.tensor_mul(bMT[lane][:, :], dM[lane][:, :], bk[:, :]))(bkQ), deps=[KQt, MTe] + ldep)
            setreader(lane, iQ, MTm)
            yield

            Scp = o("gpsimd", lambda g: g.tensor_copy(bS16[lane][:, :], Scur[:, :]), deps=[state_op[h]] + ldep)
            yield
            bkU, dU, iU = takebank(lane)
            KS0 = o("tensor", (lambda bk: lambda g: g.matmul(bk[:, :], kchunk, bS16[lane][:, :], start=True, stop=True))(bkU), deps=dU + [Scp, MTm])
            Vtr = o("tensor", lambda g: g.transpose(pbt[lane][:, :], vchunk, id_bf), deps=[mKB, m1] + ldep)
            Vev = o("vector", lambda g: g.tensor_copy(bV[lane][:, :], pbt[lane][:, :]), deps=[Vtr] + ldep)
            yield
            u1 = o("vector", (lambda bk: lambda g: g.scalar_tensor_tensor(uf[lane][:, :], bk[:, :], ch[:, 6:7], bV[lane][:, :], ALU.mult, ALU.subtract))(bkU), deps=[KS0, Vev, c_pk] + ldep)
            setreader(lane, iU, u1)
            yield
            u2 = o("vector", lambda g: g.tensor_scalar(uf[lane][:, :], uf[lane][:, :], bcol, -1.0, ALU.mult, ALU.mult), deps=[u1, bet])
            ub = o("gpsimd", lambda g: g.tensor_copy(bu[lane][:, :], uf[lane][:, :]), deps=[u2] + ldep)
            yield

            TA0t = o("tensor", lambda g: g.transpose(pbt[lane][:, :], bA[lane][:, :], id_bf), deps=[Amat, Vev])
            TA0e = o("vector", lambda g: g.tensor_copy(bTA0[lane][:, :], pbt[lane][:, :]), deps=[TA0t] + ldep)
            yield
            Pcur, TAcur = bA[lane], bTA0[lane]
            Pop, TAop = Amat, TA0e
            tev = None
            for lev in range(1, 7):
                bkp, dp, ip_ = takebank(lane)
                tmm = o("tensor", (lambda bk, Pc, TAc: lambda g: g.matmul(bk[:, :], Pc[:, :], TAc[:, :], start=True, stop=True))(bkp, Pcur, TAcur), deps=dp + [TAop, Pop])
                tev = ev_copy(bTAs[lane][lev - 1][:, :], bkp[:, :], [tmm] + ldep)
                setreader(lane, ip_, tev)
                yield
                if lev < 6:
                    bkq, dq, iq_ = takebank(lane)
                    pmm = o("tensor", (lambda bk, Pc, TAc: lambda g: g.matmul(bk[:, :], TAc[:, :], Pc[:, :], start=True, stop=True))(bkq, Pcur, TAcur), deps=dq + [TAop, Pop])
                    pev = ev_copy(bPs[lane][lev % 2][:, :], bkq[:, :], [pmm] + ldep)
                    setreader(lane, iq_, pev)
                    Pcur, Pop = bPs[lane][lev % 2], pev
                    yield
                TAcur, TAop = bTAs[lane][lev - 1], tev
            tprev_f, top_f, top_b = uf[lane], u2, ub
            for lev in range(6, 0, -1):
                bkp, dp, ip_ = takebank(lane)
                amm = o("tensor", (lambda bk, TAl: lambda g: g.matmul(bk[:, :], TAl[:, :], bu[lane][:, :], start=True, stop=True))(bkp, bTAs[lane][lev - 1]), deps=dp + [top_b, tev])
                att = o("vector", (lambda bk, tp: lambda g: g.tensor_add(tf[lane][:, :], tp[:, :], bk[:, :]))(bkp, tprev_f), deps=[amm, top_f])
                setreader(lane, ip_, att)
                yield
                acp = o("gpsimd", lambda g: g.tensor_copy(bu[lane][:, :], tf[lane][:, :]), deps=[att])
                tprev_f, top_f, top_b = tf[lane], att, acp
                yield
            bkp2, dp2, ip2 = takebank(lane)
            fmm = o("tensor", (lambda bk: lambda g: g.matmul(bk[:, :], bTA0[lane][:, :], bu[lane][:, :], start=True, stop=True))(bkp2), deps=dp2 + [top_b])
            fde = o("vector", (lambda bk: lambda g: g.tensor_sub(tf[lane][:, :], tf[lane][:, :], bk[:, :]))(bkp2), deps=[fmm, top_f])
            setreader(lane, ip2, fde)
            yield
            dcp = o("gpsimd", lambda g: g.tensor_copy(bu[lane][:, :], tf[lane][:, :]), deps=[fde])
            yield

            bkO, dO, iO = takebank(lane)
            QS0 = o("tensor", (lambda bk: lambda g: g.matmul(bk[:, :], qchunk, bS16[lane][:, :], start=True, stop=True))(bkO), deps=dO + [Scp, dcp])
            o1 = o("vector", (lambda bk: lambda g: g.tensor_scalar_mul(osc[lane][:, :], bk[:, :], ch[:, 7:8]))(bkO), deps=[QS0, c_qs, Af])
            setreader(lane, iO, o1)
            yield
            bkO2, dO2, iO2 = takebank(lane)
            Mdl = o("tensor", (lambda bk: lambda g: g.matmul(bk[:, :], bMT[lane][:, :], bu[lane][:, :], start=True, stop=True))(bkO2), deps=dO2 + [dcp])
            oo = o("vector", (lambda bk: lambda g: g.tensor_add(off_[lane][:, :], osc[lane][:, :], bk[:, :]))(bkO2), deps=[Mdl, o1] + ldep)
            setreader(lane, iO2, oo)
            yield

            dr = o("vector", lambda g: g.tensor_scalar_mul(bu[lane][:, :], tf[lane][:, :], ch[:, 8:9]), deps=[Mdl, c_rr, fde])
            Ktr = o("tensor", lambda g: g.transpose(pbt[lane][0:128, 0:64], kchunk, id_bf[0:64, 0:64]), deps=[TA0e, Vev])
            Kev = o("vector", lambda g: g.tensor_copy(bK[lane][:, :], pbt[lane][0:128, 0:64]), deps=[Ktr] + ldep)
            yield
            bkSU, dSU, iSU = takebank(lane)
            SMM = o("tensor", (lambda bk: lambda g: g.matmul(bk[0:64, :], bK[lane][:, :], bu[lane][:, :], start=True, stop=True))(bkSU), deps=dSU + [dr, Kev])
            Sup = o("vector", (lambda bk: lambda g: g.scalar_tensor_tensor(
                Snxt[:, :], Scur[:, :], eGbc[0:64, h:h + 1], bk[0:64, :], ALU.mult, ALU.add))(bkSU), deps=[SMM, state_op[h], eGe])
            setreader(lane, iSU, Sup)
            state_op[h] = Sup
            yield

            sqo = o("scalar", lambda g: g.activation(osc[lane][:, :], off_[lane][:, :], AF.Square), deps=[oo, o1])
            rsum = o("vector", lambda g: g.reduce_sum(ch[:, 9:10], osc[lane][:, :], AX), deps=[sqo])
            yield
            rl = o("scalar", lambda g: g.activation(ch[:, 10:11], ch[:, 9:10], AF.Ln, scale=kc("inv128"), bias=kc("epsf")), deps=[rsum])
            re = o("scalar", lambda g: g.activation(ch[:, 10:11], ch[:, 10:11], AF.Exp, scale=kc("neghalf")), deps=[rl])
            yield
            on = o("vector", lambda g: g.tensor_scalar_mul(bon[lane][:, :], off_[lane][:, :], ch[:, 10:11]), deps=[re, oo])
            ontr = o("tensor", lambda g: g.transpose(pbt[lane][:, :], bon[lane][:, :], id_bf), deps=[on, Kev])
            yield
            og = o("vector", (lambda hh, cis: lambda g: g.tensor_mul(
                oslab[:, hh * TBS + cis * L: hh * TBS + (cis + 1) * L], pbt[lane][:, :],
                gsilb[hh][:, cc0:cc0 + L]))(h, cisub),
                deps=[ontr, Sup, gsops[(tb, h)]] + probes + ([slab_fin[tb - 1][h]] if tb > 0 else []))
            lane_new_last[lane] = og
            yield

        def lane_run(lane, heads):
            for h in heads:
                yield from head_body(h, lane)

        return lane_new_last, [lane_run(0, (0, 2)), lane_run(1, (1, 3))]

    def delta_drive(tb, gp):
        nonlocal state_cur, state_nxt
        dfin = []
        rr = 0
        for cisub in range(CPB):
            lane_new_last, gens = delta_chunk(tb, cisub, conv_fin[tb])
            alive = [True, True]
            while alive[0] or alive[1]:
                for i in (0, 1):
                    if alive[i]:
                        try:
                            next(gens[i])
                        except StopIteration:
                            alive[i] = False
                rr += 1
                if gp is not None and rr % 3 == 0:
                    try:
                        next(gp)
                    except StopIteration:
                        gp = None
            for lpos in range(NL):
                if lane_new_last[lpos] is not None:
                    lane_last[lpos] = lane_new_last[lpos]
            dfin = [x for x in lane_new_last if x]
            state_cur, state_nxt = state_nxt, state_cur
        delta_fin[tb] = dfin
        for h in range(HPC):
            hd = [lane_last[h % 2]]
            tgt = myslab_h if FUSED else slab_h
            sd = o("sync", (lambda hh, ttb, tg: lambda g: g.dma_start(
                out=tg[hh * 128:(hh + 1) * 128, ttb * TBS:(ttb + 1) * TBS],
                in_=oslab[:, hh * TBS:(hh + 1) * TBS]))(h, tb, tgt),
                dma=True, slot=f"slab{h}", deps=hd + [t_regs])
            slab_fin[tb][h] = sd
        return gp

    if P1:
        for _ in projconv_gen(0):
            pass
        for tb in range(1, NTB):
            gp = projconv_gen(tb)
            gp = delta_drive(tb - 1, gp)
            if gp is not None:
                for _ in gp:
                    pass
        delta_drive(NTB - 1, None)


    cc_op = None
    if FUSED:
        allslab = [slab_fin[tb][h] for tb in range(NTB) for h in range(HPC)]
        cc_op = o("gpsimd", lambda g: g.collective_compute(
            "AllGather", ALU.bypass, replica_groups=[list(range(8))],
            ins=[myslab_h.ap()],
            outs=[ccagg_h.ap()]),
            dma=True, slot="cc", inc=1, deps=allslab + [t_regs])

    if P3:
        if P1:
            es1.close()
        es2 = ExitStack()
        E2 = es2.enter_context
        x1fm = E2(st("x1fm", [128, 16 * TOK], BF16))
        comb_sb = E2(st("comb_sb", [128, 144], F32))
        combfm = E2(st("combfm", [16, TOK], F32))
        combbf = E2(st("combbf", [16, TOK], BF16))
        sgvsig = E2(st("sgvsig", [128, 8], F32))
        gwT_sb = E2(st("gwT_sb", [128, 16 * 16], BF16))
        sgvT_sb = E2(st("sgvT_sb", [128, 16], BF16))
        pq = [E2(pt(f"pq{i}", [128, 512], F32)) for i in range(4)]

        ld3 = []
        ld3.append(o("gpsimd", lambda g: g.dma_start(out=gwT_sb[:, :], in_=gwT_h[:, :]), dma=True, slot="m3", deps=[cc_op]))
        ld3.append(o("gpsimd", lambda g: g.dma_start(out=sgvT_sb[:, :], in_=sgvT_h[:, :]), dma=True, slot="m3", deps=[cc_op]))
        m3 = ld3[-1]
        dfin_gate = delta_fin[NTB - 1][0] if P1 else None

        es3 = ExitStack()
        E3 = es3.enter_context
        pr = E3(pt("pr", [128, 128], F32))
        pxb = E3(pt("pxb", [128, 128], BF16))
        xrt = [E3(st(f"xrt{i}", [128, 2048], F32)) for i in range(2)]
        x1t = [E3(st(f"x1t{i}", [128, 2048], F32)) for i in range(2)]
        sscr = E3(st("sscr", [128, 2048], BF16))
        rscr = E3(st("rscr", [128, 64], F32))
        xbf = E3(st("xbf", [128, 2048], BF16))
        ogth = E3(st("ogth", [128, 16 * TOK], BF16))
        owT_sb = E3(st("owT_sb", [128, 16 * 2048], BF16))

        og_ops = []
        if FUSED:
            og_dyn = []
            for q in range(4):
                og_dyn.append(o("sync", (lambda qq: lambda g: g.dma_start(
                    out=ogst_h[qq * 512:(qq + 1) * 512, :],
                    in_=ccagg_h.ap()[bass.ds(regs["row2"] + qq * 512, 512),
                                     bass.ds(regs["col2"], TOK)]))(q),
                    dma=True, slot="ogd", deps=[cc_op, t_regs]))
            for k in range(16):
                og_ops.append(o("sync", (lambda kk: lambda g: g.dma_start(
                    out=ogth[:, kk * TOK:(kk + 1) * TOK],
                    in_=ogst_h[kk * 128:(kk + 1) * 128, :]))(k),
                    dma=True, slot="og", deps=og_dyn))
        else:
            og_ops.append(o("sync", lambda g: g.dma_start(out=ogth[:, :], in_=ogin_h[:, :]),
                            dma=True, slot="og", deps=[]))
        ow_ops = []
        for q in range(4):
            ow_ops.append(o("gpsimd", (lambda qq: lambda g: g.dma_start(
                out=owT_sb[:, qq * 4 * 2048:(qq + 1) * 4 * 2048],
                in_=owT_h.ap()[qq * 4:(qq + 1) * 4, :, :].rearrange("k p d -> p k d")))(q),
                dma=True, slot="ow", deps=[cc_op]))

        LNV = {}
        x1w_ops = []
        ln_done = []
        rt_last = [None]
        xr_ops = [None, None]
        xrt_free = [None, None]
        x1t_free = [None, None]
        pq_free = [None, None, None, None]
        pxb_free = [None]
        pr_free = [None]
        for tt in range(8):
            xr_ops[tt % 2] = o("sync", (lambda t_: lambda g: g.dma_start(
                out=xrt[t_ % 2][:, :], in_=xrows_h[t_ * 128:(t_ + 1) * 128, :]))(tt),
                dma=True, slot=f"xr{tt % 2}", deps=[cc_op, xrt_free[tt % 2]])
            attev = []
            for oc in range(4):
                bk = pq[oc % 2]
                mmd = og_ops + ow_ops + [pq_free[oc % 2], dfin_gate]
                mm = None
                for k in range(16):
                    mm = o("tensor", (lambda kk, occ, bkk, t_: lambda g: g.matmul(
                        bkk[:, :], ogth[:, kk * TOK + t_ * 128: kk * TOK + (t_ + 1) * 128],
                        owT_sb[:, kk * 2048 + occ * 512: kk * 2048 + (occ + 1) * 512],
                        start=(kk == 0), stop=(kk == 15)))(k, oc, bk, tt),
                        deps=(mmd if k == 0 else ()))
                ev = o("vector", (lambda occ, bkk, t_: lambda g: g.tensor_add(
                    x1t[t_ % 2][:, occ * 512:(occ + 1) * 512], xrt[t_ % 2][:, occ * 512:(occ + 1) * 512],
                    bkk[:, :]))(oc, bk, tt), deps=[mm, xr_ops[tt % 2], x1t_free[tt % 2]])
                pq_free[oc % 2] = ev
                xrt_free[tt % 2] = ev
                attev.append(ev)
            xt = x1t[tt % 2]
            rs = o("vector", lambda g, xt=xt: g.reduce_sum(comb_sb[:, 133:134], xt[:, :], AX), deps=attev)
            mn = o("vector", lambda g: g.tensor_scalar_mul(comb_sb[:, 128:129], comb_sb[:, 133:134], 1.0 / 2048), deps=[rs])
            sq = o("scalar", lambda g, xt=xt: g.activation(sscr[:, :], xt[:, :], AF.Square), deps=attev + [ln_done[-1] if ln_done else None, rt_last[0]])
            ss = o("vector", lambda g: g.reduce_sum(comb_sb[:, 129:130], sscr[:, :], AX), deps=[sq])
            ms = o("vector", lambda g: g.tensor_mul(comb_sb[:, 130:131], comb_sb[:, 128:129], comb_sb[:, 128:129]), deps=[mn])
            vr = o("vector", lambda g: g.scalar_tensor_tensor(comb_sb[:, 131:132], comb_sb[:, 129:130], 1.0 / 2048, comb_sb[:, 130:131], ALU.mult, ALU.subtract), deps=[ss, ms])
            rl = o("scalar", lambda g: g.activation(comb_sb[:, 132:133], comb_sb[:, 131:132], AF.Ln, bias=kc("eps5")), deps=[vr])
            re_ = o("scalar", lambda g: g.activation(comb_sb[:, 132:133], comb_sb[:, 132:133], AF.Exp, scale=kc("neghalf")), deps=[rl])
            x1n = o("vector", lambda g, xt=xt: g.tensor_scalar(xt[:, :], xt[:, :], comb_sb[:, 128:129], comb_sb[:, 132:133], ALU.subtract, ALU.mult), deps=[re_, mn] + attev)
            ln_done.append(x1n)
            x1w = o("sync", (lambda t_, xt=xt: lambda g: g.dma_start(
                out=x1sp_h[t_ * 128:(t_ + 1) * 128, :], in_=xt[:, :]))(tt), dma=True, slot="x1w", deps=[x1n])
            x1w_ops.append(x1w)
            cb = o("scalar", lambda g, xt=xt: g.activation(xbf[:, :], xt[:, :], AF.Copy), deps=[x1n, rt_last[0], x1w])
            x1t_free[tt % 2] = cb
            for k in range(16):
                trx = o("tensor", (lambda kk: lambda g: g.transpose(
                    pxb[:, :], xbf[:, kk * 128:(kk + 1) * 128], id_bf))(k), deps=[cb, pxb_free[0], mKB])
                evx = o("vector", (lambda kk, t_: lambda g: g.tensor_copy(
                    x1fm[:, kk * TOK + t_ * 128: kk * TOK + (t_ + 1) * 128], pxb[:, :]))(k, tt), deps=[trx])
                pxb_free[0] = evx
            xfm_done = evx
            rmm = None
            for k in range(16):
                rmm = o("tensor", (lambda kk, t_: lambda g: g.matmul(
                    pr[:, 0:16], x1fm[:, kk * TOK + t_ * 128: kk * TOK + (t_ + 1) * 128],
                    gwT_sb[:, kk * 16:(kk + 1) * 16], start=(kk == 0), stop=(kk == 15)))(k, tt),
                    deps=([xfm_done, m3, pr_free[0]] if k == 0 else ()))
            smm = None
            for k in range(16):
                smm = o("tensor", (lambda kk, t_: lambda g: g.matmul(
                    pr[:, 16:17], x1fm[:, kk * TOK + t_ * 128: kk * TOK + (t_ + 1) * 128],
                    sgvT_sb[:, kk:kk + 1], start=(kk == 0), stop=(kk == 15)))(k, tt),
                    deps=([rmm] if k == 0 else ()))
            sgv_o = o("scalar", (lambda t_: lambda g: g.activation(sgvsig[:, t_:t_ + 1], pr[:, 16:17], AF.Sigmoid))(tt), deps=[smm])
            lcp = o("vector", lambda g: g.tensor_copy(rscr[:, 0:16], pr[:, 0:16]), deps=[rmm, sq])
            pr_free[0] = lcp
            lmx = o("vector", lambda g: g.reduce_max(rscr[:, 16:17], rscr[:, 0:16], AX), deps=[lcp])
            nmx = o("vector", lambda g: g.tensor_scalar_mul(rscr[:, 16:17], rscr[:, 16:17], -1.0), deps=[lmx])
            pex = o("scalar", lambda g: g.activation(rscr[:, 0:16], rscr[:, 0:16], AF.Exp, bias=rscr[:, 16:17]), deps=[nmx])
            wcp = o("vector", lambda g: g.tensor_copy(rscr[:, 32:48], rscr[:, 0:16]), deps=[pex])
            last = wcp
            for rnd in range(3):
                mx = o("vector", lambda g: g.reduce_max(rscr[:, 17:18], rscr[:, 32:48], AX), deps=[last])
                mk = o("vector", lambda g: g.tensor_scalar(rscr[:, 48:64], rscr[:, 32:48], rscr[:, 17:18], -1e30, ALU.is_ge, ALU.mult), deps=[mx])
                ad = o("vector", lambda g: g.tensor_add(rscr[:, 32:48], rscr[:, 32:48], rscr[:, 48:64]), deps=[mk])
                last = ad
            m4 = o("vector", lambda g: g.reduce_max(rscr[:, 17:18], rscr[:, 32:48], AX), deps=[last])
            kp = o("vector", lambda g: g.tensor_single_scalar(rscr[:, 32:48], rscr[:, 0:16], rscr[:, 17:18], ALU.is_ge), deps=[m4])
            cw = o("vector", lambda g: g.tensor_mul(rscr[:, 0:16], rscr[:, 0:16], rscr[:, 32:48]), deps=[kp])
            sm = o("vector", lambda g: g.reduce_sum(rscr[:, 16:17], rscr[:, 0:16], AX), deps=[cw])
            rc = o("vector", lambda g: g.reciprocal(rscr[:, 16:17], rscr[:, 16:17]), deps=[sm])
            cmb = o("vector", (lambda t_: lambda g: g.tensor_scalar_mul(
                comb_sb[:, t_ * 16:(t_ + 1) * 16], rscr[:, 0:16], rscr[:, 16:17]))(tt), deps=[rc])
            trc = o("tensor", (lambda t_: lambda g: g.transpose(
                pr[0:16, 0:128], comb_sb[:, t_ * 16:(t_ + 1) * 16], idf32))(tt), deps=[cmb, pr_free[0]])
            evc = o("vector", (lambda t_: lambda g: g.tensor_copy(
                combfm[:, t_ * 128:(t_ + 1) * 128], pr[0:16, 0:128]))(tt), deps=[trc])
            pr_free[0] = evc
            rt_last[0] = evc
        cbf = o("vector", lambda g: g.tensor_copy(combbf[:, :], combfm[:, :]), deps=[rt_last[0]])
    if P3:
        es3.close()
        sgbuf = [E2(st(f"sgbuf{i}", [128, 16 * 128], BF16)) for i in range(4)]
        silb = [E2(st(f"silb{i}", [128, 512], BF16)) for i in range(2)]
        stg = [E2(st(f"stg{i}", [128, 512], F32)) for i in range(2)]
        es4 = ExitStack()
        E4 = es4.enter_context
        hS = E4(st("hS", [128, 44 * 512], BF16))
        sdoc = [E4(st("sdoc0", [128, 44 * 512], BF16))] * 2

        ph3a_done = [cbf] + x1w_ops + ln_done
        acc_last = [None]

        sg_free = [None, None, None, None]
        sd_free = [None, None]
        hs_free = [None]
        psg_free = [None, None]
        pdn_free = [None, None]
        silb_free_sh = [None, None]
        stg_free_sh = [None, None]
        for th in range(2):
            hs_ops = []
            for j in range(44):
                bi = j % 2
                dg = o("gpsimd", (lambda jj, bb, t_: lambda g: g.dma_start(
                    out=sgbuf[bb][:, :], in_=sguT_h[jj, :, :]))(j, bi, th),
                    dma=True, slot=f"sg{bi}", deps=[sg_free[bi], ph3a_done[0], cc_op])
                du = o("gpsimd", (lambda jj, bb, t_: lambda g: g.dma_start(
                    out=sgbuf[2 + bb][:, :], in_=sguT_h[44 + jj, :, :]))(j, bi, th),
                    dma=True, slot=f"sg{2 + bi}", deps=[sg_free[2 + bi], ph3a_done[0], cc_op])
                mmg = None
                for k in range(16):
                    mmg = o("tensor", (lambda kk, bb, t_: lambda g: g.matmul(
                        pq[0][:, :], sgbuf[bb][:, kk * 128:(kk + 1) * 128],
                        x1fm[:, kk * TOK + t_ * 512: kk * TOK + (t_ + 1) * 512],
                        start=(kk == 0), stop=(kk == 15)))(k, bi, th),
                        deps=([dg, psg_free[0], cbf] if k == 0 else ()))
                mmu = None
                for k in range(16):
                    mmu = o("tensor", (lambda kk, bb, t_: lambda g: g.matmul(
                        pq[1][:, :], sgbuf[2 + bb][:, kk * 128:(kk + 1) * 128],
                        x1fm[:, kk * TOK + t_ * 512: kk * TOK + (t_ + 1) * 512],
                        start=(kk == 0), stop=(kk == 15)))(k, bi, th),
                        deps=([du, psg_free[1]] if k == 0 else ()))
                sil = o("scalar", (lambda bb: lambda g: g.activation(
                    silb[bb][:, :], pq[0][:, :], AF.Silu))(bi), deps=[mmg, hs_free[0], silb_free_sh[bi]])
                psg_free[0] = sil
                hsm = o("vector", (lambda jj, bb: lambda g: g.tensor_mul(
                    hS[:, jj * 512:(jj + 1) * 512], silb[bb][:, :], pq[1][:, :]))(j, bi), deps=[sil, mmu])
                psg_free[1] = hsm
                silb_free_sh[bi] = hsm
                sg_free[bi] = mmg
                sg_free[2 + bi] = mmu
                hs_ops.append(hsm)
            dn_ops = []
            for oc in range(4):
                sb_i = 0
                sld = o("gpsimd", (lambda occ, sb: lambda g: g.dma_start(
                    out=sdoc[sb][:, :], in_=sdT_h[occ, :, :]))(oc, sb_i),
                    dma=True, slot=f"sd{sb_i}", deps=[sd_free[sb_i], hs_ops[0] if oc < 2 else None])
                for ts_ in range(4):
                    bk = pq[2 + ts_ % 2]
                    mmd = None
                    for j in range(44):
                        mmd = o("tensor", (lambda jj, occ, tss, bkk, t_, sb: lambda g: g.matmul(
                            bkk[:, :], hS[:, jj * 512 + tss * 128: jj * 512 + (tss + 1) * 128],
                            sdoc[sb][:, jj * 512:(jj + 1) * 512],
                            start=(jj == 0), stop=(jj == 43)))(j, oc, ts_, bk, th, sb_i),
                            deps=(([sld, pdn_free[ts_ % 2]] + hs_ops) if j == 0 else ()))
                    stv = o("vector", (lambda tss, bkk, t_: lambda g: g.tensor_scalar_mul(
                        stg[tss % 2][:, :], bkk[:, :], sgvsig[:, t_ * 4 + tss: t_ * 4 + tss + 1]))(ts_, bk, th),
                        deps=[mmd, sgv_o, acc_last[0], stg_free_sh[ts_ % 2]])
                    pdn_free[ts_ % 2] = stv
                    wr = o("gpsimd", (lambda tss, occ, t_, ss: lambda g: g.dma_start(
                        out=acc_h[(t_ * 4 + tss) * 128:(t_ * 4 + tss + 1) * 128, occ * 512:(occ + 1) * 512],
                        in_=ss[:, :]))(ts_, oc, th, stg[ts_ % 2]),
                        dma=True, slot="acc", deps=[stv])
                    stg_free_sh[ts_ % 2] = wr
                    dn_ops.append(wr)
                sd_free[sb_i] = dn_ops[-1] if dn_ops else None
            hs_free[0] = dn_ops[-1]
            acc_last[0] = dn_ops[-1]
        shared_done = acc_last[0]
        es4.close()
        es5 = ExitStack()
        E5 = es5.enter_context
        sel16_sb = E5(st("sel16_sb", [16, 16 * 128], BF16))
        sel_ld = o("sync", lambda g: g.dma_start(out=sel16_sb[:, :], in_=sel16_h[:, :]), dma=True, slot="m3", deps=[cc_op, shared_done])
        cbsb = [E5(st(f"cbsb{i}", [128, 1024], BF16)) for i in range(2)]
        hE = [E5(st(f"hE{i}", [128, 11 * 1024], BF16)) for i in range(2)]
        wdb = [E5(st(f"wdb{i}", [128, 22 * 512], BF16)) for i in range(2)]
        pd = [E5(pt(f"pd{i}", [128, 512], F32)) for i in range(2)]

        wd_free = [None, None]
        he_free = [None, None]
        pge_free = list(psg_free) + list(pdn_free)
        pde_free = [None, None]
        silb_free = [None, None]
        stg_free = [None, None]
        exp_add_last = [shared_done]
        for e in range(NE):
            eb = e % 2
            cbops = []
            for tb2 in range(2):
                cbm = o("tensor", (lambda ee, tbb: lambda g: g.matmul(
                    pq[0][:, :], sel16_sb[:, ee * 128:(ee + 1) * 128],
                    combbf[:, tbb * 512:(tbb + 1) * 512], start=True, stop=True))(e, tb2),
                    deps=[cbf, pge_free[0], pde_free[0], sel_ld, exp_add_last[0] if e == 0 else None])
                cbe = o("vector", (lambda tbb, bb: lambda g: g.tensor_copy(
                    cbsb[bb][:, tbb * 512:(tbb + 1) * 512], pq[0][:, :]))(tb2, eb), deps=[cbm, he_free[eb]])
                pge_free[0] = cbe
                pde_free[0] = cbe
                cbops.append(cbe)
            he_ops = []
            for j in range(11):
                bi = j % 2
                dg = o("gpsimd", (lambda ee, jj, bb: lambda g: g.dma_start(
                    out=sgbuf[bb][:, :], in_=wguT_h[ee, jj, :, :]))(e, j, bi),
                    dma=True, slot=f"sg{bi}", deps=[sg_free[bi]])
                du = o("gpsimd", (lambda ee, jj, bb: lambda g: g.dma_start(
                    out=sgbuf[2 + bb][:, :], in_=wguT_h[ee, 11 + jj, :, :]))(e, j, bi),
                    dma=True, slot=f"sg{2 + bi}", deps=[sg_free[2 + bi]])
                mmg = [None, None]
                for tb2 in range(2):
                    for k in range(16):
                        mmg[tb2] = o("tensor", (lambda kk, bb, tbb: lambda g: g.matmul(
                            pq[0 + tbb][:, :], sgbuf[bb][:, kk * 128:(kk + 1) * 128],
                            x1fm[:, kk * TOK + tbb * 512: kk * TOK + (tbb + 1) * 512],
                            start=(kk == 0), stop=(kk == 15)))(k, bi, tb2),
                            deps=([dg, pge_free[tb2]] if k == 0 else ()))
                mmu = [None, None]
                for tb2 in range(2):
                    for k in range(16):
                        mmu[tb2] = o("tensor", (lambda kk, bb, tbb: lambda g: g.matmul(
                            pq[2 + tbb][:, :], sgbuf[2 + bb][:, kk * 128:(kk + 1) * 128],
                            x1fm[:, kk * TOK + tbb * 512: kk * TOK + (tbb + 1) * 512],
                            start=(kk == 0), stop=(kk == 15)))(k, bi, tb2),
                            deps=([du, pge_free[2 + tb2]] if k == 0 else ()))
                for tb2 in range(2):
                    sil = o("scalar", (lambda bb, tbb: lambda g: g.activation(
                        silb[bb][:, :], pq[0 + tbb][:, :], AF.Silu))(tb2 % 2, tb2), deps=[mmg[tb2], he_free[eb], silb_free[tb2 % 2]])
                    pge_free[tb2] = sil
                    hsm = o("vector", (lambda jj, bb, tbb, hb: lambda g: g.scalar_tensor_tensor(
                        hE[hb][:, jj * 1024 + tbb * 512: jj * 1024 + (tbb + 1) * 512],
                        pq[2 + tbb][:, :], 1.0, silb[bb][:, :], ALU.mult, ALU.mult))(j, tb2 % 2, tb2, eb), deps=[sil, mmu[tb2], cbops[-1]])
                    hcm = o("vector", (lambda jj, tbb, hb, bb: lambda g: g.tensor_mul(
                        hE[hb][:, jj * 1024 + tbb * 512: jj * 1024 + (tbb + 1) * 512],
                        hE[hb][:, jj * 1024 + tbb * 512: jj * 1024 + (tbb + 1) * 512],
                        cbsb[bb][:, tbb * 512:(tbb + 1) * 512]))(j, tb2, eb, eb), deps=[hsm] + cbops)
                    pge_free[2 + tb2] = hcm
                    silb_free[tb2 % 2] = hsm
                    he_ops.append(hcm)
                sg_free[bi] = mmg[1]
                sg_free[2 + bi] = mmu[1]
            dn_last = None
            for ocp in range(2):
                wdd = o("gpsimd", (lambda ee, op_, bb: lambda g: g.dma_start(
                    out=wdb[bb][:, :],
                    in_=wdT4_h.ap()[ee, 2 * op_:2 * op_ + 2, :, :].rearrange("a p t -> p a t")))(e, ocp, eb),
                    dma=True, slot=f"wd{eb}", deps=[wd_free[eb], he_ops[0] if e > 0 or ocp > 0 else cbops[0]])
                for ts_ in range(8):
                    mmd = [None, None]
                    for oh in range(2):
                        for j in range(11):
                            mmd[oh] = o("tensor", (lambda jj, ohh, tss, hb, wb: lambda g: g.matmul(
                                pd[ohh][:, :], hE[hb][:, jj * 1024 + tss * 128: jj * 1024 + (tss + 1) * 128],
                                wb[:, (ohh * 11 + jj) * 512:(ohh * 11 + jj + 1) * 512],
                                start=(jj == 0), stop=(jj == 10)))(j, oh, ts_, eb, wdb[eb]),
                                deps=(([wdd, pde_free[oh]] + (he_ops if oh == 0 else [])) if j == 0 else ()))
                    for oh in range(2):
                        oc = 2 * ocp + oh
                        stv = o("vector", (lambda ohh, sg: lambda g: g.tensor_copy(
                            sg[:, :], pd[ohh][:, :]))(oh, stg[oh]), deps=[mmd[oh], exp_add_last[0], stg_free[oh]])
                        pde_free[oh] = stv
                        wr = o("gpsimd", (lambda tss, occ, ss: lambda g: g.dma_start(
                            out=acc_h[tss * 128:(tss + 1) * 128, occ * 512:(occ + 1) * 512],
                            in_=ss[:, :], accum_op=ALU.add))(ts_, oc, stg[oh]),
                            dma=True, slot="acc", deps=[stv])
                        stg_free[oh] = wr
                        dn_last = wr
                wd_free[eb] = dn_last
            exp_add_last[0] = dn_last
            he_free[eb] = dn_last

        if DBG3 and not FUSED:
            o("sync", lambda g: g.dma_start(out=dbg3_h[:, :], in_=x1sp_h[:, :]),
              dma=True, slot="dbg3", deps=x1w_ops)
            o("sync", lambda g: g.dma_start(out=dbg4_h[:, :], in_=acc_h[:, :]),
              dma=True, slot="dbg3", deps=[exp_add_last[0]])
            o("sync", lambda g: g.dma_start(out=dbg5_h[:, 0:144], in_=comb_sb[:, :]),
              dma=True, slot="dbg3", deps=[cbf, exp_add_last[0]])
            o("sync", lambda g: g.dma_start(out=dbg5_h[:, 144:152], in_=sgvsig[:, :]),
              dma=True, slot="dbg3", deps=[exp_add_last[0]])
            o("gpsimd", lambda g: g.dma_start(out=dbg5_h[:, 256:1280], in_=cbsb[1][:, :]),
              dma=True, slot="dbg3b", deps=[exp_add_last[0]])
            o("gpsimd", lambda g: g.dma_start(out=dbg5_h[:, 1280:2304], in_=hE[1][:, 0:1024]),
              dma=True, slot="dbg3b", deps=[exp_add_last[0]])
        es5.close()
        es6 = ExitStack()
        E6 = es6.enter_context
        fxr = [E6(st(f"fxr{i}", [128, 2048], F32)) for i in range(2)]
        fx1 = [E6(st(f"fx1{i}", [128, 2048], F32)) for i in range(2)]
        fsq = E6(st("fsq", [128, 2048], BF16))
        ac_free = [None, None]
        xs_free = [None, None]
        ln2_prev = [None]
        for tt in range(8):
            ar = o("sync", (lambda t_: lambda g: g.dma_start(
                out=fxr[t_ % 2][:, :], in_=acc_h[t_ * 128:(t_ + 1) * 128, :]))(tt),
                dma=True, slot=f"xr{tt % 2}", deps=[exp_add_last[0], ac_free[tt % 2]])
            xr_ = o("sync", (lambda t_: lambda g: g.dma_start(
                out=fx1[t_ % 2][:, :], in_=x1sp_h[t_ * 128:(t_ + 1) * 128, :]))(tt),
                dma=True, slot=f"x1r{tt % 2}", deps=x1w_ops + [xs_free[tt % 2]])
            ad = o("vector", (lambda t_: lambda g: g.tensor_add(
                fx1[t_ % 2][:, :], fx1[t_ % 2][:, :], fxr[t_ % 2][:, :]))(tt), deps=[ar, xr_])
            xt = fx1[tt % 2]
            rs = o("vector", lambda g, xt=xt: g.reduce_sum(comb_sb[:, 133:134], xt[:, :], AX), deps=[ad])
            mn = o("vector", lambda g: g.tensor_scalar_mul(comb_sb[:, 128:129], comb_sb[:, 133:134], 1.0 / 2048), deps=[rs])
            sq = o("scalar", lambda g, xt=xt: g.activation(fsq[:, 0:2048], xt[:, :], AF.Square), deps=[ad, ln2_prev[0]])
            ss = o("vector", lambda g: g.reduce_sum(comb_sb[:, 129:130], fsq[:, 0:2048], AX), deps=[sq])
            ms = o("vector", lambda g: g.tensor_mul(comb_sb[:, 130:131], comb_sb[:, 128:129], comb_sb[:, 128:129]), deps=[mn])
            vr = o("vector", lambda g: g.scalar_tensor_tensor(comb_sb[:, 131:132], comb_sb[:, 129:130], 1.0 / 2048, comb_sb[:, 130:131], ALU.mult, ALU.subtract), deps=[ss, ms])
            rl = o("scalar", lambda g: g.activation(comb_sb[:, 132:133], comb_sb[:, 131:132], AF.Ln, bias=kc("eps5")), deps=[vr])
            re_ = o("scalar", lambda g: g.activation(comb_sb[:, 132:133], comb_sb[:, 132:133], AF.Exp, scale=kc("neghalf")), deps=[rl])
            x1n = o("vector", lambda g, xt=xt: g.tensor_scalar(xt[:, :], xt[:, :], comb_sb[:, 128:129], comb_sb[:, 132:133], ALU.subtract, ALU.mult), deps=[re_, mn, ad])
            ln2_prev[0] = x1n
            wo = o("scalar", (lambda t_, xt=xt: lambda g: g.dma_start(
                out=out_h[t_ * 128:(t_ + 1) * 128, :], in_=xt[:, :]))(tt), dma=True, slot="out", deps=[x1n])
            ac_free[tt % 2] = x1n
            xs_free[tt % 2] = wo

    fo = P.final_ops()
    o("sync", lambda g: None, deps=[x for x in fo if x.eng != "sync"])

    if P3:
        es6.close()
        es2.close()
    elif P1:
        es1.close()
    blk = E(nc.Block())
    P.emit(nc, blk, es)
    es.close()
    return nc



def _prep_core(c, I):
    b, hg = c // 4, c % 4
    hs = slice(hg * HPC, (hg + 1) * HPC)
    x = I["L0_attn_x"]
    qk_w, vg_w = I["L0_attn_qk_w"], I["L0_attn_vg_w"]
    a_w, b_w = I["L0_attn_a_w"], I["L0_attn_b_w"]

    xb = np.ascontiguousarray(x[b].T)
    xT = np.ascontiguousarray(
        xb.reshape(16, 128, NTB, TBS).transpose(2, 1, 0, 3).reshape(NTB, 128, 16 * TBS))

    rows = []
    for h in range(HPC):
        rows.append(qk_w[(hg * HPC + h) * 64:(hg * HPC + h + 1) * 64])
    for h in range(HPC):
        rows.append(qk_w[1024 + (hg * HPC + h) * 64: 1024 + (hg * HPC + h + 1) * 64])
    for h in range(HPC):
        rows.append(vg_w[(hg * HPC + h) * 128:(hg * HPC + h + 1) * 128])
    for h in range(HPC):
        rows.append(vg_w[2048 + (hg * HPC + h) * 128: 2048 + (hg * HPC + h + 1) * 128])
    rows.append(a_w[hs])
    rows.append(b_w[hs])
    wcat = np.concatenate(rows, axis=0)
    wcat = np.concatenate([wcat, np.zeros((8, C), np.float32)], axis=0)
    wproj = np.ascontiguousarray(wcat.T.reshape(16, 128, 1552).transpose(1, 0, 2).reshape(128, 16 * 1552))

    convw = np.zeros((128, 48), np.float32)
    for h in range(HPC):
        convw[0:64, h * 4:(h + 1) * 4] = I["L0_attn_q_conv_w"][(hg * HPC + h) * 64:(hg * HPC + h + 1) * 64, 0, :]
        convw[0:64, 16 + h * 4:16 + (h + 1) * 4] = I["L0_attn_k_conv_w"][(hg * HPC + h) * 64:(hg * HPC + h + 1) * 64, 0, :]
        convw[:, 32 + h * 4:32 + (h + 1) * 4] = I["L0_attn_v_conv_w"][(hg * HPC + h) * 128:(hg * HPC + h + 1) * 128, 0, :]

    cvec = np.zeros((128, 12), np.float32)
    cvec[:, 0:4] = np.broadcast_to(I["L0_attn_dt_bias"][hs], (128, 4))
    cvec[:, 4:8] = np.broadcast_to(I["L0_attn_b_bias"][hs], (128, 4))
    cvec[:, 8:12] = np.broadcast_to(I["L0_attn_beta_bias"][hs], (128, 4))
    alog = np.ascontiguousarray(I["L0_attn_A_log"][hs][None, :])

    meta = np.array([[b * HPC * 128 * 4, (c % 4) * TOK, 0]], np.int32)
    d = dict(xT=xT, wproj=wproj, convw=convw, cvec=cvec, alog=alog, meta=meta)
    if STAGE >= 2:
        tok0 = c * TOK
        d["xrows"] = np.ascontiguousarray(x.reshape(-1, C)[tok0:tok0 + TOK])
    return d


def _shared_weights(I):
    d = {}
    if STAGE >= 2:
        d["owT"] = np.ascontiguousarray(I["L0_attn_o_w"].T.reshape(16, 128, 2048))
        d["gwT"] = np.ascontiguousarray(
            I["L0_ffn_gate_w"].T.reshape(16, 128, 16).transpose(1, 0, 2).reshape(128, 256))
        d["sgvT"] = np.ascontiguousarray(
            I["L0_ffn_shared_gate_vec"].T.reshape(16, 128, 1).transpose(1, 0, 2).reshape(128, 16))
        sel = np.zeros((16, 16 * 128), np.float32)
        for e in range(16):
            sel[e, e * 128:(e + 1) * 128] = 1.0
        d["sel16"] = sel.astype(BFD)
    if STAGE >= 3:
        egu = I["L0_ffn_expert_gate_up"]
        d["wguT"] = np.ascontiguousarray(
            egu.transpose(0, 2, 1).reshape(16, 16, 128, 22, 128).transpose(0, 3, 2, 1, 4).reshape(16, 22, 128, 16 * 128))
        ed = I["L0_ffn_expert_down"]
        d["wdT4"] = np.ascontiguousarray(
            ed.transpose(0, 2, 1).reshape(16, 11, 128, 4, 512).transpose(0, 3, 2, 1, 4).reshape(16, 4, 128, 11 * 512))
        d["sguT"] = np.ascontiguousarray(
            I["L0_ffn_shared_gate_up"].T.reshape(16, 128, 88, 128).transpose(2, 1, 0, 3).reshape(88, 128, 16 * 128))
        d["sdT"] = np.ascontiguousarray(
            I["L0_ffn_shared_down"].T.reshape(44, 128, 4, 512).transpose(2, 1, 0, 3).reshape(4, 128, 44 * 512))
    return d


def _consts():
    jj, ll = np.meshgrid(np.arange(128), np.arange(128), indexing="ij")
    kcf = np.zeros((128, 652), np.float32)
    kcf[:, 0:128] = np.where(ll < jj, 0.0, NEG)
    kcf[:, 128:256] = np.where(jj <= ll, 0.0, NEG)
    kcf[:, 256:384] = np.eye(128, dtype=np.float32)
    kcf[:, 384:512] = (jj <= ll).astype(np.float32)
    kcf[:, 512:640] = 1.0
    vals = {"zero": 0.0, "eps6": 1e-6, "one": 1.0, "eps5": 1e-5, "epsf": EPSF,
            "neghalf": -0.5, "half": 0.5, "two": 2.0, "lnscale": LNS,
            "inv128": 1.0 / 128, "negone": -1.0, "inv2048": 1.0 / 2048}
    order = ["zero", "eps6", "one", "eps5", "epsf", "neghalf", "half", "two",
             "lnscale", "inv128", "negone", "inv2048"]
    for i, k in enumerate(order):
        kcf[:, 640 + i] = vals[k]
    kcb = np.zeros((128, 130), BFD)
    kcb[:, 0:128] = np.eye(128, dtype=np.float32).astype(BFD)
    kcb[:, 128] = np.ones(128, BFD)
    kcb[:, 129] = np.ones(128, BFD)
    return kcf, kcb


def kernel(**inputs):
    kcf, kcb = _consts()
    shared = _shared_weights(inputs)
    if MODE == "fused":
        nc = build_program("fused")
        in_maps = []
        for c in range(8):
            m = _prep_core(c, inputs)
            m["kcf"] = kcf
            m["kcb"] = kcb
            m.update(shared)
            in_maps.append(m)
        res = run_bass_kernel_spmd(nc, in_maps, core_ids=list(range(8)))
        outs = [np.asarray(r["out"]) for r in res.results]
        return np.concatenate(outs, axis=0).reshape(B, T, C).astype(np.float32)

    nc1 = build_program("p1")
    in1 = []
    for c in range(8):
        m = _prep_core(c, inputs)
        m.pop("meta", None)
        m.pop("xrows", None)
        m["kcf"] = kcf
        m["kcb"] = kcb
        in1.append(m)
    res1 = run_bass_kernel_spmd(nc1, in1, core_ids=list(range(8)))
    slabs = [np.asarray(r["slab"]) for r in res1.results]
    ogf = np.zeros((2048, 2 * T), BFD)
    for c in range(8):
        b, hg = c // 4, c % 4
        ogf[hg * 512:(hg + 1) * 512, b * T:(b + 1) * T] = slabs[c]
    nc3 = build_program("p3")
    in3 = []
    x = inputs["L0_attn_x"].astype(np.float32, copy=False)
    for c in range(8):
        oga = ogf[:, c * TOK:(c + 1) * TOK]
        ogin = np.ascontiguousarray(oga.reshape(16, 128, TOK).transpose(1, 0, 2).reshape(128, 16 * TOK))
        m = dict(kcf=kcf, kcb=kcb, ogin=ogin,
                 xrows=np.ascontiguousarray(x.reshape(-1, C)[c * TOK:(c + 1) * TOK]))
        m.update(shared)
        in3.append(m)
    res3 = run_bass_kernel_spmd(nc3, in3, core_ids=list(range(8)))
    outs = [np.asarray(r["out"]) for r in res3.results]
    if DBG3:
        kernel.dbg3 = [np.asarray(r["dbg3"]) for r in res3.results]
        kernel.dbg4 = [np.asarray(r["dbg4"]) for r in res3.results]
        kernel.dbg5 = [np.asarray(r["dbg5"]) for r in res3.results]
        kernel.slabs = slabs
    return np.concatenate(outs, axis=0).reshape(B, T, C).astype(np.float32)


# version 07 (final)
# speedup vs baseline: 1.1649x; 1.0197x over previous
import numpy as np
import ml_dtypes
from contextlib import ExitStack

import concourse.bass as bass
import concourse.mybir as mybir
from concourse.bass_utils import run_bass_kernel_spmd

F32 = mybir.dt.float32
BF16 = mybir.dt.bfloat16
I32 = mybir.dt.int32
AF = mybir.ActivationFunctionType
ALU = mybir.AluOpType
AX = mybir.AxisListType.X
BFD = ml_dtypes.bfloat16

B, T, C = 2, 4096, 2048
H, DKH, DVH = 16, 64, 128
HPC = 4
L = 128
NCH = T // L
TBS = 512
NTB = T // TBS
CPB = TBS // L
TOK = 1024
NE, EI, SI = 16, 1408, 5632
SCALE = DKH ** -0.5
NEG = -1e30
LNS = float(np.log(SCALE))
EPSF = float(np.finfo(np.float32).eps)

STAGE = 3
PROBE = False
MODE = "fused"
DBG3 = False
ENGS = ("tensor", "vector", "scalar", "gpsimd", "sync")


class Op:
    __slots__ = ("eng", "fn", "deps", "dma", "slot", "need_inc", "sem", "val", "inc")

    def __init__(self, eng, fn, deps, dma, slot, inc):
        self.eng, self.fn, self.deps, self.dma = eng, fn, deps, dma
        self.slot = slot
        self.need_inc = dma
        self.sem = None
        self.val = None
        self.inc = inc


class Prog:
    def __init__(self):
        self.ops = []

    def op(self, eng, fn, deps=(), dma=False, slot=None, inc=16):
        if dma:
            assert slot is not None, "DMA ops need a slot"
        o = Op(eng, fn, [d for d in deps if d is not None], dma, slot, inc)
        self.ops.append(o)
        return o

    def emit(self, nc, blk, es):
        for o in self.ops:
            for d in o.deps:
                d.need_inc = True
        csem = {e: es.enter_context(nc.semaphore(f"c_{e}")) for e in ENGS}
        slots = {}
        for o in self.ops:
            if o.dma and o.slot not in slots:
                slots[o.slot] = [es.enter_context(nc.semaphore(f"s_{o.slot}")), 0]
        cctr = {e: 0 for e in ENGS}
        for o in self.ops:
            if o.dma:
                sl = slots[o.slot]
                sl[1] += o.inc
                o.sem, o.val = sl[0], sl[1]
            elif o.need_inc:
                cctr[o.eng] += 1
                o.sem, o.val = csem[o.eng], cctr[o.eng]

        def body(eng_name):
            def f(eng):
                satisfied = {}
                for o in self.ops:
                    if o.eng != eng_name:
                        continue
                    bysem = {}
                    for d in o.deps:
                        if satisfied.get(id(d.sem), -1) >= d.val:
                            continue
                        prev = bysem.get(id(d.sem))
                        if prev is None or d.val > prev[1]:
                            bysem[id(d.sem)] = (d.sem, d.val)
                    for smh, v in bysem.values():
                        eng.wait_ge(smh, v)
                        satisfied[id(smh)] = max(satisfied.get(id(smh), -1), v)
                    inst = o.fn(eng)
                    if o.dma:
                        inst.then_inc(o.sem, o.inc)
                    elif o.need_inc:
                        if inst is None:
                            inst = eng.nop()
                        inst.then_inc(o.sem, 1)
            return f
        for name in ENGS:
            getattr(blk, name)(body(name))

    def final_ops(self):
        last = {}
        for o in self.ops:
            last[o.eng] = o
        return list(last.values())


def build_program(part="fused"):
    nc = bass.Bass("TRN2", target_bir_lowering=False)
    P1 = part in ("fused", "p1")
    P3 = part in ("fused", "p3")
    P = Prog()
    o = P.op

    dt = nc.dram_tensor
    FUSED = part == "fused"
    if P1:
        xT_h = dt("xT", [NTB, 128, 16 * TBS], F32, kind="ExternalInput")
        wproj_h = dt("wproj", [128, 16 * 1552], F32, kind="ExternalInput")
        convw_h = dt("convw", [128, 48], F32, kind="ExternalInput")
        cvec_h = dt("cvec", [128, 12], F32, kind="ExternalInput")
        alog_h = dt("alog", [1, HPC], F32, kind="ExternalInput")
    kcf_h = dt("kcf", [128, 652], F32, kind="ExternalInput")
    kcb_h = dt("kcb", [128, 130], BF16, kind="ExternalInput")
    if FUSED:
        meta_h = dt("meta", [1, 3], I32, kind="ExternalInput")
    if P3:
        xrows_h = dt("xrows", [TOK, C], F32, kind="ExternalInput")
        owT_h = dt("owT", [16, 128, 2048], F32, kind="ExternalInput")
        gwT_h = dt("gwT", [128, 16 * 16], F32, kind="ExternalInput")
        sgvT_h = dt("sgvT", [128, 16], F32, kind="ExternalInput")
        sel16_h = dt("sel16", [16, 16 * 128], BF16, kind="ExternalInput")
        wguT_h = dt("wguT", [NE, 22, 128, 16 * 128], F32, kind="ExternalInput")
        wdT4_h = dt("wdT4", [NE, 4, 128, 11 * 512], F32, kind="ExternalInput")
        sguT_h = dt("sguT", [88, 128, 16 * 128], F32, kind="ExternalInput")
        sdT_h = dt("sdT", [4, 128, 44 * 512], F32, kind="ExternalInput")
        if not FUSED:
            ogin_h = dt("ogin", [128, 16 * TOK], BF16, kind="ExternalInput")

    dbg_h = None
    if P3:
        out_h = dt("out", [TOK, C], F32, kind="ExternalOutput")
        if DBG3 and not FUSED:
            dbg3_h = dt("dbg3", [TOK, C], F32, kind="ExternalOutput")
            dbg4_h = dt("dbg4", [TOK, C], F32, kind="ExternalOutput")
            dbg5_h = dt("dbg5", [128, 2304], F32, kind="ExternalOutput")
        x1sp_h = dt("x1_spill", [TOK, C], F32, kind="Internal")
        acc_h = dt("acc_dram", [TOK, C], F32, kind="Internal")
    if P1 and not P3:
        slab_h = dt("slab", [HPC * 128, T], BF16, kind="ExternalOutput")
    if FUSED:
        myslab_h = dt("myslab", [HPC * 128, T], BF16, kind="Internal")
        ccagg_h = dt("ccagg", [8 * HPC * 128, T], BF16, kind="Internal")
        ogst_h = dt("ogst", [2048, TOK], BF16, kind="Internal")

    es = ExitStack()
    E = es.enter_context
    es1 = ExitStack()
    E1 = es1.enter_context
    st = nc.sbuf_tensor

    kcf_sb = E(st("kcf_sb", [128, 652], F32))
    kcb_sb = E(st("kcb_sb", [128, 130], BF16))
    if FUSED:
        meta_sb = E(st("meta_sb", [1, 3], I32))
    pt = nc.psum_tensor
    pp = [E(pt(f"pp{i}", [128, TBS], F32)) for i in range(2)]
    if P1:
        wproj_sb = E1(st("wproj_sb", [128, 16 * 1552], BF16))
        xsb = [E1(st(f"xsb{i}", [128, 16 * TBS], BF16)) for i in range(2)]
        convw_sb = E1(st("convw_sb", [128, 48], F32))
        cvec_sb = E1(st("cvec_sb", [128, 12], F32))
        alog_sb = E1(st("alog_sb", [1, HPC], F32))
        eavec = E1(st("eavec", [1, 2 * HPC], F32))
        negea_bc = E1(st("negea_bc", [128, HPC], F32))

        qrw = [E1(st(f"qrw{h}", [64, TBS + 3], BF16)) for h in range(HPC)]
        krw = [E1(st(f"krw{h}", [64, TBS + 3], BF16)) for h in range(HPC)]
        vrw = [E1(st(f"vrw{h}", [128, TBS + 3], BF16)) for h in range(HPC)]
        qpo = [E1(st(f"qpo{h}", [64, 2 * TBS], BF16)) for h in range(HPC)]
        kpo = [E1(st(f"kpo{h}", [64, 2 * TBS], BF16)) for h in range(HPC)]
        vpo = [E1(st(f"vpo{h}", [128, 2 * TBS], BF16)) for h in range(HPC)]
        gpo = [E1(st(f"gpo{h}", [128, 2 * TBS], BF16)) for h in range(HPC)]
        qsq = [E1(st(f"qsq{h}", [64, 2 * TBS], BF16)) for h in range(HPC)]
        ksq = [E1(st(f"ksq{h}", [64, 2 * TBS], BF16)) for h in range(HPC)]
        asb = E1(st("asb", [4, 2 * TBS], F32))
        bsb = E1(st("bsb", [4, 2 * TBS], F32))
        ctmp = [E1(st(f"ctmp{i}", [128, TBS], F32)) for i in range(3)]
        ctmv = [E1(st(f"ctmv{i}", [128, TBS], F32)) for i in range(2)]

        NL = 2
        dA = [E1(st(f"dA{l}", [128, 128], F32)) for l in range(NL)]
        dM = [E1(st(f"dM{l}", [128, 128], F32)) for l in range(NL)]
        uf = [E1(st(f"uf{l}", [128, 128], F32)) for l in range(NL)]
        tf = [E1(st(f"tf{l}", [128, 128], F32)) for l in range(NL)]
        off_ = [E1(st(f"off{l}", [128, 128], F32)) for l in range(NL)]
        osc = [E1(st(f"osc{l}", [128, 128], F32)) for l in range(NL)]
        bA = [E1(st(f"bA{l}", [128, 128], BF16)) for l in range(NL)]
        bTA0 = [E1(st(f"bTA0{l}", [128, 128], BF16)) for l in range(NL)]
        bTAs = [[E1(st(f"bTAs{l}_{i}", [128, 128], BF16)) for i in range(6)] for l in range(NL)]
        bPs = [[E1(st(f"bPs{l}_{i}", [128, 128], BF16)) for i in range(2)] for l in range(NL)]
        bMT = [E1(st(f"bMT{l}", [128, 128], BF16)) for l in range(NL)]
        bu = [E1(st(f"bu{l}", [128, 128], BF16)) for l in range(NL)]
        bV = [E1(st(f"bV{l}", [128, 128], BF16)) for l in range(NL)]
        bK = [E1(st(f"bK{l}", [128, 64], BF16)) for l in range(NL)]
        bS16h = [E1(st(f"bS16h{h}", [64, DVH], BF16)) for h in range(HPC)]
        bon = [E1(st(f"bon{l}", [128, 128], BF16)) for l in range(NL)]
        gsilb = [E1(st(f"gsilb{h}", [128, 2 * TBS], BF16)) for h in range(HPC)]
        rowsf = [E1(st(f"rowsf{l}", [1, 256], F32)) for l in range(NL)]
        Ssb = [E1(st(f"Ssb{h}", [64, DVH], F32)) for h in range(HPC)]
        S2 = [E1(st(f"S2_{h}", [64, DVH], F32)) for h in range(HPC)]
        colh = [E1(st(f"colh{h}", [128, 12], F32)) for h in range(HPC)]
        cols = E1(st("cols", [128, 24], F32))
        gam4 = E1(st("gam4", [128, 4], F32))
        Gam4 = E1(st("Gam4", [4, 1], F32))
        Gbc = E1(st("Gbc", [128, HPC], F32))
        eGbc = E1(st("eGbc", [64, HPC], F32))
        grow = E1(st("grow", [1, 4], F32))
        oslab = E1(st("oslab", [128, HPC * TBS], BF16))

        pf = [[E1(pt(f"pf{l}_{i}", [128, 128], F32)) for i in range(2)] for l in range(NL)]
        pbt = [E1(pt(f"pbt{l}", [128, 128], BF16)) for l in range(NL)]

    maskA = kcf_sb[:, 0:128]
    maskMT = kcf_sb[:, 128:256]
    idf32 = kcf_sb[:, 256:384]
    tuones = kcf_sb[:, 384:512]
    onesf = kcf_sb[:, 512:640]
    ones_row_f = kcf_sb[0:1, 512:640]
    ones_col_f = kcf_sb[:, 512:513]
    KC = {"zero": 640, "eps6": 641, "one": 642, "eps5": 643, "epsf": 644,
          "neghalf": 645, "half": 646, "two": 647, "lnscale": 648,
          "inv128": 649, "negone": 650, "inv2048": 651}
    def kc(name, p=128):
        return kcf_sb[0:p, KC[name]:KC[name] + 1]
    id_bf = kcb_sb[:, 0:128]
    ones64_bf = kcb_sb[0:64, 128:129]

    if P1:
        ld_wproj = o("gpsimd", lambda g: g.dma_start(out=wproj_sb[:, :], in_=wproj_h[:, :]), dma=True, slot="wproj")
    ld_misc = []
    pairs = [(kcf_sb, kcf_h), (kcb_sb, kcb_h)]
    if P1:
        pairs = [(convw_sb, convw_h), (cvec_sb, cvec_h), (alog_sb, alog_h)] + pairs
    if FUSED:
        pairs = pairs + [(meta_sb, meta_h)]
    for dst, src in pairs:
        ld_misc.append(o("sync", (lambda d, s_: lambda g: g.dma_start(out=d[:, :], in_=s_[:, :]))(dst, src), dma=True, slot="misc"))
    mA = mCV = mAL = mKF = mKB = mMT = ld_misc[-1]
    MISC = ld_misc

    if P1:
        t_cv = o("vector", lambda g: g.tensor_add(cvec_sb[:, 4:8], cvec_sb[:, 4:8], cvec_sb[:, 8:12]), deps=MISC)
        t_ea = o("scalar", lambda g: g.activation(eavec[0:1, 0:HPC], alog_sb[0:1, :], AF.Exp), deps=MISC)
        t_ea2 = o("vector", lambda g: g.tensor_scalar_mul(eavec[0:1, HPC:2 * HPC], eavec[0:1, 0:HPC], -1.0), deps=[t_ea])
        t_eab = o("tensor", lambda g: g.matmul(pf[0][0][:, 0:HPC], ones_row_f, eavec[0:1, HPC:2 * HPC], start=True, stop=True), deps=[t_ea2, mKF])
        t_eac = o("vector", lambda g: g.tensor_copy(negea_bc[:, :], pf[0][0][:, 0:HPC]), deps=[t_eab])

        init_misc = [t_cv, t_eac, mA, mKB, mMT]
        sinit = []
        state_init = {}
        for h in range(HPC):
            s0op = o("vector", (lambda hh: lambda g: g.memset(Ssb[hh][:, :], 0.0))(h))
            sinit.append(s0op)
            sinit.append(o("vector", (lambda hh: lambda g: g.memset(S2[hh][:, :], 0.0))(h)))
            sbop = o("vector", (lambda hh: lambda g: g.memset(bS16h[hh][:, :], 0.0))(h))
            sinit.append(sbop)
            state_init[h] = (s0op, sbop)
        halo0 = []
        for h in range(HPC):
            for rw, pw in ((qrw[h], 64), (krw[h], 64), (vrw[h], 128)):
                halo0.append(o("vector", (lambda r, p_: lambda g: g.memset(r[0:p_, 0:3], 0.0))(rw, pw)))

        ld_x = [None] * NTB
        ld_x[0] = o("gpsimd", lambda g: g.dma_start(out=xsb[0][:, :], in_=xT_h[0, :, :]), dma=True, slot="x0", deps=[])

    ftiles = []
    offc = 0
    for h in range(HPC):
        ftiles.append(("q", h, 64, offc)); offc += 64
    for h in range(HPC):
        ftiles.append(("k", h, 64, offc)); offc += 64
    for h in range(HPC):
        ftiles.append(("v", h, 128, offc)); offc += 128
    for h in range(HPC):
        ftiles.append(("g", h, 128, offc)); offc += 128
    ftiles.append(("a", 0, 4, offc)); offc += 4
    ftiles.append(("b", 0, 4, offc)); offc += 4

    bank_evac = [None, None]
    conv_fin = [None] * NTB
    delta_fin = [None] * NTB
    slab_fin = [[None] * HPC for _ in range(NTB)]
    lane_last = [None, None]
    chunk_shared_last = [None]
    state_cur, state_nxt = (Ssb, S2) if P1 else (None, None)
    state_op = [None] * HPC
    state_bf_op = [None] * HPC
    if P1:
        for h in range(HPC):
            state_op[h], state_bf_op[h] = state_init[h]
    ab_evac = [None] * NTB

    evac_toggle = [0]
    def evac_eng():
        evac_toggle[0] ^= 1
        return "vector" if evac_toggle[0] else "scalar"

    def ev_copy(dst, src, deps):
        e = evac_eng()
        if e == "vector":
            return o("vector", lambda g: g.tensor_copy(dst, src), deps=deps)
        return o("scalar", lambda g: g.activation(dst, src, AF.Copy), deps=deps)

    regs = {}
    t_regs = None
    def setup_regs():
        def fn(g):
            r0 = g.alloc_register("rrow2")
            r1 = g.alloc_register("rcol2")
            g.reg_load(r0, meta_sb[0:1, 0:1])
            g.reg_load(r1, meta_sb[0:1, 1:2])
            regs["row2"] = g.snap(r0, min_val=0, max_val=HPC * 128 * 4)
            regs["col2"] = g.snap(r1, min_val=0, max_val=3 * TOK)
            return None
        return fn
    if FUSED:
        t_regs = o("sync", setup_regs(), deps=[mMT])

    conv_buf_last = {}
    gsops = {}
    def projconv_gen(tb):
        if tb + 1 < NTB:
            ld_x[tb + 1] = o("gpsimd", (lambda t: lambda g: g.dma_start(
                out=xsb[t % 2][:, :], in_=xT_h[t, :, :]))(tb + 1), dma=True,
                slot=f"x{(tb + 1) % 2}",
                deps=[ld_x[tb]] + ([bank_evac[0], bank_evac[1]] if tb >= 1 else []))
            yield
        deps = [ld_x[tb], ld_wproj]
        if tb >= 2 and delta_fin[tb - 2] is not None:
            deps += delta_fin[tb - 2]
        col0 = (tb % 2) * TBS
        evacs = {}
        for fi, (kind, h, fw, fo) in enumerate(ftiles):
            bank = pp[fi % 2]
            mmdeps = list(deps)
            if bank_evac[fi % 2] is not None:
                mmdeps.append(bank_evac[fi % 2])
            if kind in ("q", "k", "v") and conv_fin[tb - 1] is not None:
                mmdeps += conv_fin[tb - 1]
            for k in range(16):
                mm = o("tensor", (lambda kk, ffo, ffw, bk, ttb: lambda g: g.matmul(
                    bk[0:ffw, :], wproj_sb[:, kk * 1552 + ffo: kk * 1552 + ffo + ffw],
                    xsb[ttb % 2][:, kk * TBS:(kk + 1) * TBS],
                    start=(kk == 0), stop=(kk == 15)))(k, fo, fw, bank, tb),
                    deps=(mmdeps if k == 0 else ()))
            if kind == "q":
                dst = qrw[h][:, 3:3 + TBS]
            elif kind == "k":
                dst = krw[h][:, 3:3 + TBS]
            elif kind == "v":
                dst = vrw[h][:, 3:3 + TBS]
            elif kind == "g":
                dst = gpo[h][:, col0:col0 + TBS]
            elif kind == "a":
                dst = asb[:, col0:col0 + TBS]
            else:
                dst = bsb[:, col0:col0 + TBS]
            evdeps = [mm] + halo0
            if tb >= 2 and delta_fin[tb - 2] is not None:
                evdeps += delta_fin[tb - 2]
            ev = o("vector", (lambda d, bk, ffw: lambda g: g.tensor_copy(d, bk[0:ffw, :]))(dst, bank, fw), deps=evdeps)
            bank_evac[fi % 2] = ev
            evacs[(kind, h)] = ev
            yield
        ab_evac[tb] = [evacs[("a", 0)], evacs[("b", 0)]]
        finals = []
        qk_i = 0
        v_i = 0
        for kind, raws, posts, sqs, cwo in (("q", qrw, qpo, qsq, 0),
                                            ("k", krw, kpo, ksq, 16),
                                            ("v", vrw, vpo, None, 32)):
            for h in range(HPC):
                raw, pw = raws[h], (64 if kind != "v" else 128)
                eng = "vector"
                if kind == "v":
                    buf = ctmv[v_i % 2]
                    bkey = ("v", v_i % 2)
                    v_i += 1
                else:
                    buf = ctmp[qk_i % 3]
                    bkey = ("qk", qk_i % 3)
                    qk_i += 1
                dep = [evacs[(kind, h)], conv_buf_last.get(bkey)]
                cc = o(eng, (lambda rw, p_, co, bf: lambda g: g.tensor_scalar_mul(
                    bf[0:p_, :], rw[0:p_, 3:3 + TBS], convw_sb[0:p_, co + 3:co + 4]))(raw, pw, cwo + 4 * h, buf), deps=dep)
                yield
                for i in (2, 1, 0):
                    cc = o(eng, (lambda rw, p_, co, ii, bf: lambda g: g.scalar_tensor_tensor(
                        bf[0:p_, :], rw[0:p_, ii:ii + TBS], convw_sb[0:p_, co + ii:co + ii + 1],
                        bf[0:p_, :], ALU.mult, ALU.add))(raw, pw, cwo + 4 * h, i, buf), deps=[cc])
                    yield
                tcp = o(eng, (lambda rw, p_: lambda g: g.tensor_copy(
                    rw[0:p_, 0:3], rw[0:p_, TBS:TBS + 3]))(raw, pw), deps=[cc])
                sil = o("scalar", (lambda po, p_, bf: lambda g: g.activation(
                    po[0:p_, col0:col0 + TBS], bf[0:p_, :], AF.Silu))(posts[h], pw, buf), deps=[cc])
                conv_buf_last[bkey] = sil
                finals.append(tcp)
                finals.append(sil)
                yield
                if sqs is not None:
                    sq = o("scalar", (lambda s_, po, p_: lambda g: g.activation(
                        s_[0:p_, col0:col0 + TBS], po[0:p_, col0:col0 + TBS], AF.Square))(sqs[h], posts[h], pw), deps=[sil])
                    finals.append(sq)
                    yield
        for h in range(HPC):
            gdep = [evacs[("g", h)]]
            if tb >= 2 and delta_fin[tb - 2] is not None:
                gdep += delta_fin[tb - 2]
            gs = o("scalar", (lambda hh: lambda g: g.activation(
                gsilb[hh][:, col0:col0 + TBS], gpo[hh][:, col0:col0 + TBS], AF.Silu))(h), deps=gdep)
            gsops[(tb, h)] = gs
            finals.append(gs)
            yield
        conv_fin[tb] = finals

    def delta_chunk(tb, cisub, convdep):
        ci = tb * CPB + cisub
        col0 = (tb % 2) * TBS
        cc0 = col0 + cisub * L
        banks = [{"rot": 0, "free": [None, None]} for _ in range(NL)]

        def takebank(l):
            b = banks[l]
            i = b["rot"]
            b["rot"] ^= 1
            dep = [b["free"][i]] if b["free"][i] else []
            return pf[l][i], dep, i

        def setreader(l, i, op_):
            banks[l]["free"][i] = op_

        shdep = ([x for x in lane_last if x] + [chunk_shared_last[0]] +
                 ab_evac[tb] + [t_cv, t_eac, mKF, mKB] + convdep)
        bk0, d0, i0 = takebank(0)
        trA = o("tensor", (lambda bk: lambda g: g.transpose(bk[:, 0:4], asb[:, cc0:cc0 + L], idf32[0:4, 0:4]))(bk0), deps=shdep + d0)
        bk1, d1, i1 = takebank(1)
        trB = o("tensor", (lambda bk: lambda g: g.transpose(bk[:, 0:4], bsb[:, cc0:cc0 + L], idf32[0:4, 0:4]))(bk1), deps=shdep + d1)
        za = o("vector", (lambda bk: lambda g: g.tensor_add(cols[:, 12:16], bk[:, 0:4], cvec_sb[:, 0:4]))(bk0), deps=[trA])
        setreader(0, i0, za)
        zb = o("vector", (lambda bk: lambda g: g.tensor_add(cols[:, 16:20], bk[:, 0:4], cvec_sb[:, 4:8]))(bk1), deps=[trB])
        setreader(1, i1, zb)
        bsig = o("scalar", lambda g: g.activation(cols[:, 0:4], cols[:, 16:20], AF.Sigmoid), deps=[zb])
        bet = o("vector", lambda g: g.tensor_scalar_mul(cols[:, 0:4], cols[:, 0:4], 2.0), deps=[bsig])
        sab = o("scalar", lambda g: g.activation(cols[:, 20:24], cols[:, 12:16], AF.Abs), deps=[za])
        sex = o("scalar", lambda g: g.activation(cols[:, 20:24], cols[:, 20:24], AF.Exp, scale=kc("negone")), deps=[sab])
        sln = o("scalar", lambda g: g.activation(cols[:, 20:24], cols[:, 20:24], AF.Ln, bias=kc("one")), deps=[sex])
        srl = o("scalar", lambda g: g.activation(cols[:, 12:16], cols[:, 12:16], AF.Relu), deps=[sln])
        ssp = o("vector", lambda g: g.tensor_add(cols[:, 12:16], cols[:, 12:16], cols[:, 20:24]), deps=[srl])
        gg = o("vector", lambda g: g.tensor_mul(cols[:, 4:8], cols[:, 12:16], negea_bc[:, :]), deps=[ssp, t_eac])
        bk2, d2, i2 = takebank(0)
        gmm = o("tensor", (lambda bk: lambda g: g.matmul(bk[:, 0:4], tuones, cols[:, 4:8], start=True, stop=True))(bk2), deps=d2 + [gg])
        gev = o("vector", (lambda bk: lambda g: g.tensor_copy(gam4[:, :], bk[:, 0:4]))(bk2), deps=[gmm])
        setreader(0, i2, gev)
        bk3, d3, i3 = takebank(1)
        Gmm = o("tensor", (lambda bk: lambda g: g.matmul(bk[0:4, 0:1], cols[:, 4:8], ones_col_f, start=True, stop=True))(bk3), deps=d3 + [gg])
        Gev = o("vector", (lambda bk: lambda g: g.tensor_copy(Gam4[:, :], bk[0:4, 0:1]))(bk3), deps=[Gmm])
        setreader(1, i3, Gev)
        bk4, d4, i4 = takebank(0)
        trG = o("tensor", (lambda bk: lambda g: g.transpose(bk[0:1, 0:4], Gam4[:, :], idf32[0:4, 0:4]))(bk4), deps=d4 + [Gev])
        gre = o("vector", (lambda bk: lambda g: g.tensor_copy(grow[:, :], bk[0:1, 0:4]))(bk4), deps=[trG])
        setreader(0, i4, gre)
        bk5, d5, i5 = takebank(1)
        Gbm = o("tensor", (lambda bk: lambda g: g.matmul(bk[:, 0:4], ones_row_f, grow[:, :], start=True, stop=True))(bk5), deps=d5 + [gre])
        Gbe = o("vector", (lambda bk: lambda g: g.tensor_copy(Gbc[:, :], bk[:, 0:4]))(bk5), deps=[Gbm])
        setreader(1, i5, Gbe)
        eGe = o("scalar", lambda g: g.activation(eGbc[:, :], Gbc[0:64, :], AF.Exp), deps=[Gbe])
        chunk_shared_last[0] = eGe
        if PROBE and STAGE == 1 and tb == 0 and cisub == 0:
            pass

        lane_new_last = [None, None]
        probes = []

        def dump(src, dcol, width, dep, rows=128, cast=False):
            if not (PROBE and STAGE == 1 and tb == 0 and cisub == 0):
                return
            eng = "gpsimd" if cast else "sync"
            probes.append(o(eng, (lambda s_, dc, w_, r_: lambda g: g.dma_start(
                out=dbg2_h[0:r_, dc:dc + w_], in_=s_))(src, dcol, width, rows),
                dma=True, slot="probe", deps=[dep, t_regs]))

        def head_body(h, lane):
            Scur = state_cur[h]
            Snxt = state_nxt[h]
            ldep = [lane_new_last[lane] if lane_new_last[lane] is not None else lane_last[lane]]
            ldep = [x for x in ldep if x is not None]
            ch = colh[h]
            gcol = gam4[:, h:h + 1]
            bcol = cols[:, h:h + 1]
            Gcol = Gbc[:, h:h + 1]
            kchunk = kpo[h][:, cc0:cc0 + L]
            qchunk = qpo[h][:, cc0:cc0 + L]
            vchunk = vpo[h][:, cc0:cc0 + L]

            bkS, dS, iS = takebank(lane)
            m1 = o("tensor", (lambda bk: lambda g: g.matmul(bk[:, 0:1], qsq[h][:, cc0:cc0 + L], ones64_bf, start=True, stop=True))(bkS), deps=dS + ldep + [gev, bet, eGe])
            m2 = o("tensor", (lambda bk: lambda g: g.matmul(bk[:, 1:2], ksq[h][:, cc0:cc0 + L], ones64_bf, start=True, stop=True))(bkS), deps=[m1])
            yield
            l1 = o("scalar", (lambda bk: lambda g: g.activation(ch[:, 0:2], bk[:, 0:2], AF.Ln, bias=kc("eps6")))(bkS), deps=[m2])
            setreader(lane, iS, l1)
            yield
            c_uA = o("vector", lambda g: g.scalar_tensor_tensor(ch[:, 2:3], ch[:, 1:2], -0.5, gcol, ALU.mult, ALU.add), deps=[l1, gev] + ldep)
            c_w = o("vector", lambda g: g.scalar_tensor_tensor(ch[:, 3:4], ch[:, 1:2], 0.5, gcol, ALU.mult, ALU.add), deps=[l1])
            yield
            c_nw = o("vector", lambda g: g.tensor_scalar_mul(ch[:, 4:5], ch[:, 3:4], -1.0), deps=[c_w])
            c_a0 = o("vector", lambda g: g.scalar_tensor_tensor(ch[:, 5:6], ch[:, 0:1], -0.5, gcol, ALU.mult, ALU.add), deps=[l1])
            yield
            c_a = o("vector", lambda g: g.tensor_scalar_add(ch[:, 5:6], ch[:, 5:6], LNS), deps=[c_a0])
            c_pk = o("scalar", lambda g: g.activation(ch[:, 6:7], ch[:, 2:3], AF.Exp), deps=[c_uA])
            yield
            c_qs = o("scalar", lambda g: g.activation(ch[:, 7:8], ch[:, 5:6], AF.Exp), deps=[c_a])
            c_r0 = o("vector", lambda g: g.tensor_sub(ch[:, 8:9], Gcol, ch[:, 3:4]), deps=[c_w, Gbe])
            yield
            c_rr = o("scalar", lambda g: g.activation(ch[:, 8:9], ch[:, 8:9], AF.Exp), deps=[c_r0])
            yield

            bkW, dW, iW = takebank(lane)
            trW = o("tensor", (lambda bk: lambda g: g.transpose(bk[0:1, :], ch[:, 3:4], idf32))(bkW), deps=dW + [c_w])
            evW = o("vector", (lambda bk: lambda g: g.tensor_copy(rowsf[lane][0:1, 0:128], bk[0:1, :]))(bkW), deps=[trW])
            setreader(lane, iW, evW)
            yield
            bkA_, dA_, iA_ = takebank(lane)
            trA2 = o("tensor", (lambda bk: lambda g: g.transpose(bk[0:1, :], ch[:, 5:6], idf32))(bkA_), deps=dA_ + [c_a])
            evA2 = o("vector", (lambda bk: lambda g: g.tensor_copy(rowsf[lane][0:1, 128:256], bk[0:1, :]))(bkA_), deps=[trA2])
            setreader(lane, iA_, evA2)
            yield

            bkG, dG, iG = takebank(lane)
            G2A = o("tensor", (lambda bk: lambda g: g.matmul(bk[:, :], ones_row_f, rowsf[lane][0:1, 0:128], start=True, stop=True))(bkG), deps=dG + [evW])
            Dm = o("vector", (lambda bk: lambda g: g.scalar_tensor_tensor(
                dA[lane][:, :], maskA, ch[:, 2:3], bk[:, :], ALU.add, ALU.subtract))(bkG), deps=[c_uA, mKF, G2A] + ldep)
            setreader(lane, iG, Dm)
            yield
            Aex = o("scalar", lambda g: g.activation(dA[lane][:, :], dA[lane][:, :], AF.Exp), deps=[Dm])
            yield
            bkK, dK, iK = takebank(lane)
            KKt = o("tensor", (lambda bk: lambda g: g.matmul(bk[:, :], kchunk, kchunk, start=True, stop=True))(bkK), deps=dK + [Aex])
            Af = o("vector", (lambda bk: lambda g: g.tensor_mul(osc[lane][:, :], dA[lane][:, :], bk[:, :]))(bkK), deps=[KKt, Aex])
            setreader(lane, iK, Af)
            yield
            Amat = o("vector", lambda g: g.tensor_scalar_mul(bA[lane][:, :], osc[lane][:, :], bcol), deps=[Af, bet] + ldep)
            yield

            bkG2, dG2, iG2 = takebank(lane)
            G2M = o("tensor", (lambda bk: lambda g: g.matmul(bk[:, :], ones_row_f, rowsf[lane][0:1, 128:256], start=True, stop=True))(bkG2), deps=dG2 + [evA2])
            DmM = o("vector", (lambda bk: lambda g: g.scalar_tensor_tensor(
                dM[lane][:, :], maskMT, ch[:, 4:5], bk[:, :], ALU.add, ALU.add))(bkG2), deps=[c_nw, mKF, G2M] + ldep)
            setreader(lane, iG2, DmM)
            yield
            MTe = o("scalar", lambda g: g.activation(dM[lane][:, :], dM[lane][:, :], AF.Exp), deps=[DmM])
            yield
            bkQ, dQ, iQ = takebank(lane)
            KQt = o("tensor", (lambda bk: lambda g: g.matmul(bk[:, :], kchunk, qchunk, start=True, stop=True))(bkQ), deps=dQ + [MTe])
            MTm = o("vector", (lambda bk: lambda g: g.tensor_mul(bMT[lane][:, :], dM[lane][:, :], bk[:, :]))(bkQ), deps=[KQt, MTe] + ldep)
            setreader(lane, iQ, MTm)
            yield

            yield
            bkU, dU, iU = takebank(lane)
            KS0 = o("tensor", (lambda bk: lambda g: g.matmul(bk[:, :], kchunk, bS16h[h][:, :], start=True, stop=True))(bkU), deps=dU + [state_bf_op[h], MTm])
            Vtr = o("tensor", lambda g: g.transpose(pbt[lane][:, :], vchunk, id_bf), deps=[mKB, m1] + ldep)
            Vev = o("vector", lambda g: g.tensor_copy(bV[lane][:, :], pbt[lane][:, :]), deps=[Vtr] + ldep)
            yield
            u1 = o("vector", (lambda bk: lambda g: g.scalar_tensor_tensor(uf[lane][:, :], bk[:, :], ch[:, 6:7], bV[lane][:, :], ALU.mult, ALU.subtract))(bkU), deps=[KS0, Vev, c_pk] + ldep)
            setreader(lane, iU, u1)
            yield
            ub = o("vector", lambda g: g.tensor_scalar(bu[lane][:, :], uf[lane][:, :], bcol, -1.0, ALU.mult, ALU.mult), deps=[u1, bet] + ldep)
            u2 = o("vector", lambda g: g.tensor_scalar(uf[lane][:, :], uf[lane][:, :], bcol, -1.0, ALU.mult, ALU.mult), deps=[u1, bet, ub])
            yield

            TA0t = o("tensor", lambda g: g.transpose(pbt[lane][:, :], bA[lane][:, :], id_bf), deps=[Amat, Vev])
            TA0e = o("vector", lambda g: g.tensor_copy(bTA0[lane][:, :], pbt[lane][:, :]), deps=[TA0t] + ldep)
            yield
            Pcur, TAcur = bA[lane], bTA0[lane]
            Pop, TAop = Amat, TA0e
            tev = None
            for lev in range(1, 7):
                bkp, dp, ip_ = takebank(lane)
                tmm = o("tensor", (lambda bk, Pc, TAc: lambda g: g.matmul(bk[:, :], Pc[:, :], TAc[:, :], start=True, stop=True))(bkp, Pcur, TAcur), deps=dp + [TAop, Pop])
                tev = ev_copy(bTAs[lane][lev - 1][:, :], bkp[:, :], [tmm] + ldep)
                setreader(lane, ip_, tev)
                yield
                if lev < 6:
                    bkq, dq, iq_ = takebank(lane)
                    pmm = o("tensor", (lambda bk, Pc, TAc: lambda g: g.matmul(bk[:, :], TAc[:, :], Pc[:, :], start=True, stop=True))(bkq, Pcur, TAcur), deps=dq + [TAop, Pop])
                    pev = ev_copy(bPs[lane][lev % 2][:, :], bkq[:, :], [pmm] + ldep)
                    setreader(lane, iq_, pev)
                    Pcur, Pop = bPs[lane][lev % 2], pev
                    yield
                TAcur, TAop = bTAs[lane][lev - 1], tev
            tprev_f, top_f, top_b = uf[lane], u2, ub
            for lev in range(6, 0, -1):
                bkp, dp, ip_ = takebank(lane)
                amm = o("tensor", (lambda bk, TAl: lambda g: g.matmul(bk[:, :], TAl[:, :], bu[lane][:, :], start=True, stop=True))(bkp, bTAs[lane][lev - 1]), deps=dp + [top_b, tev])
                bnew = o("vector", (lambda bk, tp: lambda g: g.tensor_add(bu[lane][:, :], tp[:, :], bk[:, :]))(bkp, tprev_f), deps=[amm, top_f, top_b])
                att = o("vector", (lambda bk, tp: lambda g: g.tensor_add(tf[lane][:, :], tp[:, :], bk[:, :]))(bkp, tprev_f), deps=[amm, top_f, bnew])
                setreader(lane, ip_, att)
                tprev_f, top_f, top_b = tf[lane], att, bnew
                yield
            bkp2, dp2, ip2 = takebank(lane)
            fmm = o("tensor", (lambda bk: lambda g: g.matmul(bk[:, :], bTA0[lane][:, :], bu[lane][:, :], start=True, stop=True))(bkp2), deps=dp2 + [top_b])
            bneg = o("vector", (lambda bk: lambda g: g.tensor_sub(bu[lane][:, :], tf[lane][:, :], bk[:, :]))(bkp2), deps=[fmm, top_f, top_b])
            fde = o("vector", (lambda bk: lambda g: g.tensor_sub(tf[lane][:, :], tf[lane][:, :], bk[:, :]))(bkp2), deps=[fmm, top_f, bneg])
            setreader(lane, ip2, fde)
            yield

            bkO, dO, iO = takebank(lane)
            QS0 = o("tensor", (lambda bk: lambda g: g.matmul(bk[:, :], qchunk, bS16h[h][:, :], start=True, stop=True))(bkO), deps=dO + [state_bf_op[h], bneg])
            o1 = o("vector", (lambda bk: lambda g: g.tensor_scalar_mul(osc[lane][:, :], bk[:, :], ch[:, 7:8]))(bkO), deps=[QS0, c_qs, Af])
            setreader(lane, iO, o1)
            yield
            bkO2, dO2, iO2 = takebank(lane)
            Mdl = o("tensor", (lambda bk: lambda g: g.matmul(bk[:, :], bMT[lane][:, :], bu[lane][:, :], start=True, stop=True))(bkO2), deps=dO2 + [bneg])
            oo = o("vector", (lambda bk: lambda g: g.tensor_add(off_[lane][:, :], osc[lane][:, :], bk[:, :]))(bkO2), deps=[Mdl, o1] + ldep)
            setreader(lane, iO2, oo)
            yield

            dr = o("vector", lambda g: g.tensor_scalar_mul(bu[lane][:, :], tf[lane][:, :], ch[:, 8:9]), deps=[Mdl, c_rr, fde])
            Ktr = o("tensor", lambda g: g.transpose(pbt[lane][0:128, 0:64], kchunk, id_bf[0:64, 0:64]), deps=[TA0e, Vev])
            Kev = o("vector", lambda g: g.tensor_copy(bK[lane][:, :], pbt[lane][0:128, 0:64]), deps=[Ktr] + ldep)
            yield
            bkSU, dSU, iSU = takebank(lane)
            SMM = o("tensor", (lambda bk: lambda g: g.matmul(bk[0:64, :], bK[lane][:, :], bu[lane][:, :], start=True, stop=True))(bkSU), deps=dSU + [dr, Kev])
            Sup = o("vector", (lambda bk: lambda g: g.scalar_tensor_tensor(
                Snxt[:, :], Scur[:, :], eGbc[0:64, h:h + 1], bk[0:64, :], ALU.mult, ALU.add))(bkSU), deps=[SMM, state_op[h], eGe])
            Sbf = o("vector", (lambda bk: lambda g: g.scalar_tensor_tensor(
                bS16h[h][:, :], Scur[:, :], eGbc[0:64, h:h + 1], bk[0:64, :], ALU.mult, ALU.add))(bkSU),
                deps=[SMM, state_bf_op[h], eGe, QS0, KS0, Sup])
            setreader(lane, iSU, Sbf)
            state_op[h] = Sup
            state_bf_op[h] = Sbf
            yield

            sqo = o("scalar", lambda g: g.activation(osc[lane][:, :], off_[lane][:, :], AF.Square), deps=[oo, o1])
            rsum = o("vector", lambda g: g.reduce_sum(ch[:, 9:10], osc[lane][:, :], AX), deps=[sqo])
            yield
            rl = o("scalar", lambda g: g.activation(ch[:, 10:11], ch[:, 9:10], AF.Ln, scale=kc("inv128"), bias=kc("epsf")), deps=[rsum])
            re = o("scalar", lambda g: g.activation(ch[:, 10:11], ch[:, 10:11], AF.Exp, scale=kc("neghalf")), deps=[rl])
            yield
            on = o("vector", lambda g: g.tensor_scalar_mul(bon[lane][:, :], off_[lane][:, :], ch[:, 10:11]), deps=[re, oo])
            ontr = o("tensor", lambda g: g.transpose(pbt[lane][:, :], bon[lane][:, :], id_bf), deps=[on, Kev])
            yield
            og = o("vector", (lambda hh, cis: lambda g: g.tensor_mul(
                oslab[:, hh * TBS + cis * L: hh * TBS + (cis + 1) * L], pbt[lane][:, :],
                gsilb[hh][:, cc0:cc0 + L]))(h, cisub),
                deps=[ontr, Sup, gsops[(tb, h)]] + probes + ([slab_fin[tb - 1][h]] if tb > 0 else []))
            lane_new_last[lane] = og
            yield

        def lane_run(lane, heads):
            for h in heads:
                yield from head_body(h, lane)

        return lane_new_last, [lane_run(0, (0, 2)), lane_run(1, (1, 3))]

    def delta_drive(tb, gp):
        nonlocal state_cur, state_nxt
        dfin = []
        rr = 0
        for cisub in range(CPB):
            lane_new_last, gens = delta_chunk(tb, cisub, conv_fin[tb])
            alive = [True, True]
            while alive[0] or alive[1]:
                for i in (0, 1):
                    if alive[i]:
                        try:
                            next(gens[i])
                        except StopIteration:
                            alive[i] = False
                rr += 1
                if gp is not None and rr % 3 == 0:
                    try:
                        next(gp)
                    except StopIteration:
                        gp = None
            for lpos in range(NL):
                if lane_new_last[lpos] is not None:
                    lane_last[lpos] = lane_new_last[lpos]
            dfin = [x for x in lane_new_last if x]
            state_cur, state_nxt = state_nxt, state_cur
        delta_fin[tb] = dfin
        for h in range(HPC):
            hd = [lane_last[h % 2]]
            tgt = myslab_h if FUSED else slab_h
            sd = o("sync", (lambda hh, ttb, tg: lambda g: g.dma_start(
                out=tg[hh * 128:(hh + 1) * 128, ttb * TBS:(ttb + 1) * TBS],
                in_=oslab[:, hh * TBS:(hh + 1) * TBS]))(h, tb, tgt),
                dma=True, slot=f"slab{h}", deps=hd + [t_regs])
            slab_fin[tb][h] = sd
        return gp

    if P1:
        for _ in projconv_gen(0):
            pass
        for tb in range(1, NTB):
            gp = projconv_gen(tb)
            gp = delta_drive(tb - 1, gp)
            if gp is not None:
                for _ in gp:
                    pass
        delta_drive(NTB - 1, None)


    cc_op = None
    if FUSED:
        allslab = [slab_fin[tb][h] for tb in range(NTB) for h in range(HPC)]
        cc_op = o("gpsimd", lambda g: g.collective_compute(
            "AllGather", ALU.bypass, replica_groups=[list(range(8))],
            ins=[myslab_h.ap()],
            outs=[ccagg_h.ap()]),
            dma=True, slot="cc", inc=1, deps=allslab + [t_regs])

    if P3:
        if P1:
            es1.close()
        es2 = ExitStack()
        E2 = es2.enter_context
        x1fm = E2(st("x1fm", [128, 16 * TOK], BF16))
        comb_sb = E2(st("comb_sb", [128, 144], F32))
        combfm = E2(st("combfm", [16, TOK], F32))
        combbf = E2(st("combbf", [16, TOK], BF16))
        sgvsig = E2(st("sgvsig", [128, 8], F32))
        gwT_sb = E2(st("gwT_sb", [128, 16 * 16], BF16))
        sgvT_sb = E2(st("sgvT_sb", [128, 16], BF16))
        pq = [E2(pt(f"pq{i}", [128, 512], F32)) for i in range(4)]

        ld3 = []
        ld3.append(o("gpsimd", lambda g: g.dma_start(out=gwT_sb[:, :], in_=gwT_h[:, :]), dma=True, slot="m3", deps=[cc_op]))
        ld3.append(o("gpsimd", lambda g: g.dma_start(out=sgvT_sb[:, :], in_=sgvT_h[:, :]), dma=True, slot="m3", deps=[cc_op]))
        m3 = ld3[-1]
        dfin_gate = delta_fin[NTB - 1][0] if P1 else None

        es3 = ExitStack()
        E3 = es3.enter_context
        pr = E3(pt("pr", [128, 128], F32))
        pxb = E3(pt("pxb", [128, 128], BF16))
        xrt = [E3(st(f"xrt{i}", [128, 2048], F32)) for i in range(2)]
        x1t = [E3(st(f"x1t{i}", [128, 2048], F32)) for i in range(2)]
        sscr = E3(st("sscr", [128, 2048], BF16))
        rscr = E3(st("rscr", [128, 64], F32))
        xbf = E3(st("xbf", [128, 2048], BF16))
        ogth = E3(st("ogth", [128, 16 * TOK], BF16))
        owT_sb = E3(st("owT_sb", [128, 16 * 2048], BF16))

        og_ops = []
        if FUSED:
            og_dyn = []
            for q in range(4):
                og_dyn.append(o("sync", (lambda qq: lambda g: g.dma_start(
                    out=ogst_h[qq * 512:(qq + 1) * 512, :],
                    in_=ccagg_h.ap()[bass.ds(regs["row2"] + qq * 512, 512),
                                     bass.ds(regs["col2"], TOK)]))(q),
                    dma=True, slot="ogd", deps=[cc_op, t_regs]))
            for k in range(16):
                og_ops.append(o("sync", (lambda kk: lambda g: g.dma_start(
                    out=ogth[:, kk * TOK:(kk + 1) * TOK],
                    in_=ogst_h[kk * 128:(kk + 1) * 128, :]))(k),
                    dma=True, slot="og", deps=og_dyn))
        else:
            og_ops.append(o("sync", lambda g: g.dma_start(out=ogth[:, :], in_=ogin_h[:, :]),
                            dma=True, slot="og", deps=[]))
        ow_ops = []
        for q in range(4):
            ow_ops.append(o("gpsimd", (lambda qq: lambda g: g.dma_start(
                out=owT_sb[:, qq * 4 * 2048:(qq + 1) * 4 * 2048],
                in_=owT_h.ap()[qq * 4:(qq + 1) * 4, :, :].rearrange("k p d -> p k d")))(q),
                dma=True, slot="ow", deps=[cc_op]))

        LNV = {}
        x1w_ops = []
        ln_done = []
        rt_last = [None]
        xr_ops = [None, None]
        xrt_free = [None, None]
        x1t_free = [None, None]
        pq_free = [None, None, None, None]
        pxb_free = [None]
        pr_free = [None]
        for tt in range(8):
            xr_ops[tt % 2] = o("sync", (lambda t_: lambda g: g.dma_start(
                out=xrt[t_ % 2][:, :], in_=xrows_h[t_ * 128:(t_ + 1) * 128, :]))(tt),
                dma=True, slot=f"xr{tt % 2}", deps=[cc_op, xrt_free[tt % 2]])
            attev = []
            for oc in range(4):
                bk = pq[oc % 2]
                mmd = og_ops + ow_ops + [pq_free[oc % 2], dfin_gate]
                mm = None
                for k in range(16):
                    mm = o("tensor", (lambda kk, occ, bkk, t_: lambda g: g.matmul(
                        bkk[:, :], ogth[:, kk * TOK + t_ * 128: kk * TOK + (t_ + 1) * 128],
                        owT_sb[:, kk * 2048 + occ * 512: kk * 2048 + (occ + 1) * 512],
                        start=(kk == 0), stop=(kk == 15)))(k, oc, bk, tt),
                        deps=(mmd if k == 0 else ()))
                ev = o("vector", (lambda occ, bkk, t_: lambda g: g.tensor_add(
                    x1t[t_ % 2][:, occ * 512:(occ + 1) * 512], xrt[t_ % 2][:, occ * 512:(occ + 1) * 512],
                    bkk[:, :]))(oc, bk, tt), deps=[mm, xr_ops[tt % 2], x1t_free[tt % 2]])
                pq_free[oc % 2] = ev
                xrt_free[tt % 2] = ev
                attev.append(ev)
            xt = x1t[tt % 2]
            rs = o("vector", lambda g, xt=xt: g.reduce_sum(comb_sb[:, 133:134], xt[:, :], AX), deps=attev)
            mn = o("vector", lambda g: g.tensor_scalar_mul(comb_sb[:, 128:129], comb_sb[:, 133:134], 1.0 / 2048), deps=[rs])
            sq = o("scalar", lambda g, xt=xt: g.activation(sscr[:, :], xt[:, :], AF.Square), deps=attev + [ln_done[-1] if ln_done else None, rt_last[0]])
            ss = o("vector", lambda g: g.reduce_sum(comb_sb[:, 129:130], sscr[:, :], AX), deps=[sq])
            ms = o("vector", lambda g: g.tensor_mul(comb_sb[:, 130:131], comb_sb[:, 128:129], comb_sb[:, 128:129]), deps=[mn])
            vr = o("vector", lambda g: g.scalar_tensor_tensor(comb_sb[:, 131:132], comb_sb[:, 129:130], 1.0 / 2048, comb_sb[:, 130:131], ALU.mult, ALU.subtract), deps=[ss, ms])
            rl = o("scalar", lambda g: g.activation(comb_sb[:, 132:133], comb_sb[:, 131:132], AF.Ln, bias=kc("eps5")), deps=[vr])
            re_ = o("scalar", lambda g: g.activation(comb_sb[:, 132:133], comb_sb[:, 132:133], AF.Exp, scale=kc("neghalf")), deps=[rl])
            x1n = o("vector", lambda g, xt=xt: g.tensor_scalar(xt[:, :], xt[:, :], comb_sb[:, 128:129], comb_sb[:, 132:133], ALU.subtract, ALU.mult), deps=[re_, mn] + attev)
            ln_done.append(x1n)
            x1w = o("sync", (lambda t_, xt=xt: lambda g: g.dma_start(
                out=x1sp_h[t_ * 128:(t_ + 1) * 128, :], in_=xt[:, :]))(tt), dma=True, slot="x1w", deps=[x1n])
            x1w_ops.append(x1w)
            cb = o("scalar", lambda g, xt=xt: g.activation(xbf[:, :], xt[:, :], AF.Copy), deps=[x1n, rt_last[0], x1w])
            x1t_free[tt % 2] = cb
            for k in range(16):
                trx = o("tensor", (lambda kk: lambda g: g.transpose(
                    pxb[:, :], xbf[:, kk * 128:(kk + 1) * 128], id_bf))(k), deps=[cb, pxb_free[0], mKB])
                evx = o("vector", (lambda kk, t_: lambda g: g.tensor_copy(
                    x1fm[:, kk * TOK + t_ * 128: kk * TOK + (t_ + 1) * 128], pxb[:, :]))(k, tt), deps=[trx])
                pxb_free[0] = evx
            xfm_done = evx
            rmm = None
            for k in range(16):
                rmm = o("tensor", (lambda kk, t_: lambda g: g.matmul(
                    pr[:, 0:16], x1fm[:, kk * TOK + t_ * 128: kk * TOK + (t_ + 1) * 128],
                    gwT_sb[:, kk * 16:(kk + 1) * 16], start=(kk == 0), stop=(kk == 15)))(k, tt),
                    deps=([xfm_done, m3, pr_free[0]] if k == 0 else ()))
            smm = None
            for k in range(16):
                smm = o("tensor", (lambda kk, t_: lambda g: g.matmul(
                    pr[:, 16:17], x1fm[:, kk * TOK + t_ * 128: kk * TOK + (t_ + 1) * 128],
                    sgvT_sb[:, kk:kk + 1], start=(kk == 0), stop=(kk == 15)))(k, tt),
                    deps=([rmm] if k == 0 else ()))
            sgv_o = o("scalar", (lambda t_: lambda g: g.activation(sgvsig[:, t_:t_ + 1], pr[:, 16:17], AF.Sigmoid))(tt), deps=[smm])
            lcp = o("vector", lambda g: g.tensor_copy(rscr[:, 0:16], pr[:, 0:16]), deps=[rmm, sq])
            pr_free[0] = lcp
            lmx = o("vector", lambda g: g.reduce_max(rscr[:, 16:17], rscr[:, 0:16], AX), deps=[lcp])
            nmx = o("vector", lambda g: g.tensor_scalar_mul(rscr[:, 16:17], rscr[:, 16:17], -1.0), deps=[lmx])
            pex = o("scalar", lambda g: g.activation(rscr[:, 0:16], rscr[:, 0:16], AF.Exp, bias=rscr[:, 16:17]), deps=[nmx])
            wcp = o("vector", lambda g: g.tensor_copy(rscr[:, 32:48], rscr[:, 0:16]), deps=[pex])
            last = wcp
            for rnd in range(3):
                mx = o("vector", lambda g: g.reduce_max(rscr[:, 17:18], rscr[:, 32:48], AX), deps=[last])
                mk = o("vector", lambda g: g.tensor_scalar(rscr[:, 48:64], rscr[:, 32:48], rscr[:, 17:18], -1e30, ALU.is_ge, ALU.mult), deps=[mx])
                ad = o("vector", lambda g: g.tensor_add(rscr[:, 32:48], rscr[:, 32:48], rscr[:, 48:64]), deps=[mk])
                last = ad
            m4 = o("vector", lambda g: g.reduce_max(rscr[:, 17:18], rscr[:, 32:48], AX), deps=[last])
            kp = o("vector", lambda g: g.tensor_single_scalar(rscr[:, 32:48], rscr[:, 0:16], rscr[:, 17:18], ALU.is_ge), deps=[m4])
            cw = o("vector", lambda g: g.tensor_mul(rscr[:, 0:16], rscr[:, 0:16], rscr[:, 32:48]), deps=[kp])
            sm = o("vector", lambda g: g.reduce_sum(rscr[:, 16:17], rscr[:, 0:16], AX), deps=[cw])
            rc = o("vector", lambda g: g.reciprocal(rscr[:, 16:17], rscr[:, 16:17]), deps=[sm])
            cmb = o("vector", (lambda t_: lambda g: g.tensor_scalar_mul(
                comb_sb[:, t_ * 16:(t_ + 1) * 16], rscr[:, 0:16], rscr[:, 16:17]))(tt), deps=[rc])
            trc = o("tensor", (lambda t_: lambda g: g.transpose(
                pr[0:16, 0:128], comb_sb[:, t_ * 16:(t_ + 1) * 16], idf32))(tt), deps=[cmb, pr_free[0]])
            evc = o("vector", (lambda t_: lambda g: g.tensor_copy(
                combfm[:, t_ * 128:(t_ + 1) * 128], pr[0:16, 0:128]))(tt), deps=[trc])
            pr_free[0] = evc
            rt_last[0] = evc
        cbf = o("vector", lambda g: g.tensor_copy(combbf[:, :], combfm[:, :]), deps=[rt_last[0]])
    if P3:
        es3.close()
        sgbuf = [E2(st(f"sgbuf{i}", [128, 16 * 128], BF16)) for i in range(4)]
        silb = [E2(st(f"silb{i}", [128, 512], BF16)) for i in range(2)]
        stg = [E2(st(f"stg{i}", [128, 512], F32)) for i in range(2)]
        es4 = ExitStack()
        E4 = es4.enter_context
        hS = E4(st("hS", [128, 44 * 512], BF16))
        sdoc = [E4(st("sdoc0", [128, 44 * 512], BF16))] * 2

        ph3a_done = [cbf] + x1w_ops + ln_done
        acc_last = [None]

        sg_free = [None, None, None, None]
        sd_free = [None, None]
        hs_free = [None]
        psg_free = [None, None]
        pdn_free = [None, None]
        silb_free_sh = [None, None]
        stg_free_sh = [None, None]
        for th in range(2):
            hs_ops = []
            for j in range(44):
                bi = j % 2
                dg = o("gpsimd", (lambda jj, bb, t_: lambda g: g.dma_start(
                    out=sgbuf[bb][:, :], in_=sguT_h[jj, :, :]))(j, bi, th),
                    dma=True, slot=f"sg{bi}", deps=[sg_free[bi], ph3a_done[0], cc_op])
                du = o("gpsimd", (lambda jj, bb, t_: lambda g: g.dma_start(
                    out=sgbuf[2 + bb][:, :], in_=sguT_h[44 + jj, :, :]))(j, bi, th),
                    dma=True, slot=f"sg{2 + bi}", deps=[sg_free[2 + bi], ph3a_done[0], cc_op])
                mmg = None
                for k in range(16):
                    mmg = o("tensor", (lambda kk, bb, t_: lambda g: g.matmul(
                        pq[0][:, :], sgbuf[bb][:, kk * 128:(kk + 1) * 128],
                        x1fm[:, kk * TOK + t_ * 512: kk * TOK + (t_ + 1) * 512],
                        start=(kk == 0), stop=(kk == 15)))(k, bi, th),
                        deps=([dg, psg_free[0], cbf] if k == 0 else ()))
                mmu = None
                for k in range(16):
                    mmu = o("tensor", (lambda kk, bb, t_: lambda g: g.matmul(
                        pq[1][:, :], sgbuf[2 + bb][:, kk * 128:(kk + 1) * 128],
                        x1fm[:, kk * TOK + t_ * 512: kk * TOK + (t_ + 1) * 512],
                        start=(kk == 0), stop=(kk == 15)))(k, bi, th),
                        deps=([du, psg_free[1]] if k == 0 else ()))
                sil = o("scalar", (lambda bb: lambda g: g.activation(
                    silb[bb][:, :], pq[0][:, :], AF.Silu))(bi), deps=[mmg, hs_free[0], silb_free_sh[bi]])
                psg_free[0] = sil
                hsm = o("vector", (lambda jj, bb: lambda g: g.tensor_mul(
                    hS[:, jj * 512:(jj + 1) * 512], silb[bb][:, :], pq[1][:, :]))(j, bi), deps=[sil, mmu])
                psg_free[1] = hsm
                silb_free_sh[bi] = hsm
                sg_free[bi] = mmg
                sg_free[2 + bi] = mmu
                hs_ops.append(hsm)
            dn_ops = []
            for oc in range(4):
                sb_i = 0
                sld = o("gpsimd", (lambda occ, sb: lambda g: g.dma_start(
                    out=sdoc[sb][:, :], in_=sdT_h[occ, :, :]))(oc, sb_i),
                    dma=True, slot=f"sd{sb_i}", deps=[sd_free[sb_i], hs_ops[0] if oc < 2 else None])
                for ts_ in range(4):
                    bk = pq[2 + ts_ % 2]
                    mmd = None
                    for j in range(44):
                        mmd = o("tensor", (lambda jj, occ, tss, bkk, t_, sb: lambda g: g.matmul(
                            bkk[:, :], hS[:, jj * 512 + tss * 128: jj * 512 + (tss + 1) * 128],
                            sdoc[sb][:, jj * 512:(jj + 1) * 512],
                            start=(jj == 0), stop=(jj == 43)))(j, oc, ts_, bk, th, sb_i),
                            deps=(([sld, pdn_free[ts_ % 2]] + hs_ops) if j == 0 else ()))
                    stv = o("vector", (lambda tss, bkk, t_: lambda g: g.tensor_scalar_mul(
                        stg[tss % 2][:, :], bkk[:, :], sgvsig[:, t_ * 4 + tss: t_ * 4 + tss + 1]))(ts_, bk, th),
                        deps=[mmd, sgv_o, acc_last[0], stg_free_sh[ts_ % 2]])
                    pdn_free[ts_ % 2] = stv
                    wr = o("gpsimd", (lambda tss, occ, t_, ss: lambda g: g.dma_start(
                        out=acc_h[(t_ * 4 + tss) * 128:(t_ * 4 + tss + 1) * 128, occ * 512:(occ + 1) * 512],
                        in_=ss[:, :]))(ts_, oc, th, stg[ts_ % 2]),
                        dma=True, slot="acc", deps=[stv])
                    stg_free_sh[ts_ % 2] = wr
                    dn_ops.append(wr)
                sd_free[sb_i] = dn_ops[-1] if dn_ops else None
            hs_free[0] = dn_ops[-1]
            acc_last[0] = dn_ops[-1]
        shared_done = acc_last[0]
        es4.close()
        es5 = ExitStack()
        E5 = es5.enter_context
        sel16_sb = E5(st("sel16_sb", [16, 16 * 128], BF16))
        sel_ld = o("sync", lambda g: g.dma_start(out=sel16_sb[:, :], in_=sel16_h[:, :]), dma=True, slot="m3", deps=[cc_op, shared_done])
        cbsb = [E5(st(f"cbsb{i}", [128, 1024], BF16)) for i in range(2)]
        hE = [E5(st(f"hE{i}", [128, 11 * 1024], BF16)) for i in range(2)]
        wdb = [E5(st(f"wdb{i}", [128, 22 * 512], BF16)) for i in range(2)]
        pd = [E5(pt(f"pd{i}", [128, 512], F32)) for i in range(2)]

        wd_free = [None, None]
        he_free = [None, None]
        pge_free = list(psg_free) + list(pdn_free)
        pde_free = [None, None]
        silb_free = [None, None]
        stg_free = [None, None]
        exp_add_last = [shared_done]
        for e in range(NE):
            eb = e % 2
            cbops = []
            for tb2 in range(2):
                cbm = o("tensor", (lambda ee, tbb: lambda g: g.matmul(
                    pq[0][:, :], sel16_sb[:, ee * 128:(ee + 1) * 128],
                    combbf[:, tbb * 512:(tbb + 1) * 512], start=True, stop=True))(e, tb2),
                    deps=[cbf, pge_free[0], pde_free[0], sel_ld, exp_add_last[0] if e == 0 else None])
                cbe = o("vector", (lambda tbb, bb: lambda g: g.tensor_copy(
                    cbsb[bb][:, tbb * 512:(tbb + 1) * 512], pq[0][:, :]))(tb2, eb), deps=[cbm, he_free[eb]])
                pge_free[0] = cbe
                pde_free[0] = cbe
                cbops.append(cbe)
            he_ops = []
            for j in range(11):
                bi = j % 2
                dg = o("gpsimd", (lambda ee, jj, bb: lambda g: g.dma_start(
                    out=sgbuf[bb][:, :], in_=wguT_h[ee, jj, :, :]))(e, j, bi),
                    dma=True, slot=f"sg{bi}", deps=[sg_free[bi]])
                du = o("gpsimd", (lambda ee, jj, bb: lambda g: g.dma_start(
                    out=sgbuf[2 + bb][:, :], in_=wguT_h[ee, 11 + jj, :, :]))(e, j, bi),
                    dma=True, slot=f"sg{2 + bi}", deps=[sg_free[2 + bi]])
                mmg = [None, None]
                for tb2 in range(2):
                    for k in range(16):
                        mmg[tb2] = o("tensor", (lambda kk, bb, tbb: lambda g: g.matmul(
                            pq[0 + tbb][:, :], sgbuf[bb][:, kk * 128:(kk + 1) * 128],
                            x1fm[:, kk * TOK + tbb * 512: kk * TOK + (tbb + 1) * 512],
                            start=(kk == 0), stop=(kk == 15)))(k, bi, tb2),
                            deps=([dg, pge_free[tb2]] if k == 0 else ()))
                mmu = [None, None]
                for tb2 in range(2):
                    for k in range(16):
                        mmu[tb2] = o("tensor", (lambda kk, bb, tbb: lambda g: g.matmul(
                            pq[2 + tbb][:, :], sgbuf[2 + bb][:, kk * 128:(kk + 1) * 128],
                            x1fm[:, kk * TOK + tbb * 512: kk * TOK + (tbb + 1) * 512],
                            start=(kk == 0), stop=(kk == 15)))(k, bi, tb2),
                            deps=([du, pge_free[2 + tb2]] if k == 0 else ()))
                for tb2 in range(2):
                    sil = o("scalar", (lambda bb, tbb: lambda g: g.activation(
                        silb[bb][:, :], pq[0 + tbb][:, :], AF.Silu))(tb2 % 2, tb2), deps=[mmg[tb2], he_free[eb], silb_free[tb2 % 2]])
                    pge_free[tb2] = sil
                    hsm = o("vector", (lambda jj, bb, tbb, hb: lambda g: g.scalar_tensor_tensor(
                        hE[hb][:, jj * 1024 + tbb * 512: jj * 1024 + (tbb + 1) * 512],
                        pq[2 + tbb][:, :], 1.0, silb[bb][:, :], ALU.mult, ALU.mult))(j, tb2 % 2, tb2, eb), deps=[sil, mmu[tb2], cbops[-1]])
                    hcm = o("vector", (lambda jj, tbb, hb, bb: lambda g: g.tensor_mul(
                        hE[hb][:, jj * 1024 + tbb * 512: jj * 1024 + (tbb + 1) * 512],
                        hE[hb][:, jj * 1024 + tbb * 512: jj * 1024 + (tbb + 1) * 512],
                        cbsb[bb][:, tbb * 512:(tbb + 1) * 512]))(j, tb2, eb, eb), deps=[hsm] + cbops)
                    pge_free[2 + tb2] = hcm
                    silb_free[tb2 % 2] = hsm
                    he_ops.append(hcm)
                sg_free[bi] = mmg[1]
                sg_free[2 + bi] = mmu[1]
            dn_last = None
            for ocp in range(2):
                wdd = o("gpsimd", (lambda ee, op_, bb: lambda g: g.dma_start(
                    out=wdb[bb][:, :],
                    in_=wdT4_h.ap()[ee, 2 * op_:2 * op_ + 2, :, :].rearrange("a p t -> p a t")))(e, ocp, eb),
                    dma=True, slot=f"wd{eb}", deps=[wd_free[eb], he_ops[0] if e > 0 or ocp > 0 else cbops[0]])
                for ts_ in range(8):
                    mmd = [None, None]
                    for oh in range(2):
                        for j in range(11):
                            mmd[oh] = o("tensor", (lambda jj, ohh, tss, hb, wb: lambda g: g.matmul(
                                pd[ohh][:, :], hE[hb][:, jj * 1024 + tss * 128: jj * 1024 + (tss + 1) * 128],
                                wb[:, (ohh * 11 + jj) * 512:(ohh * 11 + jj + 1) * 512],
                                start=(jj == 0), stop=(jj == 10)))(j, oh, ts_, eb, wdb[eb]),
                                deps=(([wdd, pde_free[oh]] + (he_ops if oh == 0 else [])) if j == 0 else ()))
                    for oh in range(2):
                        oc = 2 * ocp + oh
                        stv = o("vector", (lambda ohh, sg: lambda g: g.tensor_copy(
                            sg[:, :], pd[ohh][:, :]))(oh, stg[oh]), deps=[mmd[oh], exp_add_last[0], stg_free[oh]])
                        pde_free[oh] = stv
                        wr = o("gpsimd", (lambda tss, occ, ss: lambda g: g.dma_start(
                            out=acc_h[tss * 128:(tss + 1) * 128, occ * 512:(occ + 1) * 512],
                            in_=ss[:, :], accum_op=ALU.add))(ts_, oc, stg[oh]),
                            dma=True, slot="acc", deps=[stv])
                        stg_free[oh] = wr
                        dn_last = wr
                wd_free[eb] = dn_last
            exp_add_last[0] = dn_last
            he_free[eb] = dn_last

        if DBG3 and not FUSED:
            o("sync", lambda g: g.dma_start(out=dbg3_h[:, :], in_=x1sp_h[:, :]),
              dma=True, slot="dbg3", deps=x1w_ops)
            o("sync", lambda g: g.dma_start(out=dbg4_h[:, :], in_=acc_h[:, :]),
              dma=True, slot="dbg3", deps=[exp_add_last[0]])
            o("sync", lambda g: g.dma_start(out=dbg5_h[:, 0:144], in_=comb_sb[:, :]),
              dma=True, slot="dbg3", deps=[cbf, exp_add_last[0]])
            o("sync", lambda g: g.dma_start(out=dbg5_h[:, 144:152], in_=sgvsig[:, :]),
              dma=True, slot="dbg3", deps=[exp_add_last[0]])
            o("gpsimd", lambda g: g.dma_start(out=dbg5_h[:, 256:1280], in_=cbsb[1][:, :]),
              dma=True, slot="dbg3b", deps=[exp_add_last[0]])
            o("gpsimd", lambda g: g.dma_start(out=dbg5_h[:, 1280:2304], in_=hE[1][:, 0:1024]),
              dma=True, slot="dbg3b", deps=[exp_add_last[0]])
        es5.close()
        es6 = ExitStack()
        E6 = es6.enter_context
        fxr = [E6(st(f"fxr{i}", [128, 2048], F32)) for i in range(2)]
        fx1 = [E6(st(f"fx1{i}", [128, 2048], F32)) for i in range(2)]
        fsq = E6(st("fsq", [128, 2048], BF16))
        ac_free = [None, None]
        xs_free = [None, None]
        ln2_prev = [None]
        for tt in range(8):
            ar = o("sync", (lambda t_: lambda g: g.dma_start(
                out=fxr[t_ % 2][:, :], in_=acc_h[t_ * 128:(t_ + 1) * 128, :]))(tt),
                dma=True, slot=f"xr{tt % 2}", deps=[exp_add_last[0], ac_free[tt % 2]])
            xr_ = o("sync", (lambda t_: lambda g: g.dma_start(
                out=fx1[t_ % 2][:, :], in_=x1sp_h[t_ * 128:(t_ + 1) * 128, :]))(tt),
                dma=True, slot=f"x1r{tt % 2}", deps=x1w_ops + [xs_free[tt % 2]])
            ad = o("vector", (lambda t_: lambda g: g.tensor_add(
                fx1[t_ % 2][:, :], fx1[t_ % 2][:, :], fxr[t_ % 2][:, :]))(tt), deps=[ar, xr_])
            xt = fx1[tt % 2]
            rs = o("vector", lambda g, xt=xt: g.reduce_sum(comb_sb[:, 133:134], xt[:, :], AX), deps=[ad])
            mn = o("vector", lambda g: g.tensor_scalar_mul(comb_sb[:, 128:129], comb_sb[:, 133:134], 1.0 / 2048), deps=[rs])
            sq = o("scalar", lambda g, xt=xt: g.activation(fsq[:, 0:2048], xt[:, :], AF.Square), deps=[ad, ln2_prev[0]])
            ss = o("vector", lambda g: g.reduce_sum(comb_sb[:, 129:130], fsq[:, 0:2048], AX), deps=[sq])
            ms = o("vector", lambda g: g.tensor_mul(comb_sb[:, 130:131], comb_sb[:, 128:129], comb_sb[:, 128:129]), deps=[mn])
            vr = o("vector", lambda g: g.scalar_tensor_tensor(comb_sb[:, 131:132], comb_sb[:, 129:130], 1.0 / 2048, comb_sb[:, 130:131], ALU.mult, ALU.subtract), deps=[ss, ms])
            rl = o("scalar", lambda g: g.activation(comb_sb[:, 132:133], comb_sb[:, 131:132], AF.Ln, bias=kc("eps5")), deps=[vr])
            re_ = o("scalar", lambda g: g.activation(comb_sb[:, 132:133], comb_sb[:, 132:133], AF.Exp, scale=kc("neghalf")), deps=[rl])
            x1n = o("vector", lambda g, xt=xt: g.tensor_scalar(xt[:, :], xt[:, :], comb_sb[:, 128:129], comb_sb[:, 132:133], ALU.subtract, ALU.mult), deps=[re_, mn, ad])
            ln2_prev[0] = x1n
            wo = o("scalar", (lambda t_, xt=xt: lambda g: g.dma_start(
                out=out_h[t_ * 128:(t_ + 1) * 128, :], in_=xt[:, :]))(tt), dma=True, slot="out", deps=[x1n])
            ac_free[tt % 2] = x1n
            xs_free[tt % 2] = wo

    fo = P.final_ops()
    o("sync", lambda g: None, deps=[x for x in fo if x.eng != "sync"])

    if P3:
        es6.close()
        es2.close()
    elif P1:
        es1.close()
    blk = E(nc.Block())
    P.emit(nc, blk, es)
    es.close()
    return nc



def _prep_core(c, I):
    b, hg = c // 4, c % 4
    hs = slice(hg * HPC, (hg + 1) * HPC)
    x = I["L0_attn_x"]
    qk_w, vg_w = I["L0_attn_qk_w"], I["L0_attn_vg_w"]
    a_w, b_w = I["L0_attn_a_w"], I["L0_attn_b_w"]

    xb = np.ascontiguousarray(x[b].T)
    xT = np.ascontiguousarray(
        xb.reshape(16, 128, NTB, TBS).transpose(2, 1, 0, 3).reshape(NTB, 128, 16 * TBS))

    rows = []
    for h in range(HPC):
        rows.append(qk_w[(hg * HPC + h) * 64:(hg * HPC + h + 1) * 64])
    for h in range(HPC):
        rows.append(qk_w[1024 + (hg * HPC + h) * 64: 1024 + (hg * HPC + h + 1) * 64])
    for h in range(HPC):
        rows.append(vg_w[(hg * HPC + h) * 128:(hg * HPC + h + 1) * 128])
    for h in range(HPC):
        rows.append(vg_w[2048 + (hg * HPC + h) * 128: 2048 + (hg * HPC + h + 1) * 128])
    rows.append(a_w[hs])
    rows.append(b_w[hs])
    wcat = np.concatenate(rows, axis=0)
    wcat = np.concatenate([wcat, np.zeros((8, C), np.float32)], axis=0)
    wproj = np.ascontiguousarray(wcat.T.reshape(16, 128, 1552).transpose(1, 0, 2).reshape(128, 16 * 1552))

    convw = np.zeros((128, 48), np.float32)
    for h in range(HPC):
        convw[0:64, h * 4:(h + 1) * 4] = I["L0_attn_q_conv_w"][(hg * HPC + h) * 64:(hg * HPC + h + 1) * 64, 0, :]
        convw[0:64, 16 + h * 4:16 + (h + 1) * 4] = I["L0_attn_k_conv_w"][(hg * HPC + h) * 64:(hg * HPC + h + 1) * 64, 0, :]
        convw[:, 32 + h * 4:32 + (h + 1) * 4] = I["L0_attn_v_conv_w"][(hg * HPC + h) * 128:(hg * HPC + h + 1) * 128, 0, :]

    cvec = np.zeros((128, 12), np.float32)
    cvec[:, 0:4] = np.broadcast_to(I["L0_attn_dt_bias"][hs], (128, 4))
    cvec[:, 4:8] = np.broadcast_to(I["L0_attn_b_bias"][hs], (128, 4))
    cvec[:, 8:12] = np.broadcast_to(I["L0_attn_beta_bias"][hs], (128, 4))
    alog = np.ascontiguousarray(I["L0_attn_A_log"][hs][None, :])

    meta = np.array([[b * HPC * 128 * 4, (c % 4) * TOK, 0]], np.int32)
    d = dict(xT=xT, wproj=wproj, convw=convw, cvec=cvec, alog=alog, meta=meta)
    if STAGE >= 2:
        tok0 = c * TOK
        d["xrows"] = np.ascontiguousarray(x.reshape(-1, C)[tok0:tok0 + TOK])
    return d


def _shared_weights(I):
    d = {}
    if STAGE >= 2:
        d["owT"] = np.ascontiguousarray(I["L0_attn_o_w"].T.reshape(16, 128, 2048))
        d["gwT"] = np.ascontiguousarray(
            I["L0_ffn_gate_w"].T.reshape(16, 128, 16).transpose(1, 0, 2).reshape(128, 256))
        d["sgvT"] = np.ascontiguousarray(
            I["L0_ffn_shared_gate_vec"].T.reshape(16, 128, 1).transpose(1, 0, 2).reshape(128, 16))
        sel = np.zeros((16, 16 * 128), np.float32)
        for e in range(16):
            sel[e, e * 128:(e + 1) * 128] = 1.0
        d["sel16"] = sel.astype(BFD)
    if STAGE >= 3:
        egu = I["L0_ffn_expert_gate_up"]
        d["wguT"] = np.ascontiguousarray(
            egu.transpose(0, 2, 1).reshape(16, 16, 128, 22, 128).transpose(0, 3, 2, 1, 4).reshape(16, 22, 128, 16 * 128))
        ed = I["L0_ffn_expert_down"]
        d["wdT4"] = np.ascontiguousarray(
            ed.transpose(0, 2, 1).reshape(16, 11, 128, 4, 512).transpose(0, 3, 2, 1, 4).reshape(16, 4, 128, 11 * 512))
        d["sguT"] = np.ascontiguousarray(
            I["L0_ffn_shared_gate_up"].T.reshape(16, 128, 88, 128).transpose(2, 1, 0, 3).reshape(88, 128, 16 * 128))
        d["sdT"] = np.ascontiguousarray(
            I["L0_ffn_shared_down"].T.reshape(44, 128, 4, 512).transpose(2, 1, 0, 3).reshape(4, 128, 44 * 512))
    return d


def _consts():
    jj, ll = np.meshgrid(np.arange(128), np.arange(128), indexing="ij")
    kcf = np.zeros((128, 652), np.float32)
    kcf[:, 0:128] = np.where(ll < jj, 0.0, NEG)
    kcf[:, 128:256] = np.where(jj <= ll, 0.0, NEG)
    kcf[:, 256:384] = np.eye(128, dtype=np.float32)
    kcf[:, 384:512] = (jj <= ll).astype(np.float32)
    kcf[:, 512:640] = 1.0
    vals = {"zero": 0.0, "eps6": 1e-6, "one": 1.0, "eps5": 1e-5, "epsf": EPSF,
            "neghalf": -0.5, "half": 0.5, "two": 2.0, "lnscale": LNS,
            "inv128": 1.0 / 128, "negone": -1.0, "inv2048": 1.0 / 2048}
    order = ["zero", "eps6", "one", "eps5", "epsf", "neghalf", "half", "two",
             "lnscale", "inv128", "negone", "inv2048"]
    for i, k in enumerate(order):
        kcf[:, 640 + i] = vals[k]
    kcb = np.zeros((128, 130), BFD)
    kcb[:, 0:128] = np.eye(128, dtype=np.float32).astype(BFD)
    kcb[:, 128] = np.ones(128, BFD)
    kcb[:, 129] = np.ones(128, BFD)
    return kcf, kcb


def kernel(**inputs):
    kcf, kcb = _consts()
    shared = _shared_weights(inputs)
    if MODE == "fused":
        nc = build_program("fused")
        in_maps = []
        for c in range(8):
            m = _prep_core(c, inputs)
            m["kcf"] = kcf
            m["kcb"] = kcb
            m.update(shared)
            in_maps.append(m)
        res = run_bass_kernel_spmd(nc, in_maps, core_ids=list(range(8)))
        outs = [np.asarray(r["out"]) for r in res.results]
        return np.concatenate(outs, axis=0).reshape(B, T, C).astype(np.float32)

    nc1 = build_program("p1")
    in1 = []
    for c in range(8):
        m = _prep_core(c, inputs)
        m.pop("meta", None)
        m.pop("xrows", None)
        m["kcf"] = kcf
        m["kcb"] = kcb
        in1.append(m)
    res1 = run_bass_kernel_spmd(nc1, in1, core_ids=list(range(8)))
    slabs = [np.asarray(r["slab"]) for r in res1.results]
    ogf = np.zeros((2048, 2 * T), BFD)
    for c in range(8):
        b, hg = c // 4, c % 4
        ogf[hg * 512:(hg + 1) * 512, b * T:(b + 1) * T] = slabs[c]
    nc3 = build_program("p3")
    in3 = []
    x = inputs["L0_attn_x"].astype(np.float32, copy=False)
    for c in range(8):
        oga = ogf[:, c * TOK:(c + 1) * TOK]
        ogin = np.ascontiguousarray(oga.reshape(16, 128, TOK).transpose(1, 0, 2).reshape(128, 16 * TOK))
        m = dict(kcf=kcf, kcb=kcb, ogin=ogin,
                 xrows=np.ascontiguousarray(x.reshape(-1, C)[c * TOK:(c + 1) * TOK]))
        m.update(shared)
        in3.append(m)
    res3 = run_bass_kernel_spmd(nc3, in3, core_ids=list(range(8)))
    outs = [np.asarray(r["out"]) for r in res3.results]
    if DBG3:
        kernel.dbg3 = [np.asarray(r["dbg3"]) for r in res3.results]
        kernel.dbg4 = [np.asarray(r["dbg4"]) for r in res3.results]
        kernel.dbg5 = [np.asarray(r["dbg5"]) for r in res3.results]
        kernel.slabs = slabs
    return np.concatenate(outs, axis=0).reshape(B, T, C).astype(np.float32)
